# version 05
# baseline (speedup 1.0000x reference)
import numpy as np
import ml_dtypes
import concourse.bass as bass
import concourse.mybir as mybir
from concourse.bass_utils import run_bass_kernel_spmd
from concourse.bass_core_rust import IndirectOffsetOnAxis
from concourse.library_config import mlp as MLP_LIB

F32 = mybir.dt.float32
BF16 = mybir.dt.bfloat16
I16 = mybir.dt.int16
I32 = mybir.dt.int32
AF = mybir.ActivationFunctionType
ALU = mybir.AluOpType
AX = mybir.AxisListType

T = 2048
C = 1024
KT = 8
NH = 16
HD = 64
NE = 64
FF = 512
NCH = 4
NQT = 16
EPB = 12288
CAP = 768

TAP = None

ENGINES = ("sy", "gp", "te", "ve", "sc")


def _build(tap=None):
    nc = bass.Bass("TRN2", target_bir_lowering=False)

    di = {}
    def dram_in(name, shape, dt=F32):
        di[name] = nc.dram_tensor(name, shape, dt, kind="ExternalInput").ap()
        return di[name]

    xt = dram_in("xt", [128, KT * T])
    wqs = dram_in("wqs", [128, KT * C]); wks = dram_in("wks", [128, KT * C])
    wvs = dram_in("wvs", [128, KT * C]); wos = dram_in("wos", [128, KT * C])
    bqs = dram_in("bqs", [128, 8]); bks = dram_in("bks", [128, 8])
    bvs = dram_in("bvs", [128, 8]); bos = dram_in("bos", [128, 8])
    wmods = dram_in("wmods", [128, KT * 6144])
    cv = dram_in("cv", [128, 8])
    gws = dram_in("gws", [128, KT * 64])
    wsg = dram_in("wsg", [128, KT * FF]); wsu = dram_in("wsu", [128, KT * FF])
    wsd = dram_in("wsd", [128, 4 * C])
    experts = dram_in("experts", [NE, 128, EPB])
    cosr = dram_in("cosr", [128, T], BF16); sinr = dram_in("sinr", [128, T], BF16)
    idnb_d = dram_in("idnb", [128, 128], BF16)
    idnf_d = dram_in("idnf", [128, 128], F32)
    masktd = dram_in("masktd", [128, 1024], F32)
    tok16d = dram_in("tok16d", [128, 16], I16)
    ones128d = dram_in("ones128d", [128, 128], BF16)
    trisd = dram_in("trisd", [128, 128], BF16)
    tri64d = dram_in("tri64d", [64, 64], BF16)
    e48d = dram_in("e48d", [128, 64], F32)
    e6d = dram_in("e6d", [128, 64], F32)
    brepd = dram_in("brepd", [128, 64], F32)

    xo = nc.dram_tensor("xo", [128, KT * T], F32, kind="ExternalOutput").ap()
    xp_dram = nc.dram_tensor("xp_scratch", [128, KT * T], F32).ap()
    h2tok_dram = nc.dram_tensor("h2tok_scratch", [T + 1, C], BF16).ap()
    lists_dram = nc.dram_tensor("lists_scratch", [16 * 3072 + 64], I16).ap()
    cwrep_dram = nc.dram_tensor("cwrep_scratch", [64 * CAP + 16], F32).ap()
    ytok_dram = nc.dram_tensor("ytok_scratch", [T + 1, C], F32).ap()
    ytok2_dram = nc.dram_tensor("ytok2_scratch", [T + 1, C], F32).ap()

    sem_guards = []
    sems = {}
    cnt = {}
    for k in ("DW", "DX", "DO", "T", "V", "S"):
        g = nc.semaphore("sem_" + k)
        sems[k] = g.__enter__(); sem_guards.append(g)
        cnt[k] = 0

    live = {}
    def alloc(name, shape, dt, side="right"):
        g = nc.sbuf_tensor(name, shape, dt, side=side)
        t = g.__enter__(); live[name] = g
        return t
    def palloc(name, shape, dt):
        g = nc.psum_tensor(name, shape, dt)
        t = g.__enter__(); live[name] = g
        return t
    def free(*names):
        for n in names:
            live.pop(n).__exit__(None, None, None)

    sched = None
    def em(e, fn):
        sched[e].append(fn)
    def dma(e, dst, src, key):
        cnt[key] += 16
        v = cnt[key]
        em(e, lambda E: E.dma_start(dst, src).then_inc(sems[key], 16))
        return v
    def wait(e, key, v):
        em(e, lambda E: E.wait_ge(sems[key], v))
    def inc(e, key, fn):
        cnt[key] += 1
        em(e, lambda E: fn(E).then_inc(sems[key], 1))
        return cnt[key]
    def vsep(e, key):
        v = mark(e, key)
        wait(e, key, v)
    def mark(e, key):
        cnt[key] += 1
        f = sched[e].pop()
        em(e, lambda E, f=f: f(E).then_inc(sems[key], 1))
        return cnt[key]
    def run_phase():
        nonlocal sched
        s = sched
        sched = None
        with nc.Block() as blk:
            @blk.sync
            def _(E):
                for f in s["sy"]: f(E)
            @blk.gpsimd
            def _(E):
                for f in s["gp"]: f(E)
            @blk.tensor
            def _(E):
                for f in s["te"]: f(E)
            @blk.vector
            def _(E):
                for f in s["ve"]: f(E)
            @blk.scalar
            def _(E):
                for f in s["sc"]: f(E)
    def phase():
        nonlocal sched
        sched = {e: [] for e in ENGINES}

    mod_sb = alloc("mod_sb", [128, 48], F32, side="left")
    der = alloc("der", [128, 64], F32, side="left")
    rscr = alloc("rscr", [128, 16], F32, side="left")
    bq_sb = alloc("bq_sb", [128, 8], F32, side="left"); bk_sb = alloc("bk_sb", [128, 8], F32, side="left")
    bv_sb = alloc("bv_sb", [128, 8], F32, side="left"); bo_sb = alloc("bo_sb", [128, 8], F32, side="left")
    bqs_s = alloc("bqs_s", [128, 8], F32, side="left"); bks_s = alloc("bks_s", [128, 8], F32, side="left")
    cv_sb = alloc("cv_sb", [128, 8], F32, side="left"); cvs = alloc("cvs", [128, 8], BF16, side="left")
    idn_b = alloc("idn_b", [128, 128], BF16, side="left")
    idn_f = alloc("idn_f", [128, 128], F32, side="left")
    maskT_sb = alloc("maskT_sb", [128, 1024], F32, side="left")
    tok16 = alloc("tok16", [128, 16], I16, side="left")
    ones128b = alloc("ones128b", [128, 128], BF16, side="left")
    triSb = alloc("triSb", [128, 128], BF16, side="left")
    tri64b = alloc("tri64b", [64, 64], BF16, side="left")
    e48c = alloc("e48c", [128, 64], F32, side="left")
    e6c = alloc("e6c", [128, 64], F32, side="left")
    brep = alloc("brep", [128, 64], F32, side="left")
    gw_f = alloc("gw_f", [128, KT * 64], F32, side="left")
    ones_c = alloc("ones_c", [128, 1], BF16, side="left")
    ones_r = alloc("ones_r", [1, 128], F32, side="left")

    h_bf = alloc("h_bf", [128, KT * T], BF16, side="left")

    psum_s = palloc("psum_s", [128, 2048], F32)
    ps_a = palloc("ps_a", [128, 512], F32)
    ps_b = palloc("ps_b", [128, 512], F32)

    phase()
    wmod_buf = alloc("wmod_buf", [128, KT * 6144], BF16)

    dma("sy", cv_sb[:, :], cv, "DX")
    dma("sy", idn_b[:, :], idnb_d, "DX")
    dma("sy", idn_f[:, :], idnf_d, "DX")
    dma("sy", maskT_sb[:, :], masktd, "DX")
    dma("sy", tok16[:, :], tok16d, "DX")
    dma("sy", ones128b[:, :], ones128d, "DX")
    dma("sy", triSb[:, :], trisd, "DX")
    dma("sy", tri64b[:, :], tri64d, "DX")
    dma("sy", e48c[:, :], e48d, "DX")
    dma("sy", e6c[:, :], e6d, "DX")
    dma("sy", brep[:, :], brepd, "DX")
    dma("sy", gw_f[:, :], gws, "DX")
    dma("sy", bq_sb[:, :], bqs, "DX")
    dma("sy", bk_sb[:, :], bks, "DX")
    dma("sy", bv_sb[:, :], bvs, "DX")
    dma("sy", bo_sb[:, :], bos, "DX")
    dX_consts = cnt["DX"]
    em("gp", lambda E: E.load_library(MLP_LIB))
    dma("gp", wmod_buf[:, :], wmods, "DW")
    dW_wmod = cnt["DW"]

    em("ve", lambda E: E.memset(der[:, :], 0.0))
    em("ve", lambda E: E.memset(ones_c[:, :], 1.0))
    em("ve", lambda E: E.memset(ones_r[:, :], 1.0))
    vs_ones = mark("ve", "V")

    wait("sc", "DX", dX_consts)
    ss_cvs = inc("sc", "S", lambda E: E.activation(cvs[:, :], cv_sb[:, :], AF.Silu))

    wait("te", "DW", dW_wmod)
    wait("te", "S", ss_cvs)
    for mt in range(48):
        for kt in range(KT):
            o, l, r = psum_s[:, mt:mt + 1], wmod_buf[:, kt * 6144 + mt * 128: kt * 6144 + mt * 128 + 128], cvs[:, kt:kt + 1]
            if kt < KT - 1:
                em("te", lambda E, o=o, l=l, r=r, kt=kt: E.matmul(o, l, r, start=(kt == 0), stop=False))
            else:
                em("te", lambda E, o=o, l=l, r=r: E.matmul(o, l, r, start=False, stop=True))
    ts_mod = mark("te", "T")

    wait("ve", "T", ts_mod)
    em("ve", lambda E: E.tensor_copy(mod_sb[:, :], psum_s[:, 0:48]))
    vsep("ve", "V")
    em("ve", lambda E: E.tensor_single_scalar(der[:, 8:16], mod_sb[:, 8:16], 1.0, ALU.add))
    em("ve", lambda E: E.tensor_single_scalar(der[:, 32:40], mod_sb[:, 32:40], 1.0, ALU.add))
    vs_mod1 = mark("ve", "V")
    wait("sc", "V", vs_mod1)
    em("sc", lambda E: E.sign(rscr[:, 0:8], mod_sb[:, 16:24]))
    vsep("sc", "S")
    em("sc", lambda E: E.activation(rscr[:, 8:16], mod_sb[:, 16:24], AF.Abs))
    ss_ga = mark("sc", "S")
    wait("ve", "S", ss_ga)
    em("ve", lambda E: E.tensor_scalar_max(rscr[:, 8:16], rscr[:, 8:16], 0.25))
    vsep("ve", "V")
    em("ve", lambda E: E.tensor_mul(der[:, 16:24], rscr[:, 0:8], rscr[:, 8:16]))
    vs_ga = mark("ve", "V")
    wait("ve", "V", vs_ga)
    wait("sc", "V", vs_ga)
    em("sc", lambda E: E.sign(rscr[:, 0:8], mod_sb[:, 40:48]))
    vsep("sc", "S")
    em("sc", lambda E: E.activation(rscr[:, 8:16], mod_sb[:, 40:48], AF.Abs))
    ss_gb = mark("sc", "S")
    wait("ve", "S", ss_gb)
    em("ve", lambda E: E.tensor_scalar_max(rscr[:, 8:16], rscr[:, 8:16], 0.25))
    vsep("ve", "V")
    em("ve", lambda E: E.tensor_mul(der[:, 40:48], rscr[:, 0:8], rscr[:, 8:16]))
    vsep("ve", "V")
    em("ve", lambda E: E.tensor_mul(der[:, 48:56], der[:, 16:24], bo_sb[:, :]))
    vsep("ve", "V")
    em("ve", lambda E: E.reciprocal(der[:, 56:64], der[:, 40:48]))
    em("ve", lambda E: E.tensor_single_scalar(bqs_s[:, :], bq_sb[:, :], 0.125, ALU.mult))
    em("ve", lambda E: E.tensor_single_scalar(bks_s[:, :], bk_sb[:, :], 0.125, ALU.mult))
    vs_modp = mark("ve", "V")
    wait("ve", "V", vs_modp)
    run_phase()
    free("wmod_buf")

    if tap == "mod":
        return _finish_tap(nc, sems, cnt, xo, [(mod_sb[:, 0:48], 0), (der[:, 8:64], 64)]), sem_guards

    x_sb = alloc("x_sb", [128, KT * T], F32)

    phase()
    dma("sy", x_sb[:, :], xt, "DX")
    dX_x = cnt["DX"]

    def emit_ln(src, one_gain_col, shift_col):
        rowbuf = live_get("rowbuf")
        a_rep = live_get("a_rep"); b_rep = live_get("b_rep")
        ra = rowbuf[0:1, 0:T]; rb = rowbuf[0:1, T:2 * T]; rc = rowbuf[0:1, 2 * T:3 * T]
        for mt in range(KT):
            em("sc", lambda E, mt=mt: E.activation(h_bf[:, mt * T:(mt + 1) * T], src[:, mt * T:(mt + 1) * T], AF.Square))
        s_sq = mark("sc", "S")
        wait("te", "S", s_sq)
        for c in range(NCH):
            for kt in range(KT):
                o = psum_s[0:1, c * 512:(c + 1) * 512]
                r = h_bf[:, kt * T + c * 512: kt * T + (c + 1) * 512]
                em("te", lambda E, o=o, r=r, kt=kt: E.matmul(o, ones_c[:, :], r, start=(kt == 0), stop=(kt == KT - 1)))
        t_sq = mark("te", "T")
        wait("ve", "T", t_sq)
        em("ve", lambda E: E.tensor_single_scalar(rb, psum_s[0:1, 0:T], 1.0 / C, ALU.mult))
        v_sq = mark("ve", "V")
        wait("sc", "T", t_sq)
        for mt in range(KT):
            em("sc", lambda E, mt=mt: E.copy(h_bf[:, mt * T:(mt + 1) * T], src[:, mt * T:(mt + 1) * T]))
        s_xbf = mark("sc", "S")
        wait("te", "S", s_xbf)
        wait("te", "V", v_sq)
        for c in range(NCH):
            for kt in range(KT):
                o = psum_s[0:1, c * 512:(c + 1) * 512]
                r = h_bf[:, kt * T + c * 512: kt * T + (c + 1) * 512]
                em("te", lambda E, o=o, r=r, kt=kt: E.matmul(o, ones_c[:, :], r, start=(kt == 0), stop=(kt == KT - 1)))
        t_sm = mark("te", "T")
        wait("ve", "T", t_sm)
        em("ve", lambda E: E.tensor_single_scalar(ra, psum_s[0:1, 0:T], 1.0 / C, ALU.mult))
        v_sm = mark("ve", "V")
        wait("sc", "V", v_sm)
        s_sqr = inc("sc", "S", lambda E: E.activation(psum_s[0:1, 0:T], ra, AF.Square))
        wait("ve", "S", s_sqr)
        em("ve", lambda E: E.tensor_sub(rb, rb, psum_s[0:1, 0:T]))
        em("ve", lambda E: E.tensor_single_scalar(rb, rb, 1e-5, ALU.add))
        em("ve", lambda E: E.reciprocal(rc, rb))
        v_var = mark("ve", "V")
        wait("sc", "V", v_var)
        s_rstd = inc("sc", "S", lambda E: E.activation(rb, rc, AF.Sqrt))
        wait("ve", "S", s_rstd)
        em("ve", lambda E: E.tensor_mul(ra, ra, rb))
        v_rows = mark("ve", "V")
        wait("te", "V", v_rows)
        for c in range(NCH):
            em("te", lambda E, c=c: E.matmul(psum_s[:, c * 512:(c + 1) * 512], ones_r[:, :], rb[:, c * 512:(c + 1) * 512], start=True, stop=True))
        t_ar = mark("te", "T")
        wait("ve", "T", t_ar)
        em("ve", lambda E: E.tensor_copy(a_rep[:, :], psum_s[:, 0:T]))
        v_ar = mark("ve", "V")
        wait("te", "V", v_ar)
        for c in range(NCH):
            em("te", lambda E, c=c: E.matmul(psum_s[:, c * 512:(c + 1) * 512], ones_r[:, :], ra[:, c * 512:(c + 1) * 512], start=True, stop=True))
        t_br = mark("te", "T")
        wait("ve", "T", t_br)
        em("ve", lambda E: E.tensor_copy(b_rep[:, :], psum_s[:, 0:T]))
        mark("ve", "V")
        return a_rep, b_rep

    def live_get(name):
        return live[name].tensor.ap() if hasattr(live[name], "tensor") else None

    rowbuf = alloc("rowbuf", [1, 3 * T], F32)
    a_rep = alloc("a_rep", [128, T], F32)
    b_rep = alloc("b_rep", [128, T], F32)
    tmpT = alloc("tmpT", [128, T], F32)

    def live_get(name, _m={"rowbuf": rowbuf, "a_rep": a_rep, "b_rep": b_rep}):
        return _m[name]

    wait("sc", "DX", dX_x)
    emit_ln(x_sb, None, None)
    for mt in range(KT):
        xs = x_sb[:, mt * T:(mt + 1) * T]
        em("ve", lambda E, xs=xs: E.tensor_mul(tmpT[:, :], xs, a_rep[:, :]))
        em("ve", lambda E: E.tensor_sub(tmpT[:, :], tmpT[:, :], b_rep[:, :]))
        em("ve", lambda E, mt=mt: E.tensor_scalar(h_bf[:, mt * T:(mt + 1) * T], tmpT[:, :],
                                                  der[:, 8 + mt:9 + mt], mod_sb[:, 0 + mt:1 + mt],
                                                  op0=ALU.mult, op1=ALU.add))
        em("ve", lambda E, xs=xs, mt=mt: E.tensor_single_scalar(xs, xs, der[:, 48 + mt:49 + mt], ALU.add))
    vs_ln1 = mark("ve", "V")
    wait("sy", "V", vs_ln1)
    dma("sy", xp_dram, x_sb[:, :], "DX")
    dX_spill = cnt["DX"]
    wait("sy", "DX", dX_spill)
    run_phase()
    free("tmpT", "b_rep", "a_rep", "rowbuf", "x_sb")

    if tap == "h1":
        return _finish_tap(nc, sems, cnt, xo, [(h_bf[:, :], 0)]), sem_guards

    q_rot = alloc("q_rot", [128, KT * T], BF16, side="left")
    k_rot = alloc("k_rot", [128, KT * T], BF16, side="left")

    phase()
    wq_sb = alloc("wq_sb", [128, KT * C], BF16)
    wk_sb = alloc("wk_sb", [128, KT * C], BF16)
    cos_sb = alloc("cos_sb", [128, T], BF16)
    sin_sb = alloc("sin_sb", [128, T], BF16)
    qk_raw = alloc("qk_raw", [128, 8 * 512], F32)
    tmpr = alloc("tmpr", [128, 2 * 512], F32)

    dma("sy", cos_sb[:, :], cosr, "DX")
    dma("sy", sin_sb[:, :], sinr, "DX")
    dX_rope = cnt["DX"]
    wait("gp", "T", ts_mod)
    dma("gp", wq_sb[:, :], wqs, "DW")
    dW_q = cnt["DW"]
    dma("gp", wk_sb[:, :], wks, "DW")
    dW_k = cnt["DW"]

    proj_bank_busy = {}
    def emit_proj(w_sb, rot_dst, bias_scaled, scale):
        bank_busy = proj_bank_busy
        for n in range(NCH):
            for m in range(KT):
                bk = m % 4
                o = psum_s[:, bk * 512:(bk + 1) * 512]
                if bank_busy.get(bk) is not None:
                    wait("te", "V", bank_busy[bk])
                for kt in range(KT):
                    l = w_sb[:, kt * C + m * 128: kt * C + (m + 1) * 128]
                    r = h_bf[:, kt * T + n * 512: kt * T + (n + 1) * 512]
                    em("te", lambda E, o=o, l=l, r=r, kt=kt: E.matmul(o, l, r, start=(kt == 0), stop=(kt == KT - 1)))
                tv = mark("te", "T")
                wait("ve", "T", tv)
                em("ve", lambda E, o=o, m=m: E.tensor_scalar(qk_raw[:, m * 512:(m + 1) * 512], o,
                                                             scale, bias_scaled[:, m:m + 1],
                                                             op0=ALU.mult, op1=ALU.add))
                bank_busy[bk] = mark("ve", "V")
            cc = cos_sb[:, n * 512:(n + 1) * 512]
            ss = sin_sb[:, n * 512:(n + 1) * 512]
            for at in range(4):
                A = qk_raw[:, at * 512:(at + 1) * 512]
                Bv = qk_raw[:, (4 + at) * 512:(5 + at) * 512]
                dA = rot_dst[:, at * T + n * 512: at * T + (n + 1) * 512]
                dB = rot_dst[:, (4 + at) * T + n * 512: (4 + at) * T + (n + 1) * 512]
                em("ve", lambda E, A=A, cc=cc: E.tensor_mul(tmpr[:, 0:512], A, cc))
                em("ve", lambda E, Bv=Bv, ss=ss: E.tensor_mul(tmpr[:, 512:1024], Bv, ss))
                em("ve", lambda E, dA=dA: E.tensor_sub(dA, tmpr[:, 0:512], tmpr[:, 512:1024]))
                em("ve", lambda E, A=A, ss=ss: E.tensor_mul(tmpr[:, 0:512], A, ss))
                em("ve", lambda E, Bv=Bv, cc=cc: E.tensor_mul(tmpr[:, 512:1024], Bv, cc))
                em("ve", lambda E, dB=dB: E.tensor_add(dB, tmpr[:, 0:512], tmpr[:, 512:1024]))
            mark("ve", "V")

    wait("te", "DW", dW_q)
    wait("ve", "DX", dX_rope)
    wait("ve", "V", vs_modp)
    emit_proj(wq_sb, q_rot, bqs_s, 0.125)
    wait("te", "DW", dW_k)
    emit_proj(wk_sb, k_rot, bk_sb, 1.0)
    run_phase()
    free("tmpr", "qk_raw", "sin_sb", "cos_sb", "wk_sb", "wq_sb")

    if tap == "qrot":
        return _finish_tap(nc, sems, cnt, xo, [(q_rot[:, :], 0)]), sem_guards
    if tap == "krot":
        return _finish_tap(nc, sems, cnt, xo, [(k_rot[:, :], 0)]), sem_guards

    phase()
    wv_sb = alloc("wv_sb", [128, KT * C], BF16)
    vt_sb = alloc("vt_sb", [128, KT * T], BF16)
    dma("gp", wv_sb[:, :], wvs, "DW")
    dW_v = cnt["DW"]
    wait("te", "DW", dW_v)
    bank_busy = proj_bank_busy
    for n in range(NCH):
        for m in range(KT):
            bk = m % 4
            o = psum_s[:, bk * 512:(bk + 1) * 512]
            if bank_busy.get(bk) is not None:
                wait("te", "V", bank_busy[bk])
            for kt in range(KT):
                l = wv_sb[:, kt * C + m * 128: kt * C + (m + 1) * 128]
                r = h_bf[:, kt * T + n * 512: kt * T + (n + 1) * 512]
                em("te", lambda E, o=o, l=l, r=r, kt=kt: E.matmul(o, l, r, start=(kt == 0), stop=(kt == KT - 1)))
            tv = mark("te", "T")
            wait("ve", "T", tv)
            em("ve", lambda E, o=o, m=m, n=n: E.tensor_single_scalar(vt_sb[:, m * T + n * 512: m * T + (n + 1) * 512],
                                                                     o, bv_sb[:, m:m + 1], ALU.add))
            bank_busy[bk] = mark("ve", "V")
    run_phase()

    phase()
    v_tok = alloc("v_tok", [128, NQT * NH * HD], BF16, side="left")
    ps_tr0 = palloc("ps_tr0", [128, 1024], BF16)
    ps_tr1 = palloc("ps_tr1", [128, 1024], BF16)
    trb = [ps_tr0, ps_tr1]
    tr_busy = [None, None]
    i = 0
    for h in range(NH):
        for lt in range(NQT):
            src = vt_sb[(h % 2) * 64:(h % 2) * 64 + 64, (h // 2) * T + lt * 128: (h // 2) * T + lt * 128 + 128]
            idn64 = idn_b[0:64, 0:64] if h % 2 == 0 else idn_b[64:128, 64:128]
            pb = trb[i % 2]
            if tr_busy[i % 2] is not None:
                wait("te", "V", tr_busy[i % 2])
            tv = inc("te", "T", lambda E, pb=pb, src=src, idn64=idn64: E.transpose(pb[:, 0:64], src, idn64))
            wait("ve", "T", tv)
            dst = v_tok[:, (lt * NH + h) * HD:(lt * NH + h) * HD + HD]
            tr_busy[i % 2] = inc("ve", "V", lambda E, dst=dst, pb=pb: E.tensor_copy(dst, pb[:, 0:64]))
            i += 1
    run_phase()
    free("vt_sb", "wv_sb")

    free("ps_tr1", "ps_tr0")
    phase()
    ps_dn = palloc("ps_dn", [128, 512], F32)
    attn_o = alloc("attn_o", [128, KT * T], BF16)
    pT2 = [alloc("pT0", [128, NQT * 512], BF16), alloc("pT1", [128, NQT * 512], BF16)]
    drow = alloc("drow", [1, 512], F32)
    rep_sb = alloc("rep_sb", [128, 512], F32)
    em("ve", lambda E: E.memset(rscr[:, 6:7], -30.0))
    v_c30 = mark("ve", "V")
    wait("sc", "V", v_c30)

    pv_ps = [ps_a, ps_b]
    pv_busy = [None, None]
    half_free = [None, None]
    pT_free = [None, None]
    dn_free = None
    drow_free = None

    qc_list = [0] if tap == "att0" else range(NCH)
    h_list = [0] if tap == "att0" else range(NH)
    hidx = 0
    for qc in qc_list:
        for h in h_list:
            r0 = 32 * (h % 4)
            a_ti, b_ti = h // 4, 4 + h // 4
            par = hidx % 2
            pT = pT2[par]
            nlt = 4 * qc + 4
            first_exp = None
            for p in range(nlt // 2):
                hf = p % 2
                for li in range(2):
                    lt = 2 * p + li
                    o = psum_s[:, hf * 1024 + li * 512: hf * 1024 + (li + 1) * 512]
                    if li == 0 and half_free[hf] is not None:
                        wait("te", "S", half_free[hf])
                    lA = k_rot[r0:r0 + 32, a_ti * T + lt * 128: a_ti * T + (lt + 1) * 128]
                    rA = q_rot[r0:r0 + 32, a_ti * T + qc * 512: a_ti * T + (qc + 1) * 512]
                    lB = k_rot[r0:r0 + 32, b_ti * T + lt * 128: b_ti * T + (lt + 1) * 128]
                    rB = q_rot[r0:r0 + 32, b_ti * T + qc * 512: b_ti * T + (qc + 1) * 512]
                    em("te", lambda E, o=o, lA=lA, rA=rA, r0=r0: E.matmul(o, lA, rA, start=True, stop=False, tile_position=(r0, 0)))
                    em("te", lambda E, o=o, lB=lB, rB=rB, r0=r0: E.matmul(o, lB, rB, start=False, stop=True, tile_position=(r0, 0)))
                t_s = mark("te", "T")
                need_mask = (2 * p + 1) >= 4 * qc
                v_mask = None
                if need_mask:
                    wait("ve", "T", t_s)
                    for li in range(2):
                        lt = 2 * p + li
                        if lt >= 4 * qc:
                            j = lt - 4 * qc
                            mv = maskT_sb[:, 512 - 128 * j: 1024 - 128 * j]
                            o = psum_s[:, hf * 1024 + li * 512: hf * 1024 + (li + 1) * 512]
                            em("ve", lambda E, o=o, mv=mv: E.tensor_add(o, o, mv))
                    v_mask = mark("ve", "V")
                if v_mask is not None:
                    wait("sc", "V", v_mask)
                else:
                    wait("sc", "T", t_s)
                if p == 0 and pT_free[par] is not None:
                    wait("sc", "T", pT_free[par])
                dst = pT[:, 2 * p * 512: (2 * p + 2) * 512]
                src = psum_s[:, hf * 1024: hf * 1024 + 1024]
                half_free[hf] = inc("sc", "S", lambda E, dst=dst, src=src:
                                    E.activation(dst, src, AF.Exp, bias=rscr[:, 6:7]))
                if first_exp is None:
                    first_exp = half_free[hf]
            s_done = half_free[(nlt // 2 - 1) % 2]
            if dn_free is not None:
                wait("te", "V", dn_free)
            wait("te", "S", s_done)
            for lt in range(nlt):
                o = ps_dn[0:1, 0:512]
                r = pT[:, lt * 512:(lt + 1) * 512]
                em("te", lambda E, o=o, r=r, lt=lt, nlt=nlt: E.matmul(o, ones_c[:, :], r, start=(lt == 0), stop=(lt == nlt - 1)))
            t_dn = mark("te", "T")
            pp = pv_ps[h % 2]
            row0 = 64 * (h % 2)
            if pv_busy[h % 2] is not None:
                wait("te", "V", pv_busy[h % 2])
            for lt in range(nlt):
                l = v_tok[:, (lt * NH + h) * HD:(lt * NH + h) * HD + HD]
                r = pT[:, lt * 512:(lt + 1) * 512]
                o = pp[row0:row0 + 64, 0:512]
                em("te", lambda E, o=o, l=l, r=r, lt=lt, row0=row0, nlt=nlt:
                   E.matmul(o, l, r, start=(lt == 0), stop=(lt == nlt - 1), tile_position=(0, row0)))
            t_pv = mark("te", "T")
            pT_free[par] = t_pv
            wait("ve", "T", t_dn)
            if drow_free is not None:
                wait("ve", "T", drow_free)
            v_rc = inc("ve", "V", lambda E: E.reciprocal(drow[0:1, :], ps_dn[0:1, 0:512]))
            wait("te", "V", v_rc)
            t_rep = inc("te", "T", lambda E: E.matmul(ps_dn[:, 0:512], ones_r[:, :], drow[0:1, :], start=True, stop=True))
            drow_free = t_rep
            wait("ve", "T", t_rep)
            v_rep = inc("ve", "V", lambda E: E.tensor_copy(rep_sb[:, :], ps_dn[:, 0:512]))
            dn_free = v_rep
            wait("ve", "T", t_pv)
            dst = attn_o[row0:row0 + 64, (h // 2) * T + qc * 512: (h // 2) * T + (qc + 1) * 512]
            pv_busy[h % 2] = inc("ve", "V", lambda E, dst=dst, pp=pp, row0=row0:
                                 E.tensor_mul(dst, pp[row0:row0 + 64, 0:512], rep_sb[row0:row0 + 64, :]))
            hidx += 1
    run_phase()
    if tap == "att0":
        return _finish_tap(nc, sems, cnt, xo, [(pT2[0][:, :], 0), (attn_o[:, 0:512], 8192),
                                               (rep_sb[:, :], 8192 + 512)]), sem_guards
    free("rep_sb", "drow", "pT1", "pT0", "v_tok", "k_rot", "q_rot", "ps_dn")

    if tap == "attn":
        return _finish_tap(nc, sems, cnt, xo, [(attn_o[:, :], 0)]), sem_guards

    xacc = alloc("xacc", [128, KT * T], F32, side="left")
    phase()
    wo_sb = alloc("wo_sb", [128, KT * C], BF16)
    dma("gp", wo_sb[:, :], wos, "DW")
    dW_o = cnt["DW"]
    dma("sy", xacc[:, :], xp_dram, "DX")
    dX_xp = cnt["DX"]
    wait("sy", "DX", dX_xp)
    if tap == "xp":
        run_phase()
        return _finish_tap(nc, sems, cnt, xo, [(xacc[:, :], 0)]), sem_guards
    wait("te", "DW", dW_o)
    wait("ve", "DX", dX_xp)
    bank_busy = {}
    for n in range(NCH):
        for m in range(KT):
            bk = m % 4
            o = psum_s[:, bk * 512:(bk + 1) * 512]
            if bank_busy.get(bk) is not None:
                wait("te", "V", bank_busy[bk])
            for kt in range(KT):
                l = wo_sb[:, kt * C + m * 128: kt * C + (m + 1) * 128]
                r = attn_o[:, kt * T + n * 512: kt * T + (n + 1) * 512]
                em("te", lambda E, o=o, l=l, r=r, kt=kt: E.matmul(o, l, r, start=(kt == 0), stop=(kt == KT - 1)))
            tv = mark("te", "T")
            wait("ve", "T", tv)
            dst = xacc[:, m * T + n * 512: m * T + (n + 1) * 512]
            em("ve", lambda E, dst=dst, o=o, m=m: E.scalar_tensor_tensor(dst, o, der[:, 16 + m:17 + m], dst,
                                                                         op0=ALU.mult, op1=ALU.add))
            bank_busy[bk] = mark("ve", "V")
    run_phase()
    free("wo_sb", "attn_o")

    if tap == "x2":
        return _finish_tap(nc, sems, cnt, xo, [(xacc[:, :], 0)]), sem_guards

    idx16 = alloc("idx16", [128, 3072], I16)
    cw_sb = alloc("cw_sb", [128, 6 * 64], F32)
    h_lo = alloc("h_lo", [128, KT * T], BF16)
    rowbuf = alloc("rowbuf2", [1, 3 * T], F32)
    a_rep = alloc("a_rep2", [128, T], F32)
    b_rep = alloc("b_rep2", [128, T], F32)
    tmpT = alloc("tmpT2", [128, T], F32)

    def live_get(name, _m=None):
        return {"rowbuf": rowbuf, "a_rep": a_rep, "b_rep": b_rep}[name]

    phase()
    emit_ln(xacc, None, None)
    for mt in range(KT):
        xs = xacc[:, mt * T:(mt + 1) * T]
        em("ve", lambda E, xs=xs: E.tensor_mul(tmpT[:, :], xs, a_rep[:, :]))
        em("ve", lambda E: E.tensor_sub(tmpT[:, :], tmpT[:, :], b_rep[:, :]))
        em("ve", lambda E, mt=mt: E.tensor_scalar(tmpT[:, :], tmpT[:, :],
                                                  der[:, 32 + mt:33 + mt], mod_sb[:, 24 + mt:25 + mt],
                                                  op0=ALU.mult, op1=ALU.add))
        v_h2 = mark("ve", "V")
        wait("sc", "V", v_h2)
        s_hbf = inc("sc", "S", lambda E, mt=mt: E.copy(h_bf[:, mt * T:(mt + 1) * T], tmpT[:, :]))
        wait("ve", "S", s_hbf)
        em("ve", lambda E, mt=mt: E.tensor_sub(h_lo[:, mt * T:(mt + 1) * T], tmpT[:, :], h_bf[:, mt * T:(mt + 1) * T]))
        em("ve", lambda E, xs=xs, mt=mt: E.tensor_single_scalar(xs, xs, der[:, 56 + mt:57 + mt], ALU.mult))
    vs_ln2 = mark("ve", "V")
    wait("sy", "V", vs_ln2)
    dma("sy", xp_dram, xacc[:, :], "DX")
    wait("sy", "DX", cnt["DX"])
    run_phase()
    free("tmpT2", "b_rep2", "a_rep2", "rowbuf2")
    free("xacc")

    if tap == "h2":
        return _finish_tap(nc, sems, cnt, xo, [(h_bf[:, :], 0)]), sem_guards

    h2tok_sb = alloc("h2tok_sb", [128, 17 * C], BF16, side="left")
    phase()
    ps_tb0 = palloc("ps_tb0", [128, 1024], BF16)
    ps_tb1 = palloc("ps_tb1", [128, 1024], BF16)
    ptb = [ps_tb0, ps_tb1]
    tr_busy = [None, None]
    em("ve", lambda E: E.memset(h2tok_sb[:, 16 * C:17 * C], 0.0))
    ii = 0
    for tt in range(NQT):
        for kt in range(KT):
            pb = ptb[ii % 2]
            if tr_busy[ii % 2] is not None:
                wait("te", "V", tr_busy[ii % 2])
            tv = inc("te", "T", lambda E, pb=pb, kt=kt, tt=tt:
                     E.transpose(pb[:, 0:128], h_bf[:, kt * T + tt * 128: kt * T + (tt + 1) * 128], idn_b[:, :]))
            wait("ve", "T", tv)
            tr_busy[ii % 2] = inc("ve", "V", lambda E, pb=pb, kt=kt, tt=tt:
                                  E.tensor_copy(h2tok_sb[:, tt * C + kt * 128: tt * C + (kt + 1) * 128], pb[:, 0:128]))
            ii += 1
    v_h2tok = cnt["V"]
    run_phase()
    free("ps_tb1", "ps_tb0")

    phase()
    scoresT = alloc("scoresT", [64, T], F32)
    gw_hi = alloc("gw_hi", [128, KT * 64], BF16)
    gw_lo = alloc("gw_lo", [128, KT * 64], BF16)
    s_tok = alloc("s_tok", [128, NQT * 64], F32)
    sc_tok = alloc("sc_tok", [128, NQT * 64], F32)
    w_tok = alloc("w_tok", [128, NQT * 64], F32)
    r16 = alloc("r16", [128, NQT * 8], F32)
    r16b = alloc("r16b", [128, NQT * 8], F32)
    rr = alloc("rr", [128, NQT], F32)
    rr2 = alloc("rr2", [128, NQT], F32)

    s_hi = inc("sc", "S", lambda E: E.copy(gw_hi[:, :], gw_f[:, :]))
    wait("ve", "S", s_hi)
    v_lo = inc("ve", "V", lambda E: E.tensor_sub(gw_lo[:, :], gw_f[:, :], gw_hi[:, :]))
    wait("te", "V", v_lo)
    for c in range(NCH):
        o = psum_s[0:64, c * 512:(c + 1) * 512]
        first = True
        for kt in range(KT):
            for (wl, rh) in ((gw_hi, h_bf), (gw_hi, h_lo), (gw_lo, h_bf)):
                l = wl[:, kt * 64:(kt + 1) * 64]
                r = rh[:, kt * T + c * 512: kt * T + (c + 1) * 512]
                last = (kt == KT - 1) and (wl is gw_lo)
                em("te", lambda E, o=o, l=l, r=r, first=first, last=last: E.matmul(o, l, r, start=first, stop=last))
                first = False
    t_gs = mark("te", "T")
    wait("sc", "T", t_gs)
    s_sig = inc("sc", "S", lambda E: E.activation(scoresT[:, :], psum_s[0:64, 0:T], AF.Sigmoid))
    wait("te", "S", s_sig)
    tp = [ps_a, ps_b]; tb = [None, None]
    for tt in range(NQT):
        pb = tp[tt % 2]
        if tb[tt % 2] is not None:
            wait("te", "V", tb[tt % 2])
        tv = inc("te", "T", lambda E, pb=pb, tt=tt: E.transpose(pb[:, 0:64], scoresT[:, tt * 128:(tt + 1) * 128], idn_f[0:64, 0:64]))
        wait("ve", "T", tv)
        tb[tt % 2] = inc("ve", "V", lambda E, pb=pb, tt=tt: E.tensor_copy(s_tok[:, tt * 64:(tt + 1) * 64], pb[:, 0:64]))
    E3 = lambda t: t.rearrange("p (a e) -> p a e", e=64)
    G3 = lambda t: t.rearrange("p (a g e) -> p (a g) e", g=8, e=8)
    GS = lambda t: t.rearrange("p (a g) -> p a g", g=8)
    bview = brep[:, :].rearrange("p (o e) -> p o e", o=1).broadcast_to([128, NQT, 64])
    em("ve", lambda E: E.tensor_tensor(E3(sc_tok[:, :]), E3(s_tok[:, :]), bview, op=ALU.add))
    em("ve", lambda E: E.tensor_reduce(r16[:, :].rearrange("p (a g o) -> p (a g) o", g=8, o=1),
                                       G3(sc_tok[:, :]), axis=AX.X, op=ALU.max))
    gmb = r16[:, :].rearrange("p (q o) -> p q o", o=1).broadcast_to([128, 128, 8])
    em("ve", lambda E: E.tensor_tensor(G3(w_tok[:, :]),
                                       G3(sc_tok[:, :]), gmb, op=ALU.is_equal))
    em("ve", lambda E: E.scalar_tensor_tensor(w_tok[:, :], w_tok[:, :], -1e30, sc_tok[:, :], op0=ALU.mult, op1=ALU.add))
    em("ve", lambda E: E.tensor_reduce(r16b[:, :].rearrange("p (q o) -> p q o", o=1),
                                       G3(w_tok[:, :]), axis=AX.X, op=ALU.max))
    em("ve", lambda E: E.tensor_add(r16[:, :], r16[:, :], r16b[:, :]))
    for it in range(3):
        em("ve", lambda E: E.tensor_reduce(rr[:, :].rearrange("p (a o) -> p a o", o=1),
                                           GS(r16[:, :]), axis=AX.X, op=ALU.max))
        vsep("ve", "V")
        rrb = rr[:, :].rearrange("p (a o) -> p a o", o=1).broadcast_to([128, NQT, 8])
        em("ve", lambda E: E.tensor_tensor(GS(r16b[:, :]), GS(r16[:, :]), rrb, op=ALU.is_equal))
        em("ve", lambda E: E.scalar_tensor_tensor(r16[:, :], r16b[:, :], -1e30, r16[:, :], op0=ALU.mult, op1=ALU.add))
    em("ve", lambda E: E.tensor_reduce(rr[:, :].rearrange("p (a o) -> p a o", o=1),
                                       GS(r16[:, :]), axis=AX.X, op=ALU.max))
    vsep("ve", "V")
    em("ve", lambda E: E.tensor_reduce(r16[:, :].rearrange("p (q o) -> p q o", o=1),
                                       G3(sc_tok[:, :]), axis=AX.X, op=ALU.max))
    em("ve", lambda E: E.tensor_tensor(G3(w_tok[:, :]),
                                       G3(sc_tok[:, :]),
                                       r16[:, :].rearrange("p (q o) -> p q o", o=1).broadcast_to([128, 128, 8]), op=ALU.is_equal))
    em("ve", lambda E: E.scalar_tensor_tensor(w_tok[:, :], w_tok[:, :], -1e30, sc_tok[:, :], op0=ALU.mult, op1=ALU.add))
    em("ve", lambda E: E.tensor_reduce(r16b[:, :].rearrange("p (q o) -> p q o", o=1),
                                       G3(w_tok[:, :]), axis=AX.X, op=ALU.max))
    em("ve", lambda E: E.tensor_add(r16[:, :], r16[:, :], r16b[:, :]))
    rrb = rr[:, :].rearrange("p (a o) -> p a o", o=1).broadcast_to([128, NQT, 8])
    em("ve", lambda E: E.tensor_tensor(GS(r16b[:, :]), GS(r16[:, :]), rrb, op=ALU.is_lt))
    em("ve", lambda E: E.tensor_single_scalar(r16b[:, :], r16b[:, :], -1e30, ALU.mult))
    penb = r16b[:, :].rearrange("p (q o) -> p q o", o=1).broadcast_to([128, 128, 8])
    em("ve", lambda E: E.tensor_tensor(G3(w_tok[:, :]), G3(sc_tok[:, :]), penb, op=ALU.add))
    for it in range(7):
        em("ve", lambda E: E.tensor_reduce(rr[:, :].rearrange("p (a o) -> p a o", o=1),
                                           E3(w_tok[:, :]), axis=AX.X, op=ALU.max))
        vsep("ve", "V")
        rrb2 = rr[:, :].rearrange("p (a o) -> p a o", o=1).broadcast_to([128, NQT, 64])
        em("ve", lambda E: E.tensor_tensor(E3(sc_tok[:, :]), E3(w_tok[:, :]), rrb2, op=ALU.is_equal))
        em("ve", lambda E: E.scalar_tensor_tensor(w_tok[:, :], sc_tok[:, :], -1e30, w_tok[:, :], op0=ALU.mult, op1=ALU.add))
    em("ve", lambda E: E.tensor_reduce(rr[:, :].rearrange("p (a o) -> p a o", o=1),
                                       E3(w_tok[:, :]), axis=AX.X, op=ALU.max))
    vsep("ve", "V")
    em("ve", lambda E: E.tensor_tensor(E3(sc_tok[:, :]), E3(s_tok[:, :]), bview, op=ALU.add))
    em("ve", lambda E: E.tensor_tensor(G3(w_tok[:, :]), G3(sc_tok[:, :]), penb, op=ALU.add))
    rrb2 = rr[:, :].rearrange("p (a o) -> p a o", o=1).broadcast_to([128, NQT, 64])
    em("ve", lambda E: E.tensor_tensor(E3(sc_tok[:, :]), E3(w_tok[:, :]), rrb2, op=ALU.is_ge))
    em("ve", lambda E: E.tensor_mul(w_tok[:, :], sc_tok[:, :], s_tok[:, :]))
    em("ve", lambda E: E.tensor_reduce(rr[:, :].rearrange("p (a o) -> p a o", o=1),
                                       E3(w_tok[:, :]), axis=AX.X, op=ALU.add))
    vsep("ve", "V")
    em("ve", lambda E: E.tensor_single_scalar(rr[:, :], rr[:, :], 1e-20, ALU.add))
    vsep("ve", "V")
    em("ve", lambda E: E.reciprocal(rr2[:, :], rr[:, :]))
    vsep("ve", "V")
    em("ve", lambda E: E.tensor_single_scalar(rr[:, :], rr2[:, :], 2.5, ALU.mult))
    vsep("ve", "V")
    rrb2 = rr[:, :].rearrange("p (a o) -> p a o", o=1).broadcast_to([128, NQT, 64])
    em("ve", lambda E: E.tensor_tensor(E3(w_tok[:, :]), E3(w_tok[:, :]), rrb2, op=ALU.mult))
    v_rt = mark("ve", "V")
    run_phase()

    if tap == "wd":
        return _finish_tap(nc, sems, cnt, xo, [(w_tok[:, :], 0), (sc_tok[:, :], 1024)]), sem_guards

    phase()
    selb = alloc("selb", [128, NQT * 64], BF16)
    rank_f = alloc("rank_f", [128, NQT * 64], F32)
    prank_f = alloc("prank_f", [128, NQT * 64], F32)
    selT = alloc("selT", [64, T], BF16)
    prT = alloc("prT", [64, T], F32)
    dmA = alloc("dmA", [128, NQT * 64], F32)
    dmB = alloc("dmB", [128, NQT * 64], F32)
    mscr = alloc("mscr", [128, NQT * 64], F32)
    mscr2 = alloc("mscr2", [128, NQT * 64], F32)
    dAall = alloc("dAall", [128, 128], F32)
    dBall = alloc("dBall", [128, 128], F32)
    cwall = alloc("cwall", [128, 128], F32)
    dAi = alloc("dAi", [128, 128], I32)
    dBi = alloc("dBi", [128, 128], I32)
    ri32 = alloc("ri32", [128, NQT * 64], I32)
    rs32 = alloc("rs32", [128, NQT * 64], I32)
    fill16 = alloc("fill16", [16, 3072], I16)
    zf32 = alloc("zf32", [128, 384], F32)

    em("ve", lambda E: E.memset(fill16[:, :], 2048))
    em("ve", lambda E: E.memset(zf32[:, :], 0.0))
    em("ve", lambda E: E.memset(idx16[:, :], 0))
    em("ve", lambda E: E.tensor_copy(selb[:, :], sc_tok[:, :]))
    v_selb = mark("ve", "V")

    wait("gp", "V", v_selb)
    dma("gp", bass.AP(lists_dram.tensor, 0, [[3072, 16], [1, 3072]]), fill16[:, :], "DW")
    dma("gp", bass.AP(lists_dram.tensor, 16 * 3072, [[64, 1], [1, 64]]), fill16[0:1, 0:64], "DW")
    dma("gp", bass.AP(cwrep_dram.tensor, 0, [[384, 128], [1, 384]]), zf32[:, :], "DW")
    dma("gp", bass.AP(cwrep_dram.tensor, 64 * CAP, [[16, 1], [1, 16]]), zf32[0:1, 0:16], "DW")
    dW_prefill = cnt["DW"]

    wait("te", "V", v_selb)
    tp = [ps_a, ps_b]
    tb = [None, None]
    for mt in range(NQT):
        pb = tp[mt % 2]
        if tb[mt % 2] is not None:
            wait("te", "V", tb[mt % 2])
        for kt in range(mt + 1):
            l = ones128b[:, :] if kt < mt else triSb[:, :]
            r = selb[:, kt * 64:(kt + 1) * 64]
            em("te", lambda E, pb=pb, l=l, r=r, kt=kt, mt=mt:
               E.matmul(pb[:, 0:64], l, r, start=(kt == 0), stop=(kt == mt)))
        tv = mark("te", "T")
        wait("ve", "T", tv)
        tb[mt % 2] = inc("ve", "V", lambda E, pb=pb, mt=mt: E.tensor_copy(rank_f[:, mt * 64:(mt + 1) * 64], pb[:, 0:64]))
    for tt in range(NQT):
        pb = tp[tt % 2]
        if tb[tt % 2] is not None:
            wait("te", "V", tb[tt % 2])
        tv = inc("te", "T", lambda E, pb=pb, tt=tt: E.transpose(pb[0:64, 0:128], sc_tok[:, tt * 64:(tt + 1) * 64], idn_f[:, :]))
        wait("ve", "T", tv)
        tb[tt % 2] = inc("ve", "V", lambda E, pb=pb, tt=tt: E.tensor_copy(selT[:, tt * 128:(tt + 1) * 128], pb[0:64, 0:128]))
    wait("te", "V", tb[(NQT - 1) % 2])
    for c in range(NCH):
        pb = tp[c % 2]
        if tb[c % 2] is not None:
            wait("te", "V", tb[c % 2])
        tv = inc("te", "T", lambda E, pb=pb, c=c: E.matmul(pb[0:64, 0:512], tri64b[:, :], selT[:, c * 512:(c + 1) * 512],
                                                           start=True, stop=True))
        wait("ve", "T", tv)
        tb[c % 2] = inc("ve", "V", lambda E, pb=pb, c=c: E.tensor_copy(prT[:, c * 512:(c + 1) * 512], pb[0:64, 0:512]))
    for tt in range(NQT):
        pb = tp[tt % 2]
        if tb[tt % 2] is not None:
            wait("te", "V", tb[tt % 2])
        tv = inc("te", "T", lambda E, pb=pb, tt=tt: E.transpose(pb[:, 0:64], prT[0:64, tt * 128:(tt + 1) * 128], idn_f[0:64, 0:64]))
        wait("ve", "T", tv)
        tb[tt % 2] = inc("ve", "V", lambda E, pb=pb, tt=tt: E.tensor_copy(prank_f[:, tt * 64:(tt + 1) * 64], pb[:, 0:64]))

    E48 = e48c[:, :].rearrange("p (o e) -> p o e", o=1).broadcast_to([128, NQT, 64])
    E6 = e6c[:, :].rearrange("p (o e) -> p o e", o=1).broadcast_to([128, NQT, 64])
    em("ve", lambda E: E.tensor_copy(ri32[:, :], rank_f[:, :]))
    em("ve", lambda E: E.tensor_single_scalar(rs32[:, :], ri32[:, :], 15, ALU.bitwise_and))
    em("ve", lambda E: E.tensor_copy(mscr[:, :], rs32[:, :]))
    em("ve", lambda E: E.tensor_single_scalar(rs32[:, :], ri32[:, :], 4, ALU.arith_shift_right))
    em("ve", lambda E: E.tensor_copy(mscr2[:, :], rs32[:, :]))
    em("ve", lambda E: E.tensor_single_scalar(dmA[:, :], mscr[:, :], 3072.0, ALU.mult))
    em("ve", lambda E: E.tensor_add(dmA[:, :], dmA[:, :], mscr2[:, :]))
    em("ve", lambda E: E.tensor_tensor(E3(dmA[:, :]), E3(dmA[:, :]), E48, op=ALU.add))
    em("ve", lambda E: E.tensor_single_scalar(rs32[:, :], ri32[:, :], 127, ALU.bitwise_and))
    em("ve", lambda E: E.tensor_copy(mscr[:, :], rs32[:, :]))
    em("ve", lambda E: E.tensor_single_scalar(rs32[:, :], ri32[:, :], 7, ALU.arith_shift_right))
    em("ve", lambda E: E.tensor_copy(mscr2[:, :], rs32[:, :]))
    em("ve", lambda E: E.tensor_single_scalar(dmB[:, :], mscr[:, :], 384.0, ALU.mult))
    em("ve", lambda E: E.tensor_add(dmB[:, :], dmB[:, :], mscr2[:, :]))
    em("ve", lambda E: E.tensor_tensor(E3(dmB[:, :]), E3(dmB[:, :]), E6, op=ALU.add))
    em("ve", lambda E: E.tensor_single_scalar(mscr[:, :], rank_f[:, :], float(CAP), ALU.is_lt))
    em("ve", lambda E: E.tensor_mul(dmA[:, :], dmA[:, :], mscr[:, :]))
    em("ve", lambda E: E.tensor_scalar(mscr2[:, :], mscr[:, :], -float(16 * 3072), float(16 * 3072), op0=ALU.mult, op1=ALU.add))
    em("ve", lambda E: E.tensor_add(dmA[:, :], dmA[:, :], mscr2[:, :]))
    em("ve", lambda E: E.tensor_mul(dmB[:, :], dmB[:, :], mscr[:, :]))
    em("ve", lambda E: E.tensor_scalar(mscr2[:, :], mscr[:, :], -float(64 * CAP), float(64 * CAP), op0=ALU.mult, op1=ALU.add))
    em("ve", lambda E: E.tensor_add(dmB[:, :], dmB[:, :], mscr2[:, :]))
    for k in range(8):
        em("ve", lambda E, k=k: E.tensor_single_scalar(mscr[:, :], prank_f[:, :], float(k), ALU.is_equal))
        em("ve", lambda E: E.tensor_mul(mscr[:, :], mscr[:, :], sc_tok[:, :]))
        em("ve", lambda E: E.tensor_mul(mscr2[:, :], mscr[:, :], dmA[:, :]))
        em("ve", lambda E, k=k: E.tensor_reduce(dAall[:, k * 16:(k + 1) * 16].rearrange("p (a o) -> p a o", o=1),
                                                E3(mscr2[:, :]), axis=AX.X, op=ALU.add))
        em("ve", lambda E: E.tensor_mul(mscr2[:, :], mscr[:, :], dmB[:, :]))
        em("ve", lambda E, k=k: E.tensor_reduce(dBall[:, k * 16:(k + 1) * 16].rearrange("p (a o) -> p a o", o=1),
                                                E3(mscr2[:, :]), axis=AX.X, op=ALU.add))
        em("ve", lambda E: E.tensor_mul(mscr2[:, :], mscr[:, :], w_tok[:, :]))
        em("ve", lambda E, k=k: E.tensor_reduce(cwall[:, k * 16:(k + 1) * 16].rearrange("p (a o) -> p a o", o=1),
                                                E3(mscr2[:, :]), axis=AX.X, op=ALU.add))
    vsep("ve", "V")
    em("ve", lambda E: E.tensor_copy(dAi[:, :], dAall[:, :]))
    em("ve", lambda E: E.tensor_copy(dBi[:, :], dBall[:, :]))
    v_disp = mark("ve", "V")

    wait("gp", "DW", dW_prefill)
    wait("gp", "V", v_disp)
    for tt in range(NQT):
        for k in range(8):
            col = k * 16 + tt
            em("gp", lambda E, col=col, tt=tt: E.indirect_dma_start(
                bass.AP(lists_dram.tensor, 0, [[1, 16 * 3072 + 64], [1, 1]]),
                IndirectOffsetOnAxis(dAi[:, col:col + 1], 0),
                tok16[:, tt:tt + 1], None))
            cnt["DW"] += 16
            sched["gp"][-1] = (lambda f: lambda E: f(E).then_inc(sems["DW"], 16))(sched["gp"][-1])
            em("gp", lambda E, col=col: E.indirect_dma_start(
                bass.AP(cwrep_dram.tensor, 0, [[1, 64 * CAP + 16], [1, 1]]),
                IndirectOffsetOnAxis(dBi[:, col:col + 1], 0),
                cwall[:, col:col + 1], None))
            cnt["DW"] += 16
            sched["gp"][-1] = (lambda f: lambda E: f(E).then_inc(sems["DW"], 16))(sched["gp"][-1])
    wait("gp", "DW", cnt["DW"])
    dma("gp", idx16[:, :], bass.AP(lists_dram.tensor, 0, [[0, 8], [3072, 16], [1, 3072]]), "DW")
    dma("gp", cw_sb[:, :], bass.AP(cwrep_dram.tensor, 0, [[384, 128], [1, 384]]), "DW")
    dW_disp = cnt["DW"]
    run_phase()
    free("zf32", "fill16", "rs32", "ri32", "dBi", "dAi", "cwall", "dBall", "dAall", "mscr2", "mscr",
         "dmB", "dmA", "prT", "selT", "prank_f", "rank_f", "selb")
    free("rr2", "rr", "r16b", "r16", "w_tok", "sc_tok", "s_tok", "gw_lo", "gw_hi", "scoresT", "h_lo")

    if tap == "disp":
        return _finish_tap(nc, sems, cnt, xo, [(idx16[:, :], 0), (cw_sb[:, :], 4096)]), sem_guards

    phase()
    ws_g = alloc("ws_g", [128, KT * FF], BF16)
    ws_u = alloc("ws_u", [128, KT * FF], BF16)
    ws_d = alloc("ws_d", [128, 4 * C], BF16)
    hsh = alloc("hsh", [128, 4 * T], BF16)
    sg_sh = alloc("sg_sh", [128, 4 * 512], BF16)
    ysh_st = alloc("ysh_st", [128, 2 * C], F32)
    ps_c = palloc("ps_c", [128, 512], F32)
    ps_d = palloc("ps_d", [128, 512], F32)

    dma("gp", ws_g[:, :], wsg, "DW")
    dma("gp", ws_u[:, :], wsu, "DW")
    dma("gp", ws_d[:, :], wsd, "DW")
    dW_sh = cnt["DW"]

    gb_banks = [psum_s[:, 0:512], psum_s[:, 512:1024], psum_s[:, 1024:1536], psum_s[:, 1536:2048]]
    d_banks = [ps_a, ps_b, ps_c, ps_d]
    d_busy = [None, None, None, None]
    gb_busy = [None, None, None, None]

    em("ve", lambda E: E.memset(ysh_st[:, :], 0.0))
    v_zy = mark("ve", "V")
    wait("sy", "V", v_zy)
    dma("sy", bass.AP(ytok_dram.tensor, T * C, [[C, 1], [1, C]]), ysh_st[0:1, 0:C], "DX")
    dma("sy", bass.AP(ytok2_dram.tensor, 0, [[C, 128], [C * 128, 16], [1, C]]),
        bass.AP(ysh_st[:, :].tensor, 0, [[2 * C, 128], [0, 16], [1, C]]), "DX")
    dma("sy", bass.AP(ytok2_dram.tensor, T * C, [[C, 1], [1, C]]), ysh_st[0:1, 0:C], "DX")
    dX_ytrash = cnt["DX"]
    wait("sy", "DX", dX_ytrash)

    s_si = [None]
    sh_prev_hm = [None]
    def sh_gu(n):
        v_hm = None
        for mh in range(2):
            for src, bks, tag in ((ws_g, (0, 1), "g"), (ws_u, (2, 3), "u")):
                for mi, bki in enumerate(bks):
                    m = mh * 2 + mi
                    o = gb_banks[bki]
                    if gb_busy[bki] is not None:
                        eng, v = gb_busy[bki]
                        wait("te", eng, v)
                        gb_busy[bki] = None
                    for kt in range(KT):
                        l = src[:, kt * FF + m * 128: kt * FF + (m + 1) * 128]
                        r = h_bf[:, kt * T + n * 512: kt * T + (n + 1) * 512]
                        em("te", lambda E, o=o, l=l, r=r, kt=kt: E.matmul(o, l, r, start=(kt == 0), stop=(kt == KT - 1)))
                tg = mark("te", "T")
                if tag == "g":
                    wait("sc", "T", tg)
                    if sh_prev_hm[0] is not None:
                        wait("sc", "V", sh_prev_hm[0])
                    for mi, bki in enumerate(bks):
                        m = mh * 2 + mi
                        em("sc", lambda E, m=m, bki=bki: E.activation(sg_sh[:, m * 512:(m + 1) * 512], gb_banks[bki], AF.Silu))
                    s_si[0] = mark("sc", "S")
                    for bki in bks:
                        gb_busy[bki] = ("S", s_si[0])
                else:
                    wait("ve", "T", tg)
                    wait("ve", "S", s_si[0])
                    for mi, bki in enumerate(bks):
                        m = mh * 2 + mi
                        dst = hsh[:, m * T + n * 512: m * T + (n + 1) * 512]
                        em("ve", lambda E, dst=dst, m=m, bki=bki: E.tensor_mul(dst, sg_sh[:, m * 512:(m + 1) * 512], gb_banks[bki]))
                    v_hm = mark("ve", "V")
                    sh_prev_hm[0] = v_hm
                    for bki in bks:
                        gb_busy[bki] = ("V", v_hm)
        return v_hm

    wait("te", "DW", dW_sh)
    v_hsh = None
    for n in range(NCH):
        v_hsh = sh_gu(n)
    st_rot = 0
    st_busy2 = [None, None]
    for tw in range(NQT):
        half = tw % 2
        if st_busy2[half] is not None:
            wait("ve", "DX", st_busy2[half])
        for cc in range(2):
            bank = d_banks[(st_rot := st_rot + 1) % 4]
            if d_busy[st_rot % 4] is not None:
                wait("te", "V", d_busy[st_rot % 4])
                d_busy[st_rot % 4] = None
            if tw == 0 and cc == 0:
                wait("te", "V", v_hsh)
            for kt in range(4):
                l = hsh[:, kt * T + tw * 128: kt * T + (tw + 1) * 128]
                r = ws_d[:, kt * C + cc * 512:(kt * C + cc * 512) + 512]
                em("te", lambda E, bank=bank, l=l, r=r, kt=kt: E.matmul(bank[:, :], l, r, start=(kt == 0), stop=(kt == 3)))
            td = mark("te", "T")
            wait("ve", "T", td)
            dst = ysh_st[:, half * C + cc * 512: half * C + (cc + 1) * 512]
            em("ve", lambda E, dst=dst, bank=bank: E.tensor_copy(dst, bank[:, :]))
            d_busy[st_rot % 4] = mark("ve", "V")
        v_ysh = cnt["V"]
        wait("sy", "V", v_ysh)
        dma("sy", bass.AP(ytok_dram.tensor, tw * 128 * C, [[C, 128], [1, C]]), ysh_st[:, half * C:(half + 1) * C], "DX")
        st_busy2[half] = cnt["DX"]
    dX_shared = cnt["DX"]
    run_phase()
    if tap == "ysh":
        return _finish_tap_dram(nc, sems, cnt, xo, ytok_dram, 16384), sem_guards
    free("ysh_st", "sg_sh", "hsh", "ws_d", "ws_u", "ws_g")

    exp_sb = [alloc("exp0", [128, EPB], BF16), alloc("exp1", [128, EPB], BF16)]
    gat = [alloc("gat0", [128, 8 * CAP], BF16), alloc("gat1", [128, 8 * CAP], BF16)]
    hm = [alloc("hm0", [128, 4 * CAP], BF16), alloc("hm1", [128, 4 * CAP], BF16)]
    sg = alloc("sg", [128, 4 * CAP], BF16)
    y_sb = alloc("y_sb", [128, 6 * C], F32)
    phase()
    slot_done = [None] * NE
    gu_done = [None] * NE
    dW_exp = [None] * NE
    dW_gat = [None] * NE
    dW_scat = [None] * NE
    hm_guard = [None, None]
    CH = [(0, 512), (512, 256)]

    def emit_exp_dma(e):
        if e >= 2:
            wait("gp", "T", slot_done[e - 2])
        src = bass.AP(experts.tensor, e * 128 * EPB, [[EPB, 128], [1, EPB]])
        dma("gp", exp_sb[e % 2][:, :], src, "DW")
        dW_exp[e] = cnt["DW"]

    def emit_gather(e):
        if e >= 2:
            wait("gp", "T", gu_done[e - 2])
        em("gp", lambda E, e=e: E.dma_gather(
            gat[e % 2][:, :].rearrange("p (a b) -> p a b", b=CAP),
            h2tok_sb[:, :],
            idx16[:, e * 48:(e + 1) * 48], CAP, nidx_reg, C, transpose=True,
            sbuf_tokens_per_rank=128, sbuf_free_dim_per_rank=2 * C,
            sbuf_free_dim_pad_per_rank=0, sbuf_byte_offset=0))
        cnt["DW"] += 16
        sched["gp"][-1] = (lambda f: lambda E: f(E).then_inc(sems["DW"], 16))(sched["gp"][-1])
        dW_gat[e] = cnt["DW"]

    nidx_reg = nc.gpsimd.alloc_register("nidx_reg")
    em("gp", lambda E: E.reg_mov(nidx_reg, CAP))
    wait("gp", "V", v_h2tok)
    wait("gp", "DW", dW_disp)
    emit_exp_dma(0); emit_gather(0)
    emit_exp_dma(1); emit_gather(1)

    s_si2 = [None]
    def emit_gu_e(e):
        sl = exp_sb[e % 2]
        first = True
        v_hm = None
        for (c0, w) in CH:
            for mh in range(2):
                for woff, bks, tag in ((0, (0, 1), "g"), (KT * FF, (2, 3), "u")):
                    for mi, bki in enumerate(bks):
                        m = mh * 2 + mi
                        o = gb_banks[bki]
                        ow = psum_s[:, bki * 512: bki * 512 + w]
                        if gb_busy[bki] is not None:
                            eng, v = gb_busy[bki]
                            wait("te", eng, v)
                            gb_busy[bki] = None
                        if first:
                            wait("te", "DW", dW_exp[e])
                            wait("te", "DW", dW_gat[e])
                            first = False
                        for kt in range(KT):
                            l = sl[:, woff + kt * FF + m * 128: woff + kt * FF + (m + 1) * 128]
                            r = gat[e % 2][:, kt * CAP + c0: kt * CAP + c0 + w]
                            em("te", lambda E, ow=ow, l=l, r=r, kt=kt: E.matmul(ow, l, r, start=(kt == 0), stop=(kt == KT - 1)))
                    tg = mark("te", "T")
                    if tag == "g":
                        wait("sc", "T", tg)
                        for mi, bki in enumerate(bks):
                            m = mh * 2 + mi
                            em("sc", lambda E, m=m, bki=bki, w=w, c0=c0: E.activation(
                                sg[:, m * CAP + c0: m * CAP + c0 + w], psum_s[:, bki * 512: bki * 512 + w], AF.Silu))
                        s_si2[0] = mark("sc", "S")
                        for bki in bks:
                            gb_busy[bki] = ("S", s_si2[0])
                    else:
                        wait("ve", "T", tg)
                        wait("ve", "S", s_si2[0])
                        if mh == 0 and c0 == 0 and hm_guard[e % 2] is not None:
                            wait("ve", "T", hm_guard[e % 2])
                        for mi, bki in enumerate(bks):
                            m = mh * 2 + mi
                            dst = hm[e % 2][:, m * CAP + c0: m * CAP + c0 + w]
                            em("ve", lambda E, dst=dst, m=m, bki=bki, w=w, c0=c0: E.tensor_mul(
                                dst, sg[:, m * CAP + c0: m * CAP + c0 + w], psum_s[:, bki * 512: bki * 512 + w]))
                        v_hm = mark("ve", "V")
                        for bki in bks:
                            gb_busy[bki] = ("V", v_hm)
        gu_done[e] = cnt["T"]
        return v_hm

    dr = [0]
    v_cw_wait = [False]
    def emit_down_e(e, v_hm):
        wait("te", "V", v_hm)
        for tw in range(6):
            for cc in range(2):
                dr[0] += 1
                bank = d_banks[dr[0] % 4]
                if d_busy[dr[0] % 4] is not None:
                    wait("te", "V", d_busy[dr[0] % 4])
                    d_busy[dr[0] % 4] = None
                for kt in range(4):
                    l = hm[e % 2][:, kt * CAP + tw * 128: kt * CAP + (tw + 1) * 128]
                    r = exp_sb[e % 2][:, 2 * KT * FF + kt * C + cc * 512: 2 * KT * FF + kt * C + (cc + 1) * 512]
                    em("te", lambda E, bank=bank, l=l, r=r, kt=kt: E.matmul(bank[:, :], l, r, start=(kt == 0), stop=(kt == 3)))
                td = mark("te", "T")
                if not v_cw_wait[0]:
                    wait("ve", "DW", dW_disp)
                    wait("ve", "DX", dX_shared)
                    v_cw_wait[0] = True
                if e >= 1 and tw == 0 and cc == 0 and dW_scat[e - 1] is not None:
                    wait("ve", "DW", dW_scat[e - 1])
                wait("ve", "T", td)
                dst = y_sb[:, tw * C + cc * 512: tw * C + (cc + 1) * 512]
                em("ve", lambda E, dst=dst, bank=bank, e=e, tw=tw: E.tensor_single_scalar(
                    dst, bank[:, :], cw_sb[:, e * 6 + tw: e * 6 + tw + 1], ALU.mult))
                d_busy[dr[0] % 4] = mark("ve", "V")
        hm_guard[e % 2] = cnt["T"]
        slot_done[e] = cnt["T"]
        return cnt["V"]

    for e in range(NE):
        v_hm = emit_gu_e(e)
        v_y = emit_down_e(e, v_hm)
        if e + 2 < NE:
            emit_exp_dma(e + 2)
            emit_gather(e + 2)
        if e == 0:
            wait("gp", "DX", dX_shared)
        if e >= 2:
            wait("gp", "DW", dW_scat[e - 2])
        wait("gp", "V", v_y)
        ytgt = ytok_dram if e % 2 == 0 else ytok2_dram
        em("gp", lambda E, e=e, ytgt=ytgt: E.dma_scatter_add(
            ytgt, y_sb[:, :].rearrange("p (a b) -> p a b", b=C),
            idx16[:, e * 48:(e + 1) * 48], CAP, nidx_reg, C))
        cnt["DW"] += 16
        sched["gp"][-1] = (lambda f: lambda E: f(E).then_inc(sems["DW"], 16))(sched["gp"][-1])
        dW_scat[e] = cnt["DW"]
    run_phase()
    if tap == "ytok":
        return _finish_tap_dram(nc, sems, cnt, xo, ytok_dram, 16384), sem_guards
    free("ps_d", "ps_c")
    free("y_sb", "sg", "hm1", "hm0", "gat1", "gat0", "exp1", "exp0")

    phase()
    xacc = alloc("xacc2", [128, KT * T], F32)
    yst = [alloc("yst0", [128, 2 * C], F32), alloc("yst1", [128, 2 * C], F32)]
    dma("sy", xacc[:, :], xp_dram, "DX")
    dX_xr = cnt["DX"]
    wait("sy", "DW", max(dW_scat[NE - 1], dW_scat[NE - 2]))
    dma("sy", yst[0][:, 0:C], bass.AP(ytok_dram.tensor, 0, [[C, 128], [1, C]]), "DX")
    dma("sy", yst[0][:, C:2 * C], bass.AP(ytok2_dram.tensor, 0, [[C, 128], [1, C]]), "DX")
    st_busy3 = [cnt["DX"], None]
    wait("ve", "DX", dX_xr)
    tp2 = [ps_a, ps_b]
    tb2 = [None, None]
    tile_done = [None] * NQT
    ii = 0
    for tt in range(NQT):
        if tt + 1 < NQT:
            if tt >= 1:
                wait("sy", "T", tile_done[tt - 1])
            dma("sy", yst[(tt + 1) % 2][:, 0:C], bass.AP(ytok_dram.tensor, (tt + 1) * 128 * C, [[C, 128], [1, C]]), "DX")
            dma("sy", yst[(tt + 1) % 2][:, C:2 * C], bass.AP(ytok2_dram.tensor, (tt + 1) * 128 * C, [[C, 128], [1, C]]), "DX")
            st_busy3[(tt + 1) % 2] = cnt["DX"]
        wait("ve", "DX", st_busy3[tt % 2])
        em("ve", lambda E, tt=tt: E.tensor_add(yst[tt % 2][:, 0:C], yst[tt % 2][:, 0:C], yst[tt % 2][:, C:2 * C]))
        v_ya = mark("ve", "V")
        wait("te", "V", v_ya)
        for kt in range(KT):
            pb = tp2[ii % 2]
            if tb2[ii % 2] is not None:
                wait("te", "V", tb2[ii % 2])
            tv = inc("te", "T", lambda E, pb=pb, kt=kt, tt=tt: E.transpose(
                pb[:, 0:128], yst[tt % 2][:, kt * 128:(kt + 1) * 128], idn_f[:, :]))
            wait("ve", "T", tv)
            dst = xacc[:, kt * T + tt * 128: kt * T + (tt + 1) * 128]
            tb2[ii % 2] = inc("ve", "V", lambda E, dst=dst, pb=pb: E.tensor_add(dst, dst, pb[:, 0:128]))
            ii += 1
        tile_done[tt] = cnt["T"]
    for mt in range(KT):
        em("ve", lambda E, mt=mt: E.tensor_single_scalar(xacc[:, mt * T:(mt + 1) * T], xacc[:, mt * T:(mt + 1) * T],
                                                         der[:, 40 + mt:41 + mt], ALU.mult))
    v_fin = mark("ve", "V")
    wait("sy", "V", v_fin)
    dma("sy", xo, xacc[:, :], "DO")
    wait("sy", "DO", cnt["DO"])
    run_phase()
    free("yst1", "yst0", "xacc2")
    free("cw_sb", "idx16")
    for g in reversed(sem_guards):
        g.__exit__(None, None, None)
    mybir.codegen_inst_isa_subclasses(nc)
    return nc


def _finish_tap_dram(nc, sems, cnt, xo, src_dram, nrows_elems):
    with nc.Block() as blk:
        @blk.gpsimd
        def _(E):
            E.dma_start(bass.AP(xo.tensor, 0, [[16384, 128], [1, 16384]]),
                        bass.AP(src_dram.tensor, 0, [[16384, 128], [1, 16384]])).then_inc(sems["DO"], 16)
            E.wait_ge(sems["DO"], 16)
    mybir.codegen_inst_isa_subclasses(nc)
    return nc


def _finish_tap(nc, sems, cnt, xo, taps):
    with nc.Block() as blk:
        @blk.gpsimd
        def _(E):
            n = 0
            for ap, off in taps:
                w = ap.shape[-1] if len(ap.shape) else 0
                dst = bass.AP(xo.tensor, off, [[xo.tensor.shape[1], ap.shape[0]], [1, int(np.prod(ap.shape[1:]))]])
                E.dma_start(dst, ap).then_inc(sems["DO"], 16)
                n += 16
            E.wait_ge(sems["DO"], n)
    mybir.codegen_inst_isa_subclasses(nc)
    return nc



def _swz(w):
    K, M = w.shape
    return np.ascontiguousarray(w.reshape(K // 128, 128, M).transpose(1, 0, 2).reshape(128, -1))


_CACHE = {}


def _staging(inp):
    f32 = np.float32
    perm = np.empty(1024, np.int64)
    for m in range(1024):
        blk = m // 512; h = (m % 512) // 32; j = m % 32
        perm[m] = 64 * h + 2 * j + blk
    st = {}
    st["wqs"] = _swz(np.ascontiguousarray(inp["L0_attn_Wq"].T[:, perm]))
    st["wks"] = _swz(np.ascontiguousarray(inp["L0_attn_Wk"].T[:, perm]))
    st["wvs"] = _swz(np.ascontiguousarray(inp["L0_attn_Wv"].T))
    st["wos"] = _swz(np.ascontiguousarray(inp["L0_attn_Wo"].T))
    st["bqs"] = np.ascontiguousarray(inp["L0_attn_bq"][perm].reshape(8, 128).T)
    st["bks"] = np.ascontiguousarray(inp["L0_attn_bk"][perm].reshape(8, 128).T)
    st["bvs"] = np.ascontiguousarray(inp["L0_attn_bv"].reshape(8, 128).T)
    st["bos"] = np.ascontiguousarray(inp["L0_attn_bo"].reshape(8, 128).T)
    st["wmods"] = _swz(inp["L0_wmod"])
    st["gws"] = _swz(np.ascontiguousarray(inp["L0_ffn_gate_w"].T))
    st["wsg"] = _swz(inp["L0_ffn_Ws_gate"])
    st["wsu"] = _swz(inp["L0_ffn_Ws_up"])
    st["wsd"] = _swz(inp["L0_ffn_Ws_down"])
    ex = np.empty((NE, 128, EPB), f32)
    for e in range(NE):
        ex[e, :, 0:4096] = _swz(inp["L0_ffn_We_gate"][e])
        ex[e, :, 4096:8192] = _swz(inp["L0_ffn_We_up"][e])
        ex[e, :, 8192:12288] = _swz(inp["L0_ffn_We_down"][e])
    st["experts"] = ex
    j = np.arange(32, dtype=np.float64)
    theta = 10000.0 ** (-2.0 * j / 64.0)
    t = np.arange(T, dtype=np.float64)
    fr = t[None, :] * theta[:, None]
    cos32 = np.cos(fr); sin32 = np.sin(fr)
    st["cosr"] = np.tile(cos32, (4, 1)).astype(ml_dtypes.bfloat16)
    st["sinr"] = np.tile(sin32, (4, 1)).astype(ml_dtypes.bfloat16)
    st["idnb"] = np.eye(128, dtype=ml_dtypes.bfloat16)
    st["idnf"] = np.eye(128, dtype=f32)
    q = np.arange(128)[:, None]; kk = np.arange(128)[None, :]
    mt = np.zeros((128, 1024), f32)
    mt[:, 0:512] = -1e30
    mt[:, 512:640] = np.where(q <= kk, 0.0, -1e30)
    st["masktd"] = mt
    st["brepd"] = np.broadcast_to(inp["L0_ffn_gate_bias"], (128, 64)).astype(f32).copy()
    st["tok16d"] = (np.arange(16)[None, :] * 128 + np.arange(128)[:, None]).astype(np.int16)
    st["ones128d"] = np.ones((128, 128), ml_dtypes.bfloat16)
    st["trisd"] = (q < kk).astype(ml_dtypes.bfloat16)
    e1 = np.arange(64)[:, None]; e2 = np.arange(64)[None, :]
    st["tri64d"] = (e1 < e2).astype(ml_dtypes.bfloat16)
    st["e48d"] = np.broadcast_to(np.arange(64, dtype=f32) * 48.0, (128, 64)).copy()
    st["e6d"] = np.broadcast_to(np.arange(64, dtype=f32) * 6.0, (128, 64)).copy()
    return st


def kernel(**inp):
    if "nc" not in _CACHE or _CACHE.get("tap") != TAP:
        _CACHE["nc"] = _build(TAP)
        if isinstance(_CACHE["nc"], tuple):
            _CACHE["nc"] = _CACHE["nc"][0]
        _CACHE["tap"] = TAP
    nc = _CACHE["nc"]
    st = _staging(inp)
    x = inp["L0_attn_x"]
    cva = inp["L0_c"]
    in_maps = []
    for b in range(8):
        m = dict(st)
        m["xt"] = _swz(np.ascontiguousarray(x[b].T))
        m["cv"] = np.ascontiguousarray(cva[b].reshape(8, 128).T)
        in_maps.append(m)
    res = run_bass_kernel_spmd(nc, in_maps, core_ids=list(range(8)))
    outs = []
    for b in range(8):
        o = res.results[b]["xo"]
        o = o.reshape(128, KT, T).transpose(1, 0, 2).reshape(C, T).T
        outs.append(o)
    return np.ascontiguousarray(np.stack(outs)).astype(np.float32)


# version 06
# speedup vs baseline: 1.2319x; 1.2319x over previous
import numpy as np
import ml_dtypes
import concourse.bass as bass
import concourse.mybir as mybir
from concourse.bass_utils import run_bass_kernel_spmd
from concourse.bass_core_rust import IndirectOffsetOnAxis
from concourse.library_config import mlp as MLP_LIB

F32 = mybir.dt.float32
BF16 = mybir.dt.bfloat16
I16 = mybir.dt.int16
I32 = mybir.dt.int32
AF = mybir.ActivationFunctionType
ALU = mybir.AluOpType
AX = mybir.AxisListType

T = 2048
C = 1024
KT = 8
NH = 16
HD = 64
NE = 64
FF = 512
NCH = 4
NQT = 16
EPB = 12288
CAP = 768

TAP = None

ENGINES = ("sy", "gp", "te", "ve", "sc")


def _build(tap=None):
    nc = bass.Bass("TRN2", target_bir_lowering=False)

    di = {}
    def dram_in(name, shape, dt=F32):
        di[name] = nc.dram_tensor(name, shape, dt, kind="ExternalInput").ap()
        return di[name]

    xt = dram_in("xt", [128, KT * T])
    wqs = dram_in("wqs", [128, KT * C]); wks = dram_in("wks", [128, KT * C])
    wvs = dram_in("wvs", [128, KT * C]); wos = dram_in("wos", [128, KT * C])
    bqs = dram_in("bqs", [128, 8]); bks = dram_in("bks", [128, 8])
    bvs = dram_in("bvs", [128, 8]); bos = dram_in("bos", [128, 8])
    wmods = dram_in("wmods", [128, KT * 6144])
    cv = dram_in("cv", [128, 8])
    gws = dram_in("gws", [128, KT * 64])
    wsg = dram_in("wsg", [128, KT * FF]); wsu = dram_in("wsu", [128, KT * FF])
    wsd = dram_in("wsd", [128, 4 * C])
    experts = dram_in("experts", [NE, 128, EPB])
    cosr = dram_in("cosr", [128, T], BF16); sinr = dram_in("sinr", [128, T], BF16)
    idnb_d = dram_in("idnb", [128, 128], BF16)
    idnf_d = dram_in("idnf", [128, 128], F32)
    masktd = dram_in("masktd", [128, 1024], F32)
    tok16d = dram_in("tok16d", [128, 16], I16)
    ones128d = dram_in("ones128d", [128, 128], BF16)
    trisd = dram_in("trisd", [128, 128], BF16)
    tri64d = dram_in("tri64d", [64, 64], BF16)
    e48d = dram_in("e48d", [128, 64], F32)
    e6d = dram_in("e6d", [128, 64], F32)
    brepd = dram_in("brepd", [128, 64], F32)

    xo = nc.dram_tensor("xo", [128, KT * T], F32, kind="ExternalOutput").ap()
    xp_dram = nc.dram_tensor("xp_scratch", [128, KT * T], F32).ap()
    h2tok_dram = nc.dram_tensor("h2tok_scratch", [T + 1, C], BF16).ap()
    lists_dram = nc.dram_tensor("lists_scratch", [16 * 3072 + 64], I16).ap()
    cwrep_dram = nc.dram_tensor("cwrep_scratch", [64 * CAP + 16], F32).ap()
    ytok_dram = nc.dram_tensor("ytok_scratch", [T + 1, C], BF16).ap()
    ytok2_dram = nc.dram_tensor("ytok2_scratch", [T + 1, C], BF16).ap()

    sem_guards = []
    sems = {}
    cnt = {}
    for k in ("DW", "DX", "DO", "T", "V", "S"):
        g = nc.semaphore("sem_" + k)
        sems[k] = g.__enter__(); sem_guards.append(g)
        cnt[k] = 0

    live = {}
    def alloc(name, shape, dt, side="right"):
        g = nc.sbuf_tensor(name, shape, dt, side=side)
        t = g.__enter__(); live[name] = g
        return t
    def palloc(name, shape, dt):
        g = nc.psum_tensor(name, shape, dt)
        t = g.__enter__(); live[name] = g
        return t
    def free(*names):
        for n in names:
            live.pop(n).__exit__(None, None, None)

    sched = None
    def em(e, fn):
        sched[e].append(fn)
    def dma(e, dst, src, key):
        cnt[key] += 16
        v = cnt[key]
        em(e, lambda E: E.dma_start(dst, src).then_inc(sems[key], 16))
        return v
    def wait(e, key, v):
        em(e, lambda E: E.wait_ge(sems[key], v))
    def inc(e, key, fn):
        cnt[key] += 1
        em(e, lambda E: fn(E).then_inc(sems[key], 1))
        return cnt[key]
    def vsep(e, key):
        v = mark(e, key)
        wait(e, key, v)
    def mark(e, key):
        cnt[key] += 1
        f = sched[e].pop()
        em(e, lambda E, f=f: f(E).then_inc(sems[key], 1))
        return cnt[key]
    def run_phase():
        nonlocal sched
        s = sched
        sched = None
        with nc.Block() as blk:
            @blk.sync
            def _(E):
                for f in s["sy"]: f(E)
            @blk.gpsimd
            def _(E):
                for f in s["gp"]: f(E)
            @blk.tensor
            def _(E):
                for f in s["te"]: f(E)
            @blk.vector
            def _(E):
                for f in s["ve"]: f(E)
            @blk.scalar
            def _(E):
                for f in s["sc"]: f(E)
    def phase():
        nonlocal sched
        sched = {e: [] for e in ENGINES}

    mod_sb = alloc("mod_sb", [128, 48], F32, side="left")
    der = alloc("der", [128, 64], F32, side="left")
    rscr = alloc("rscr", [128, 16], F32, side="left")
    bq_sb = alloc("bq_sb", [128, 8], F32, side="left"); bk_sb = alloc("bk_sb", [128, 8], F32, side="left")
    bv_sb = alloc("bv_sb", [128, 8], F32, side="left"); bo_sb = alloc("bo_sb", [128, 8], F32, side="left")
    bqs_s = alloc("bqs_s", [128, 8], F32, side="left"); bks_s = alloc("bks_s", [128, 8], F32, side="left")
    cv_sb = alloc("cv_sb", [128, 8], F32, side="left"); cvs = alloc("cvs", [128, 8], BF16, side="left")
    idn_b = alloc("idn_b", [128, 128], BF16, side="left")
    idn_f = alloc("idn_f", [128, 128], F32, side="left")
    maskT_sb = alloc("maskT_sb", [128, 1024], F32, side="left")
    tok16 = alloc("tok16", [128, 16], I16, side="left")
    ones128b = alloc("ones128b", [128, 128], BF16, side="left")
    triSb = alloc("triSb", [128, 128], BF16, side="left")
    tri64b = alloc("tri64b", [64, 64], BF16, side="left")
    e48c = alloc("e48c", [128, 64], F32, side="left")
    e6c = alloc("e6c", [128, 64], F32, side="left")
    brep = alloc("brep", [128, 64], F32, side="left")
    gw_f = alloc("gw_f", [128, KT * 64], F32, side="left")
    ones_c = alloc("ones_c", [128, 1], BF16, side="left")
    ones_r = alloc("ones_r", [1, 128], F32, side="left")

    h_bf = alloc("h_bf", [128, KT * T], BF16, side="left")

    psum_s = palloc("psum_s", [128, 2048], F32)
    ps_a = palloc("ps_a", [128, 512], F32)
    ps_b = palloc("ps_b", [128, 512], F32)

    phase()
    wmod_buf = alloc("wmod_buf", [128, KT * 6144], BF16)

    dma("sy", cv_sb[:, :], cv, "DX")
    dma("sy", idn_b[:, :], idnb_d, "DX")
    dma("sy", idn_f[:, :], idnf_d, "DX")
    dma("sy", maskT_sb[:, :], masktd, "DX")
    dma("sy", tok16[:, :], tok16d, "DX")
    dma("sy", ones128b[:, :], ones128d, "DX")
    dma("sy", triSb[:, :], trisd, "DX")
    dma("sy", tri64b[:, :], tri64d, "DX")
    dma("sy", e48c[:, :], e48d, "DX")
    dma("sy", e6c[:, :], e6d, "DX")
    dma("sy", brep[:, :], brepd, "DX")
    dma("sy", gw_f[:, :], gws, "DX")
    dma("sy", bq_sb[:, :], bqs, "DX")
    dma("sy", bk_sb[:, :], bks, "DX")
    dma("sy", bv_sb[:, :], bvs, "DX")
    dma("sy", bo_sb[:, :], bos, "DX")
    dX_consts = cnt["DX"]
    em("gp", lambda E: E.load_library(MLP_LIB))
    dma("gp", wmod_buf[:, :], wmods, "DW")
    dW_wmod = cnt["DW"]

    em("ve", lambda E: E.memset(der[:, :], 0.0))
    em("ve", lambda E: E.memset(ones_c[:, :], 1.0))
    em("ve", lambda E: E.memset(ones_r[:, :], 1.0))
    vs_ones = mark("ve", "V")

    wait("sc", "DX", dX_consts)
    ss_cvs = inc("sc", "S", lambda E: E.activation(cvs[:, :], cv_sb[:, :], AF.Silu))

    wait("te", "DW", dW_wmod)
    wait("te", "S", ss_cvs)
    for mt in range(48):
        for kt in range(KT):
            o, l, r = psum_s[:, mt:mt + 1], wmod_buf[:, kt * 6144 + mt * 128: kt * 6144 + mt * 128 + 128], cvs[:, kt:kt + 1]
            if kt < KT - 1:
                em("te", lambda E, o=o, l=l, r=r, kt=kt: E.matmul(o, l, r, start=(kt == 0), stop=False))
            else:
                em("te", lambda E, o=o, l=l, r=r: E.matmul(o, l, r, start=False, stop=True))
    ts_mod = mark("te", "T")

    wait("ve", "T", ts_mod)
    em("ve", lambda E: E.tensor_copy(mod_sb[:, :], psum_s[:, 0:48]))
    vsep("ve", "V")
    em("ve", lambda E: E.tensor_single_scalar(der[:, 8:16], mod_sb[:, 8:16], 1.0, ALU.add))
    em("ve", lambda E: E.tensor_single_scalar(der[:, 32:40], mod_sb[:, 32:40], 1.0, ALU.add))
    vs_mod1 = mark("ve", "V")
    wait("sc", "V", vs_mod1)
    em("sc", lambda E: E.sign(rscr[:, 0:8], mod_sb[:, 16:24]))
    vsep("sc", "S")
    em("sc", lambda E: E.activation(rscr[:, 8:16], mod_sb[:, 16:24], AF.Abs))
    ss_ga = mark("sc", "S")
    wait("ve", "S", ss_ga)
    em("ve", lambda E: E.tensor_scalar_max(rscr[:, 8:16], rscr[:, 8:16], 0.25))
    vsep("ve", "V")
    em("ve", lambda E: E.tensor_mul(der[:, 16:24], rscr[:, 0:8], rscr[:, 8:16]))
    vs_ga = mark("ve", "V")
    wait("ve", "V", vs_ga)
    wait("sc", "V", vs_ga)
    em("sc", lambda E: E.sign(rscr[:, 0:8], mod_sb[:, 40:48]))
    vsep("sc", "S")
    em("sc", lambda E: E.activation(rscr[:, 8:16], mod_sb[:, 40:48], AF.Abs))
    ss_gb = mark("sc", "S")
    wait("ve", "S", ss_gb)
    em("ve", lambda E: E.tensor_scalar_max(rscr[:, 8:16], rscr[:, 8:16], 0.25))
    vsep("ve", "V")
    em("ve", lambda E: E.tensor_mul(der[:, 40:48], rscr[:, 0:8], rscr[:, 8:16]))
    vsep("ve", "V")
    em("ve", lambda E: E.tensor_mul(der[:, 48:56], der[:, 16:24], bo_sb[:, :]))
    vsep("ve", "V")
    em("ve", lambda E: E.reciprocal(der[:, 56:64], der[:, 40:48]))
    em("ve", lambda E: E.tensor_single_scalar(bqs_s[:, :], bq_sb[:, :], 0.125, ALU.mult))
    em("ve", lambda E: E.tensor_single_scalar(bks_s[:, :], bk_sb[:, :], 0.125, ALU.mult))
    vs_modp = mark("ve", "V")
    wait("ve", "V", vs_modp)
    run_phase()
    free("wmod_buf")

    if tap == "mod":
        return _finish_tap(nc, sems, cnt, xo, [(mod_sb[:, 0:48], 0), (der[:, 8:64], 64)]), sem_guards

    x_sb = alloc("x_sb", [128, KT * T], F32)

    phase()
    dma("sy", x_sb[:, :], xt, "DX")
    dX_x = cnt["DX"]

    def emit_ln(src, one_gain_col, shift_col):
        rowbuf = live_get("rowbuf")
        a_rep = live_get("a_rep"); b_rep = live_get("b_rep")
        ra = rowbuf[0:1, 0:T]; rb = rowbuf[0:1, T:2 * T]; rc = rowbuf[0:1, 2 * T:3 * T]
        for mt in range(KT):
            em("sc", lambda E, mt=mt: E.activation(h_bf[:, mt * T:(mt + 1) * T], src[:, mt * T:(mt + 1) * T], AF.Square))
        s_sq = mark("sc", "S")
        wait("te", "S", s_sq)
        for c in range(NCH):
            for kt in range(KT):
                o = psum_s[0:1, c * 512:(c + 1) * 512]
                r = h_bf[:, kt * T + c * 512: kt * T + (c + 1) * 512]
                em("te", lambda E, o=o, r=r, kt=kt: E.matmul(o, ones_c[:, :], r, start=(kt == 0), stop=(kt == KT - 1)))
        t_sq = mark("te", "T")
        wait("ve", "T", t_sq)
        em("ve", lambda E: E.tensor_single_scalar(rb, psum_s[0:1, 0:T], 1.0 / C, ALU.mult))
        v_sq = mark("ve", "V")
        wait("sc", "T", t_sq)
        for mt in range(KT):
            em("sc", lambda E, mt=mt: E.copy(h_bf[:, mt * T:(mt + 1) * T], src[:, mt * T:(mt + 1) * T]))
        s_xbf = mark("sc", "S")
        wait("te", "S", s_xbf)
        wait("te", "V", v_sq)
        for c in range(NCH):
            for kt in range(KT):
                o = psum_s[0:1, c * 512:(c + 1) * 512]
                r = h_bf[:, kt * T + c * 512: kt * T + (c + 1) * 512]
                em("te", lambda E, o=o, r=r, kt=kt: E.matmul(o, ones_c[:, :], r, start=(kt == 0), stop=(kt == KT - 1)))
        t_sm = mark("te", "T")
        wait("ve", "T", t_sm)
        em("ve", lambda E: E.tensor_single_scalar(ra, psum_s[0:1, 0:T], 1.0 / C, ALU.mult))
        v_sm = mark("ve", "V")
        wait("sc", "V", v_sm)
        s_sqr = inc("sc", "S", lambda E: E.activation(psum_s[0:1, 0:T], ra, AF.Square))
        wait("ve", "S", s_sqr)
        em("ve", lambda E: E.tensor_sub(rb, rb, psum_s[0:1, 0:T]))
        em("ve", lambda E: E.tensor_single_scalar(rb, rb, 1e-5, ALU.add))
        em("ve", lambda E: E.reciprocal(rc, rb))
        v_var = mark("ve", "V")
        wait("sc", "V", v_var)
        s_rstd = inc("sc", "S", lambda E: E.activation(rb, rc, AF.Sqrt))
        wait("ve", "S", s_rstd)
        em("ve", lambda E: E.tensor_mul(ra, ra, rb))
        v_rows = mark("ve", "V")
        wait("te", "V", v_rows)
        for c in range(NCH):
            em("te", lambda E, c=c: E.matmul(psum_s[:, c * 512:(c + 1) * 512], ones_r[:, :], rb[:, c * 512:(c + 1) * 512], start=True, stop=True))
        t_ar = mark("te", "T")
        wait("ve", "T", t_ar)
        em("ve", lambda E: E.tensor_copy(a_rep[:, :], psum_s[:, 0:T]))
        v_ar = mark("ve", "V")
        wait("te", "V", v_ar)
        for c in range(NCH):
            em("te", lambda E, c=c: E.matmul(psum_s[:, c * 512:(c + 1) * 512], ones_r[:, :], ra[:, c * 512:(c + 1) * 512], start=True, stop=True))
        t_br = mark("te", "T")
        wait("ve", "T", t_br)
        em("ve", lambda E: E.tensor_copy(b_rep[:, :], psum_s[:, 0:T]))
        mark("ve", "V")
        return a_rep, b_rep

    def live_get(name):
        return live[name].tensor.ap() if hasattr(live[name], "tensor") else None

    rowbuf = alloc("rowbuf", [1, 3 * T], F32)
    a_rep = alloc("a_rep", [128, T], F32)
    b_rep = alloc("b_rep", [128, T], F32)
    tmpT = alloc("tmpT", [128, T], F32)

    def live_get(name, _m={"rowbuf": rowbuf, "a_rep": a_rep, "b_rep": b_rep}):
        return _m[name]

    wait("sc", "DX", dX_x)
    emit_ln(x_sb, None, None)
    for mt in range(KT):
        xs = x_sb[:, mt * T:(mt + 1) * T]
        em("ve", lambda E, xs=xs: E.tensor_mul(tmpT[:, :], xs, a_rep[:, :]))
        em("ve", lambda E: E.tensor_sub(tmpT[:, :], tmpT[:, :], b_rep[:, :]))
        em("ve", lambda E, mt=mt: E.tensor_scalar(h_bf[:, mt * T:(mt + 1) * T], tmpT[:, :],
                                                  der[:, 8 + mt:9 + mt], mod_sb[:, 0 + mt:1 + mt],
                                                  op0=ALU.mult, op1=ALU.add))
        em("ve", lambda E, xs=xs, mt=mt: E.tensor_single_scalar(xs, xs, der[:, 48 + mt:49 + mt], ALU.add))
    vs_ln1 = mark("ve", "V")
    wait("sy", "V", vs_ln1)
    dma("sy", xp_dram, x_sb[:, :], "DX")
    dX_spill = cnt["DX"]
    wait("sy", "DX", dX_spill)
    run_phase()
    free("tmpT", "b_rep", "a_rep", "rowbuf", "x_sb")

    if tap == "h1":
        return _finish_tap(nc, sems, cnt, xo, [(h_bf[:, :], 0)]), sem_guards

    q_rot = alloc("q_rot", [128, KT * T], BF16, side="left")
    k_rot = alloc("k_rot", [128, KT * T], BF16, side="left")

    phase()
    wq_sb = alloc("wq_sb", [128, KT * C], BF16)
    wk_sb = alloc("wk_sb", [128, KT * C], BF16)
    cos_sb = alloc("cos_sb", [128, T], BF16)
    sin_sb = alloc("sin_sb", [128, T], BF16)
    qk_raw = alloc("qk_raw", [128, 8 * 512], F32)
    tmpr = alloc("tmpr", [128, 2 * 512], F32)

    dma("sy", cos_sb[:, :], cosr, "DX")
    dma("sy", sin_sb[:, :], sinr, "DX")
    dX_rope = cnt["DX"]
    wait("gp", "T", ts_mod)
    dma("gp", wq_sb[:, :], wqs, "DW")
    dW_q = cnt["DW"]
    dma("gp", wk_sb[:, :], wks, "DW")
    dW_k = cnt["DW"]

    proj_bank_busy = {}
    def emit_proj(w_sb, rot_dst, bias_scaled, scale):
        bank_busy = proj_bank_busy
        for n in range(NCH):
            for m in range(KT):
                bk = m % 4
                o = psum_s[:, bk * 512:(bk + 1) * 512]
                if bank_busy.get(bk) is not None:
                    wait("te", "V", bank_busy[bk])
                for kt in range(KT):
                    l = w_sb[:, kt * C + m * 128: kt * C + (m + 1) * 128]
                    r = h_bf[:, kt * T + n * 512: kt * T + (n + 1) * 512]
                    em("te", lambda E, o=o, l=l, r=r, kt=kt: E.matmul(o, l, r, start=(kt == 0), stop=(kt == KT - 1)))
                tv = mark("te", "T")
                wait("ve", "T", tv)
                em("ve", lambda E, o=o, m=m: E.tensor_scalar(qk_raw[:, m * 512:(m + 1) * 512], o,
                                                             scale, bias_scaled[:, m:m + 1],
                                                             op0=ALU.mult, op1=ALU.add))
                bank_busy[bk] = mark("ve", "V")
            cc = cos_sb[:, n * 512:(n + 1) * 512]
            ss = sin_sb[:, n * 512:(n + 1) * 512]
            for at in range(4):
                A = qk_raw[:, at * 512:(at + 1) * 512]
                Bv = qk_raw[:, (4 + at) * 512:(5 + at) * 512]
                dA = rot_dst[:, at * T + n * 512: at * T + (n + 1) * 512]
                dB = rot_dst[:, (4 + at) * T + n * 512: (4 + at) * T + (n + 1) * 512]
                em("ve", lambda E, A=A, cc=cc: E.tensor_mul(tmpr[:, 0:512], A, cc))
                em("ve", lambda E, Bv=Bv, ss=ss: E.tensor_mul(tmpr[:, 512:1024], Bv, ss))
                em("ve", lambda E, dA=dA: E.tensor_sub(dA, tmpr[:, 0:512], tmpr[:, 512:1024]))
                em("ve", lambda E, A=A, ss=ss: E.tensor_mul(tmpr[:, 0:512], A, ss))
                em("ve", lambda E, Bv=Bv, cc=cc: E.tensor_mul(tmpr[:, 512:1024], Bv, cc))
                em("ve", lambda E, dB=dB: E.tensor_add(dB, tmpr[:, 0:512], tmpr[:, 512:1024]))
            mark("ve", "V")

    wait("te", "DW", dW_q)
    wait("ve", "DX", dX_rope)
    wait("ve", "V", vs_modp)
    emit_proj(wq_sb, q_rot, bqs_s, 0.125)
    wait("te", "DW", dW_k)
    emit_proj(wk_sb, k_rot, bk_sb, 1.0)
    run_phase()
    free("tmpr", "qk_raw", "sin_sb", "cos_sb", "wk_sb", "wq_sb")

    if tap == "qrot":
        return _finish_tap(nc, sems, cnt, xo, [(q_rot[:, :], 0)]), sem_guards
    if tap == "krot":
        return _finish_tap(nc, sems, cnt, xo, [(k_rot[:, :], 0)]), sem_guards

    phase()
    wv_sb = alloc("wv_sb", [128, KT * C], BF16)
    vt_sb = alloc("vt_sb", [128, KT * T], BF16)
    dma("gp", wv_sb[:, :], wvs, "DW")
    dW_v = cnt["DW"]
    wait("te", "DW", dW_v)
    bank_busy = proj_bank_busy
    for n in range(NCH):
        for m in range(KT):
            bk = m % 4
            o = psum_s[:, bk * 512:(bk + 1) * 512]
            if bank_busy.get(bk) is not None:
                wait("te", "V", bank_busy[bk])
            for kt in range(KT):
                l = wv_sb[:, kt * C + m * 128: kt * C + (m + 1) * 128]
                r = h_bf[:, kt * T + n * 512: kt * T + (n + 1) * 512]
                em("te", lambda E, o=o, l=l, r=r, kt=kt: E.matmul(o, l, r, start=(kt == 0), stop=(kt == KT - 1)))
            tv = mark("te", "T")
            wait("ve", "T", tv)
            em("ve", lambda E, o=o, m=m, n=n: E.tensor_single_scalar(vt_sb[:, m * T + n * 512: m * T + (n + 1) * 512],
                                                                     o, bv_sb[:, m:m + 1], ALU.add))
            bank_busy[bk] = mark("ve", "V")
    run_phase()

    phase()
    v_tok = alloc("v_tok", [128, NQT * NH * HD], BF16, side="left")
    ps_tr0 = palloc("ps_tr0", [128, 1024], BF16)
    ps_tr1 = palloc("ps_tr1", [128, 1024], BF16)
    trb = [ps_tr0, ps_tr1]
    tr_busy = [None, None]
    i = 0
    for h in range(NH):
        for lt in range(NQT):
            src = vt_sb[(h % 2) * 64:(h % 2) * 64 + 64, (h // 2) * T + lt * 128: (h // 2) * T + lt * 128 + 128]
            idn64 = idn_b[0:64, 0:64] if h % 2 == 0 else idn_b[64:128, 64:128]
            pb = trb[i % 2]
            if tr_busy[i % 2] is not None:
                wait("te", "V", tr_busy[i % 2])
            tv = inc("te", "T", lambda E, pb=pb, src=src, idn64=idn64: E.transpose(pb[:, 0:64], src, idn64))
            wait("ve", "T", tv)
            dst = v_tok[:, (lt * NH + h) * HD:(lt * NH + h) * HD + HD]
            tr_busy[i % 2] = inc("ve", "V", lambda E, dst=dst, pb=pb: E.tensor_copy(dst, pb[:, 0:64]))
            i += 1
    run_phase()
    free("vt_sb", "wv_sb")

    free("ps_tr1", "ps_tr0")
    phase()
    ps_dn = palloc("ps_dn", [128, 512], F32)
    attn_o = alloc("attn_o", [128, KT * T], BF16)
    pT2 = [alloc("pT0", [128, NQT * 512], BF16), alloc("pT1", [128, NQT * 512], BF16)]
    drow = alloc("drow", [1, 512], F32)
    rep_sb = alloc("rep_sb", [128, 512], F32)
    em("ve", lambda E: E.memset(rscr[:, 6:7], -30.0))
    v_c30 = mark("ve", "V")
    wait("sc", "V", v_c30)

    pv_ps = [ps_a, ps_b]
    pv_busy = [None, None]
    half_free = [None, None]
    pT_free = [None, None]
    dn_free = None
    drow_free = None

    qc_list = [0] if tap == "att0" else range(NCH)
    h_list = [0] if tap == "att0" else range(NH)
    hidx = 0
    for qc in qc_list:
        for h in h_list:
            r0 = 32 * (h % 4)
            a_ti, b_ti = h // 4, 4 + h // 4
            par = hidx % 2
            pT = pT2[par]
            nlt = 4 * qc + 4
            first_exp = None
            for p in range(nlt // 2):
                hf = p % 2
                for li in range(2):
                    lt = 2 * p + li
                    o = psum_s[:, hf * 1024 + li * 512: hf * 1024 + (li + 1) * 512]
                    if li == 0 and half_free[hf] is not None:
                        wait("te", "S", half_free[hf])
                    lA = k_rot[r0:r0 + 32, a_ti * T + lt * 128: a_ti * T + (lt + 1) * 128]
                    rA = q_rot[r0:r0 + 32, a_ti * T + qc * 512: a_ti * T + (qc + 1) * 512]
                    lB = k_rot[r0:r0 + 32, b_ti * T + lt * 128: b_ti * T + (lt + 1) * 128]
                    rB = q_rot[r0:r0 + 32, b_ti * T + qc * 512: b_ti * T + (qc + 1) * 512]
                    em("te", lambda E, o=o, lA=lA, rA=rA, r0=r0: E.matmul(o, lA, rA, start=True, stop=False, tile_position=(r0, 0)))
                    em("te", lambda E, o=o, lB=lB, rB=rB, r0=r0: E.matmul(o, lB, rB, start=False, stop=True, tile_position=(r0, 0)))
                t_s = mark("te", "T")
                need_mask = (2 * p + 1) >= 4 * qc
                v_mask = None
                if need_mask:
                    wait("ve", "T", t_s)
                    for li in range(2):
                        lt = 2 * p + li
                        if lt >= 4 * qc:
                            j = lt - 4 * qc
                            mv = maskT_sb[:, 512 - 128 * j: 1024 - 128 * j]
                            o = psum_s[:, hf * 1024 + li * 512: hf * 1024 + (li + 1) * 512]
                            em("ve", lambda E, o=o, mv=mv: E.tensor_add(o, o, mv))
                    v_mask = mark("ve", "V")
                if v_mask is not None:
                    wait("sc", "V", v_mask)
                else:
                    wait("sc", "T", t_s)
                if p == 0 and pT_free[par] is not None:
                    wait("sc", "T", pT_free[par])
                dst = pT[:, 2 * p * 512: (2 * p + 2) * 512]
                src = psum_s[:, hf * 1024: hf * 1024 + 1024]
                half_free[hf] = inc("sc", "S", lambda E, dst=dst, src=src:
                                    E.activation(dst, src, AF.Exp, bias=rscr[:, 6:7]))
                if first_exp is None:
                    first_exp = half_free[hf]
            s_done = half_free[(nlt // 2 - 1) % 2]
            if dn_free is not None:
                wait("te", "V", dn_free)
            wait("te", "S", s_done)
            for lt in range(nlt):
                o = ps_dn[0:1, 0:512]
                r = pT[:, lt * 512:(lt + 1) * 512]
                em("te", lambda E, o=o, r=r, lt=lt, nlt=nlt: E.matmul(o, ones_c[:, :], r, start=(lt == 0), stop=(lt == nlt - 1)))
            t_dn = mark("te", "T")
            pp = pv_ps[h % 2]
            row0 = 64 * (h % 2)
            if pv_busy[h % 2] is not None:
                wait("te", "V", pv_busy[h % 2])
            for lt in range(nlt):
                l = v_tok[:, (lt * NH + h) * HD:(lt * NH + h) * HD + HD]
                r = pT[:, lt * 512:(lt + 1) * 512]
                o = pp[row0:row0 + 64, 0:512]
                em("te", lambda E, o=o, l=l, r=r, lt=lt, row0=row0, nlt=nlt:
                   E.matmul(o, l, r, start=(lt == 0), stop=(lt == nlt - 1), tile_position=(0, row0)))
            t_pv = mark("te", "T")
            pT_free[par] = t_pv
            wait("ve", "T", t_dn)
            if drow_free is not None:
                wait("ve", "T", drow_free)
            v_rc = inc("ve", "V", lambda E: E.reciprocal(drow[0:1, :], ps_dn[0:1, 0:512]))
            wait("te", "V", v_rc)
            t_rep = inc("te", "T", lambda E: E.matmul(ps_dn[:, 0:512], ones_r[:, :], drow[0:1, :], start=True, stop=True))
            drow_free = t_rep
            wait("ve", "T", t_rep)
            v_rep = inc("ve", "V", lambda E: E.tensor_copy(rep_sb[:, :], ps_dn[:, 0:512]))
            dn_free = v_rep
            wait("ve", "T", t_pv)
            dst = attn_o[row0:row0 + 64, (h // 2) * T + qc * 512: (h // 2) * T + (qc + 1) * 512]
            pv_busy[h % 2] = inc("ve", "V", lambda E, dst=dst, pp=pp, row0=row0:
                                 E.tensor_mul(dst, pp[row0:row0 + 64, 0:512], rep_sb[row0:row0 + 64, :]))
            hidx += 1
    run_phase()
    if tap == "att0":
        return _finish_tap(nc, sems, cnt, xo, [(pT2[0][:, :], 0), (attn_o[:, 0:512], 8192),
                                               (rep_sb[:, :], 8192 + 512)]), sem_guards
    free("rep_sb", "drow", "pT1", "pT0", "v_tok", "k_rot", "q_rot", "ps_dn")

    if tap == "attn":
        return _finish_tap(nc, sems, cnt, xo, [(attn_o[:, :], 0)]), sem_guards

    xacc = alloc("xacc", [128, KT * T], F32, side="left")
    phase()
    wo_sb = alloc("wo_sb", [128, KT * C], BF16)
    dma("gp", wo_sb[:, :], wos, "DW")
    dW_o = cnt["DW"]
    dma("sy", xacc[:, :], xp_dram, "DX")
    dX_xp = cnt["DX"]
    wait("sy", "DX", dX_xp)
    if tap == "xp":
        run_phase()
        return _finish_tap(nc, sems, cnt, xo, [(xacc[:, :], 0)]), sem_guards
    wait("te", "DW", dW_o)
    wait("ve", "DX", dX_xp)
    bank_busy = {}
    for n in range(NCH):
        for m in range(KT):
            bk = m % 4
            o = psum_s[:, bk * 512:(bk + 1) * 512]
            if bank_busy.get(bk) is not None:
                wait("te", "V", bank_busy[bk])
            for kt in range(KT):
                l = wo_sb[:, kt * C + m * 128: kt * C + (m + 1) * 128]
                r = attn_o[:, kt * T + n * 512: kt * T + (n + 1) * 512]
                em("te", lambda E, o=o, l=l, r=r, kt=kt: E.matmul(o, l, r, start=(kt == 0), stop=(kt == KT - 1)))
            tv = mark("te", "T")
            wait("ve", "T", tv)
            dst = xacc[:, m * T + n * 512: m * T + (n + 1) * 512]
            em("ve", lambda E, dst=dst, o=o, m=m: E.scalar_tensor_tensor(dst, o, der[:, 16 + m:17 + m], dst,
                                                                         op0=ALU.mult, op1=ALU.add))
            bank_busy[bk] = mark("ve", "V")
    run_phase()
    free("wo_sb", "attn_o")

    if tap == "x2":
        return _finish_tap(nc, sems, cnt, xo, [(xacc[:, :], 0)]), sem_guards

    idx16 = alloc("idx16", [128, 3072], I16)
    cw_sb = alloc("cw_sb", [128, 6 * 64], F32)
    h_lo = alloc("h_lo", [128, KT * T], BF16)
    rowbuf = alloc("rowbuf2", [1, 3 * T], F32)
    a_rep = alloc("a_rep2", [128, T], F32)
    b_rep = alloc("b_rep2", [128, T], F32)
    tmpT = alloc("tmpT2", [128, T], F32)

    def live_get(name, _m=None):
        return {"rowbuf": rowbuf, "a_rep": a_rep, "b_rep": b_rep}[name]

    phase()
    emit_ln(xacc, None, None)
    for mt in range(KT):
        xs = xacc[:, mt * T:(mt + 1) * T]
        em("ve", lambda E, xs=xs: E.tensor_mul(tmpT[:, :], xs, a_rep[:, :]))
        em("ve", lambda E: E.tensor_sub(tmpT[:, :], tmpT[:, :], b_rep[:, :]))
        em("ve", lambda E, mt=mt: E.tensor_scalar(tmpT[:, :], tmpT[:, :],
                                                  der[:, 32 + mt:33 + mt], mod_sb[:, 24 + mt:25 + mt],
                                                  op0=ALU.mult, op1=ALU.add))
        v_h2 = mark("ve", "V")
        wait("sc", "V", v_h2)
        s_hbf = inc("sc", "S", lambda E, mt=mt: E.copy(h_bf[:, mt * T:(mt + 1) * T], tmpT[:, :]))
        wait("ve", "S", s_hbf)
        em("ve", lambda E, mt=mt: E.tensor_sub(h_lo[:, mt * T:(mt + 1) * T], tmpT[:, :], h_bf[:, mt * T:(mt + 1) * T]))
        em("ve", lambda E, xs=xs, mt=mt: E.tensor_single_scalar(xs, xs, der[:, 56 + mt:57 + mt], ALU.mult))
    vs_ln2 = mark("ve", "V")
    wait("sy", "V", vs_ln2)
    dma("sy", xp_dram, xacc[:, :], "DX")
    wait("sy", "DX", cnt["DX"])
    run_phase()
    free("tmpT2", "b_rep2", "a_rep2", "rowbuf2")
    free("xacc")

    if tap == "h2":
        return _finish_tap(nc, sems, cnt, xo, [(h_bf[:, :], 0)]), sem_guards

    h2tok_sb = alloc("h2tok_sb", [128, 17 * C], BF16, side="left")
    phase()
    ps_tb0 = palloc("ps_tb0", [128, 1024], BF16)
    ps_tb1 = palloc("ps_tb1", [128, 1024], BF16)
    ptb = [ps_tb0, ps_tb1]
    tr_busy = [None, None]
    em("ve", lambda E: E.memset(h2tok_sb[:, 16 * C:17 * C], 0.0))
    ii = 0
    for tt in range(NQT):
        for kt in range(KT):
            pb = ptb[ii % 2]
            if tr_busy[ii % 2] is not None:
                wait("te", "V", tr_busy[ii % 2])
            tv = inc("te", "T", lambda E, pb=pb, kt=kt, tt=tt:
                     E.transpose(pb[:, 0:128], h_bf[:, kt * T + tt * 128: kt * T + (tt + 1) * 128], idn_b[:, :]))
            wait("ve", "T", tv)
            tr_busy[ii % 2] = inc("ve", "V", lambda E, pb=pb, kt=kt, tt=tt:
                                  E.tensor_copy(h2tok_sb[:, tt * C + kt * 128: tt * C + (kt + 1) * 128], pb[:, 0:128]))
            ii += 1
    v_h2tok = cnt["V"]
    run_phase()
    free("ps_tb1", "ps_tb0")

    phase()
    scoresT = alloc("scoresT", [64, T], F32)
    gw_hi = alloc("gw_hi", [128, KT * 64], BF16)
    gw_lo = alloc("gw_lo", [128, KT * 64], BF16)
    s_tok = alloc("s_tok", [128, NQT * 64], F32)
    sc_tok = alloc("sc_tok", [128, NQT * 64], F32)
    w_tok = alloc("w_tok", [128, NQT * 64], F32)
    r16 = alloc("r16", [128, NQT * 8], F32)
    r16b = alloc("r16b", [128, NQT * 8], F32)
    rr = alloc("rr", [128, NQT], F32)
    rr2 = alloc("rr2", [128, NQT], F32)

    s_hi = inc("sc", "S", lambda E: E.copy(gw_hi[:, :], gw_f[:, :]))
    wait("ve", "S", s_hi)
    v_lo = inc("ve", "V", lambda E: E.tensor_sub(gw_lo[:, :], gw_f[:, :], gw_hi[:, :]))
    wait("te", "V", v_lo)
    for c in range(NCH):
        o = psum_s[0:64, c * 512:(c + 1) * 512]
        first = True
        for kt in range(KT):
            for (wl, rh) in ((gw_hi, h_bf), (gw_hi, h_lo), (gw_lo, h_bf)):
                l = wl[:, kt * 64:(kt + 1) * 64]
                r = rh[:, kt * T + c * 512: kt * T + (c + 1) * 512]
                last = (kt == KT - 1) and (wl is gw_lo)
                em("te", lambda E, o=o, l=l, r=r, first=first, last=last: E.matmul(o, l, r, start=first, stop=last))
                first = False
    t_gs = mark("te", "T")
    wait("sc", "T", t_gs)
    s_sig = inc("sc", "S", lambda E: E.activation(scoresT[:, :], psum_s[0:64, 0:T], AF.Sigmoid))
    wait("te", "S", s_sig)
    tp = [ps_a, ps_b]; tb = [None, None]
    for tt in range(NQT):
        pb = tp[tt % 2]
        if tb[tt % 2] is not None:
            wait("te", "V", tb[tt % 2])
        tv = inc("te", "T", lambda E, pb=pb, tt=tt: E.transpose(pb[:, 0:64], scoresT[:, tt * 128:(tt + 1) * 128], idn_f[0:64, 0:64]))
        wait("ve", "T", tv)
        tb[tt % 2] = inc("ve", "V", lambda E, pb=pb, tt=tt: E.tensor_copy(s_tok[:, tt * 64:(tt + 1) * 64], pb[:, 0:64]))
    E3 = lambda t: t.rearrange("p (a e) -> p a e", e=64)
    G3 = lambda t: t.rearrange("p (a g e) -> p (a g) e", g=8, e=8)
    GS = lambda t: t.rearrange("p (a g) -> p a g", g=8)
    bview = brep[:, :].rearrange("p (o e) -> p o e", o=1).broadcast_to([128, NQT, 64])
    em("ve", lambda E: E.tensor_tensor(E3(sc_tok[:, :]), E3(s_tok[:, :]), bview, op=ALU.add))
    em("ve", lambda E: E.tensor_reduce(r16[:, :].rearrange("p (a g o) -> p (a g) o", g=8, o=1),
                                       G3(sc_tok[:, :]), axis=AX.X, op=ALU.max))
    gmb = r16[:, :].rearrange("p (q o) -> p q o", o=1).broadcast_to([128, 128, 8])
    em("ve", lambda E: E.tensor_tensor(G3(w_tok[:, :]),
                                       G3(sc_tok[:, :]), gmb, op=ALU.is_equal))
    em("ve", lambda E: E.scalar_tensor_tensor(w_tok[:, :], w_tok[:, :], -1e30, sc_tok[:, :], op0=ALU.mult, op1=ALU.add))
    em("ve", lambda E: E.tensor_reduce(r16b[:, :].rearrange("p (q o) -> p q o", o=1),
                                       G3(w_tok[:, :]), axis=AX.X, op=ALU.max))
    em("ve", lambda E: E.tensor_add(r16[:, :], r16[:, :], r16b[:, :]))
    for it in range(3):
        em("ve", lambda E: E.tensor_reduce(rr[:, :].rearrange("p (a o) -> p a o", o=1),
                                           GS(r16[:, :]), axis=AX.X, op=ALU.max))
        vsep("ve", "V")
        rrb = rr[:, :].rearrange("p (a o) -> p a o", o=1).broadcast_to([128, NQT, 8])
        em("ve", lambda E: E.tensor_tensor(GS(r16b[:, :]), GS(r16[:, :]), rrb, op=ALU.is_equal))
        em("ve", lambda E: E.scalar_tensor_tensor(r16[:, :], r16b[:, :], -1e30, r16[:, :], op0=ALU.mult, op1=ALU.add))
    em("ve", lambda E: E.tensor_reduce(rr[:, :].rearrange("p (a o) -> p a o", o=1),
                                       GS(r16[:, :]), axis=AX.X, op=ALU.max))
    vsep("ve", "V")
    em("ve", lambda E: E.tensor_reduce(r16[:, :].rearrange("p (q o) -> p q o", o=1),
                                       G3(sc_tok[:, :]), axis=AX.X, op=ALU.max))
    em("ve", lambda E: E.tensor_tensor(G3(w_tok[:, :]),
                                       G3(sc_tok[:, :]),
                                       r16[:, :].rearrange("p (q o) -> p q o", o=1).broadcast_to([128, 128, 8]), op=ALU.is_equal))
    em("ve", lambda E: E.scalar_tensor_tensor(w_tok[:, :], w_tok[:, :], -1e30, sc_tok[:, :], op0=ALU.mult, op1=ALU.add))
    em("ve", lambda E: E.tensor_reduce(r16b[:, :].rearrange("p (q o) -> p q o", o=1),
                                       G3(w_tok[:, :]), axis=AX.X, op=ALU.max))
    em("ve", lambda E: E.tensor_add(r16[:, :], r16[:, :], r16b[:, :]))
    rrb = rr[:, :].rearrange("p (a o) -> p a o", o=1).broadcast_to([128, NQT, 8])
    em("ve", lambda E: E.tensor_tensor(GS(r16b[:, :]), GS(r16[:, :]), rrb, op=ALU.is_lt))
    em("ve", lambda E: E.tensor_single_scalar(r16b[:, :], r16b[:, :], -1e30, ALU.mult))
    penb = r16b[:, :].rearrange("p (q o) -> p q o", o=1).broadcast_to([128, 128, 8])
    em("ve", lambda E: E.tensor_tensor(G3(w_tok[:, :]), G3(sc_tok[:, :]), penb, op=ALU.add))
    for it in range(7):
        em("ve", lambda E: E.tensor_reduce(rr[:, :].rearrange("p (a o) -> p a o", o=1),
                                           E3(w_tok[:, :]), axis=AX.X, op=ALU.max))
        vsep("ve", "V")
        rrb2 = rr[:, :].rearrange("p (a o) -> p a o", o=1).broadcast_to([128, NQT, 64])
        em("ve", lambda E: E.tensor_tensor(E3(sc_tok[:, :]), E3(w_tok[:, :]), rrb2, op=ALU.is_equal))
        em("ve", lambda E: E.scalar_tensor_tensor(w_tok[:, :], sc_tok[:, :], -1e30, w_tok[:, :], op0=ALU.mult, op1=ALU.add))
    em("ve", lambda E: E.tensor_reduce(rr[:, :].rearrange("p (a o) -> p a o", o=1),
                                       E3(w_tok[:, :]), axis=AX.X, op=ALU.max))
    vsep("ve", "V")
    em("ve", lambda E: E.tensor_tensor(E3(sc_tok[:, :]), E3(s_tok[:, :]), bview, op=ALU.add))
    em("ve", lambda E: E.tensor_tensor(G3(w_tok[:, :]), G3(sc_tok[:, :]), penb, op=ALU.add))
    rrb2 = rr[:, :].rearrange("p (a o) -> p a o", o=1).broadcast_to([128, NQT, 64])
    em("ve", lambda E: E.tensor_tensor(E3(sc_tok[:, :]), E3(w_tok[:, :]), rrb2, op=ALU.is_ge))
    em("ve", lambda E: E.tensor_mul(w_tok[:, :], sc_tok[:, :], s_tok[:, :]))
    em("ve", lambda E: E.tensor_reduce(rr[:, :].rearrange("p (a o) -> p a o", o=1),
                                       E3(w_tok[:, :]), axis=AX.X, op=ALU.add))
    vsep("ve", "V")
    em("ve", lambda E: E.tensor_single_scalar(rr[:, :], rr[:, :], 1e-20, ALU.add))
    vsep("ve", "V")
    em("ve", lambda E: E.reciprocal(rr2[:, :], rr[:, :]))
    vsep("ve", "V")
    em("ve", lambda E: E.tensor_single_scalar(rr[:, :], rr2[:, :], 2.5, ALU.mult))
    vsep("ve", "V")
    rrb2 = rr[:, :].rearrange("p (a o) -> p a o", o=1).broadcast_to([128, NQT, 64])
    em("ve", lambda E: E.tensor_tensor(E3(w_tok[:, :]), E3(w_tok[:, :]), rrb2, op=ALU.mult))
    v_rt = mark("ve", "V")
    run_phase()

    if tap == "wd":
        return _finish_tap(nc, sems, cnt, xo, [(w_tok[:, :], 0), (sc_tok[:, :], 1024)]), sem_guards

    phase()
    selb = alloc("selb", [128, NQT * 64], BF16)
    rank_f = alloc("rank_f", [128, NQT * 64], F32)
    prank_f = alloc("prank_f", [128, NQT * 64], F32)
    selT = alloc("selT", [64, T], BF16)
    prT = alloc("prT", [64, T], F32)
    dmA = alloc("dmA", [128, NQT * 64], F32)
    dmB = alloc("dmB", [128, NQT * 64], F32)
    mscr = alloc("mscr", [128, NQT * 64], F32)
    mscr2 = alloc("mscr2", [128, NQT * 64], F32)
    dAall = alloc("dAall", [128, 128], F32)
    dBall = alloc("dBall", [128, 128], F32)
    cwall = alloc("cwall", [128, 128], F32)
    dAi = alloc("dAi", [128, 128], I32)
    dBi = alloc("dBi", [128, 128], I32)
    ri32 = alloc("ri32", [128, NQT * 64], I32)
    rs32 = alloc("rs32", [128, NQT * 64], I32)
    fill16 = alloc("fill16", [16, 3072], I16)
    zf32 = alloc("zf32", [128, 384], F32)

    em("ve", lambda E: E.memset(fill16[:, :], 2048))
    em("ve", lambda E: E.memset(zf32[:, :], 0.0))
    em("ve", lambda E: E.memset(idx16[:, :], 0))
    em("ve", lambda E: E.tensor_copy(selb[:, :], sc_tok[:, :]))
    v_selb = mark("ve", "V")

    wait("gp", "V", v_selb)
    dma("gp", bass.AP(lists_dram.tensor, 0, [[3072, 16], [1, 3072]]), fill16[:, :], "DW")
    dma("gp", bass.AP(lists_dram.tensor, 16 * 3072, [[64, 1], [1, 64]]), fill16[0:1, 0:64], "DW")
    dma("gp", bass.AP(cwrep_dram.tensor, 0, [[384, 128], [1, 384]]), zf32[:, :], "DW")
    dma("gp", bass.AP(cwrep_dram.tensor, 64 * CAP, [[16, 1], [1, 16]]), zf32[0:1, 0:16], "DW")
    dW_prefill = cnt["DW"]

    wait("te", "V", v_selb)
    tp = [ps_a, ps_b]
    tb = [None, None]
    for mt in range(NQT):
        pb = tp[mt % 2]
        if tb[mt % 2] is not None:
            wait("te", "V", tb[mt % 2])
        for kt in range(mt + 1):
            l = ones128b[:, :] if kt < mt else triSb[:, :]
            r = selb[:, kt * 64:(kt + 1) * 64]
            em("te", lambda E, pb=pb, l=l, r=r, kt=kt, mt=mt:
               E.matmul(pb[:, 0:64], l, r, start=(kt == 0), stop=(kt == mt)))
        tv = mark("te", "T")
        wait("ve", "T", tv)
        tb[mt % 2] = inc("ve", "V", lambda E, pb=pb, mt=mt: E.tensor_copy(rank_f[:, mt * 64:(mt + 1) * 64], pb[:, 0:64]))
    for tt in range(NQT):
        pb = tp[tt % 2]
        if tb[tt % 2] is not None:
            wait("te", "V", tb[tt % 2])
        tv = inc("te", "T", lambda E, pb=pb, tt=tt: E.transpose(pb[0:64, 0:128], sc_tok[:, tt * 64:(tt + 1) * 64], idn_f[:, :]))
        wait("ve", "T", tv)
        tb[tt % 2] = inc("ve", "V", lambda E, pb=pb, tt=tt: E.tensor_copy(selT[:, tt * 128:(tt + 1) * 128], pb[0:64, 0:128]))
    wait("te", "V", tb[(NQT - 1) % 2])
    for c in range(NCH):
        pb = tp[c % 2]
        if tb[c % 2] is not None:
            wait("te", "V", tb[c % 2])
        tv = inc("te", "T", lambda E, pb=pb, c=c: E.matmul(pb[0:64, 0:512], tri64b[:, :], selT[:, c * 512:(c + 1) * 512],
                                                           start=True, stop=True))
        wait("ve", "T", tv)
        tb[c % 2] = inc("ve", "V", lambda E, pb=pb, c=c: E.tensor_copy(prT[:, c * 512:(c + 1) * 512], pb[0:64, 0:512]))
    for tt in range(NQT):
        pb = tp[tt % 2]
        if tb[tt % 2] is not None:
            wait("te", "V", tb[tt % 2])
        tv = inc("te", "T", lambda E, pb=pb, tt=tt: E.transpose(pb[:, 0:64], prT[0:64, tt * 128:(tt + 1) * 128], idn_f[0:64, 0:64]))
        wait("ve", "T", tv)
        tb[tt % 2] = inc("ve", "V", lambda E, pb=pb, tt=tt: E.tensor_copy(prank_f[:, tt * 64:(tt + 1) * 64], pb[:, 0:64]))

    E48 = e48c[:, :].rearrange("p (o e) -> p o e", o=1).broadcast_to([128, NQT, 64])
    E6 = e6c[:, :].rearrange("p (o e) -> p o e", o=1).broadcast_to([128, NQT, 64])
    em("ve", lambda E: E.tensor_copy(ri32[:, :], rank_f[:, :]))
    em("ve", lambda E: E.tensor_single_scalar(rs32[:, :], ri32[:, :], 15, ALU.bitwise_and))
    em("ve", lambda E: E.tensor_copy(mscr[:, :], rs32[:, :]))
    em("ve", lambda E: E.tensor_single_scalar(rs32[:, :], ri32[:, :], 4, ALU.arith_shift_right))
    em("ve", lambda E: E.tensor_copy(mscr2[:, :], rs32[:, :]))
    em("ve", lambda E: E.tensor_single_scalar(dmA[:, :], mscr[:, :], 3072.0, ALU.mult))
    em("ve", lambda E: E.tensor_add(dmA[:, :], dmA[:, :], mscr2[:, :]))
    em("ve", lambda E: E.tensor_tensor(E3(dmA[:, :]), E3(dmA[:, :]), E48, op=ALU.add))
    em("ve", lambda E: E.tensor_single_scalar(rs32[:, :], ri32[:, :], 127, ALU.bitwise_and))
    em("ve", lambda E: E.tensor_copy(mscr[:, :], rs32[:, :]))
    em("ve", lambda E: E.tensor_single_scalar(rs32[:, :], ri32[:, :], 7, ALU.arith_shift_right))
    em("ve", lambda E: E.tensor_copy(mscr2[:, :], rs32[:, :]))
    em("ve", lambda E: E.tensor_single_scalar(dmB[:, :], mscr[:, :], 384.0, ALU.mult))
    em("ve", lambda E: E.tensor_add(dmB[:, :], dmB[:, :], mscr2[:, :]))
    em("ve", lambda E: E.tensor_tensor(E3(dmB[:, :]), E3(dmB[:, :]), E6, op=ALU.add))
    em("ve", lambda E: E.tensor_single_scalar(mscr[:, :], rank_f[:, :], float(CAP), ALU.is_lt))
    em("ve", lambda E: E.tensor_mul(dmA[:, :], dmA[:, :], mscr[:, :]))
    em("ve", lambda E: E.tensor_scalar(mscr2[:, :], mscr[:, :], -float(16 * 3072), float(16 * 3072), op0=ALU.mult, op1=ALU.add))
    em("ve", lambda E: E.tensor_add(dmA[:, :], dmA[:, :], mscr2[:, :]))
    em("ve", lambda E: E.tensor_mul(dmB[:, :], dmB[:, :], mscr[:, :]))
    em("ve", lambda E: E.tensor_scalar(mscr2[:, :], mscr[:, :], -float(64 * CAP), float(64 * CAP), op0=ALU.mult, op1=ALU.add))
    em("ve", lambda E: E.tensor_add(dmB[:, :], dmB[:, :], mscr2[:, :]))
    for k in range(8):
        em("ve", lambda E, k=k: E.tensor_single_scalar(mscr[:, :], prank_f[:, :], float(k), ALU.is_equal))
        em("ve", lambda E: E.tensor_mul(mscr[:, :], mscr[:, :], sc_tok[:, :]))
        em("ve", lambda E: E.tensor_mul(mscr2[:, :], mscr[:, :], dmA[:, :]))
        em("ve", lambda E, k=k: E.tensor_reduce(dAall[:, k * 16:(k + 1) * 16].rearrange("p (a o) -> p a o", o=1),
                                                E3(mscr2[:, :]), axis=AX.X, op=ALU.add))
        em("ve", lambda E: E.tensor_mul(mscr2[:, :], mscr[:, :], dmB[:, :]))
        em("ve", lambda E, k=k: E.tensor_reduce(dBall[:, k * 16:(k + 1) * 16].rearrange("p (a o) -> p a o", o=1),
                                                E3(mscr2[:, :]), axis=AX.X, op=ALU.add))
        em("ve", lambda E: E.tensor_mul(mscr2[:, :], mscr[:, :], w_tok[:, :]))
        em("ve", lambda E, k=k: E.tensor_reduce(cwall[:, k * 16:(k + 1) * 16].rearrange("p (a o) -> p a o", o=1),
                                                E3(mscr2[:, :]), axis=AX.X, op=ALU.add))
    vsep("ve", "V")
    em("ve", lambda E: E.tensor_copy(dAi[:, :], dAall[:, :]))
    em("ve", lambda E: E.tensor_copy(dBi[:, :], dBall[:, :]))
    v_disp = mark("ve", "V")

    wait("gp", "DW", dW_prefill)
    wait("gp", "V", v_disp)
    for tt in range(NQT):
        for k in range(8):
            col = k * 16 + tt
            em("gp", lambda E, col=col, tt=tt: E.indirect_dma_start(
                bass.AP(lists_dram.tensor, 0, [[1, 16 * 3072 + 64], [1, 1]]),
                IndirectOffsetOnAxis(dAi[:, col:col + 1], 0),
                tok16[:, tt:tt + 1], None))
            cnt["DW"] += 16
            sched["gp"][-1] = (lambda f: lambda E: f(E).then_inc(sems["DW"], 16))(sched["gp"][-1])
            em("gp", lambda E, col=col: E.indirect_dma_start(
                bass.AP(cwrep_dram.tensor, 0, [[1, 64 * CAP + 16], [1, 1]]),
                IndirectOffsetOnAxis(dBi[:, col:col + 1], 0),
                cwall[:, col:col + 1], None))
            cnt["DW"] += 16
            sched["gp"][-1] = (lambda f: lambda E: f(E).then_inc(sems["DW"], 16))(sched["gp"][-1])
    wait("gp", "DW", cnt["DW"])
    dma("gp", idx16[:, :], bass.AP(lists_dram.tensor, 0, [[0, 8], [3072, 16], [1, 3072]]), "DW")
    dma("gp", cw_sb[:, :], bass.AP(cwrep_dram.tensor, 0, [[384, 128], [1, 384]]), "DW")
    dW_disp = cnt["DW"]
    run_phase()
    free("zf32", "fill16", "rs32", "ri32", "dBi", "dAi", "cwall", "dBall", "dAall", "mscr2", "mscr",
         "dmB", "dmA", "prT", "selT", "prank_f", "rank_f", "selb")
    free("rr2", "rr", "r16b", "r16", "w_tok", "sc_tok", "s_tok", "gw_lo", "gw_hi", "scoresT", "h_lo")

    if tap == "disp":
        return _finish_tap(nc, sems, cnt, xo, [(idx16[:, :], 0), (cw_sb[:, :], 4096)]), sem_guards

    phase()
    ws_g = alloc("ws_g", [128, KT * FF], BF16)
    ws_u = alloc("ws_u", [128, KT * FF], BF16)
    ws_d = alloc("ws_d", [128, 4 * C], BF16)
    hsh = alloc("hsh", [128, 4 * T], BF16)
    sg_sh = alloc("sg_sh", [128, 4 * 512], BF16)
    ysh_st = alloc("ysh_st", [128, 2 * C], BF16)
    ps_c = palloc("ps_c", [128, 512], F32)
    ps_d = palloc("ps_d", [128, 512], F32)

    dma("gp", ws_g[:, :], wsg, "DW")
    dma("gp", ws_u[:, :], wsu, "DW")
    dma("gp", ws_d[:, :], wsd, "DW")
    dW_sh = cnt["DW"]

    gb_banks = [psum_s[:, 0:512], psum_s[:, 512:1024], psum_s[:, 1024:1536], psum_s[:, 1536:2048]]
    d_banks = [ps_a, ps_b, ps_c, ps_d]
    d_busy = [None, None, None, None]
    gb_busy = [None, None, None, None]

    em("ve", lambda E: E.memset(ysh_st[:, :], 0.0))
    v_zy = mark("ve", "V")
    wait("sy", "V", v_zy)
    dma("sy", bass.AP(ytok_dram.tensor, T * C, [[C, 1], [1, C]]), ysh_st[0:1, 0:C], "DX")
    dma("sy", bass.AP(ytok2_dram.tensor, 0, [[C, 128], [C * 128, 16], [1, C]]),
        bass.AP(ysh_st[:, :].tensor, 0, [[2 * C, 128], [0, 16], [1, C]]), "DX")
    dma("sy", bass.AP(ytok2_dram.tensor, T * C, [[C, 1], [1, C]]), ysh_st[0:1, 0:C], "DX")
    dX_ytrash = cnt["DX"]
    wait("sy", "DX", dX_ytrash)

    s_si = [None]
    sh_prev_hm = [None]
    def sh_gu(n):
        v_hm = None
        for mh in range(2):
            for src, bks, tag in ((ws_g, (0, 1), "g"), (ws_u, (2, 3), "u")):
                for mi, bki in enumerate(bks):
                    m = mh * 2 + mi
                    o = gb_banks[bki]
                    if gb_busy[bki] is not None:
                        eng, v = gb_busy[bki]
                        wait("te", eng, v)
                        gb_busy[bki] = None
                    for kt in range(KT):
                        l = src[:, kt * FF + m * 128: kt * FF + (m + 1) * 128]
                        r = h_bf[:, kt * T + n * 512: kt * T + (n + 1) * 512]
                        em("te", lambda E, o=o, l=l, r=r, kt=kt: E.matmul(o, l, r, start=(kt == 0), stop=(kt == KT - 1)))
                tg = mark("te", "T")
                if tag == "g":
                    wait("sc", "T", tg)
                    if sh_prev_hm[0] is not None:
                        wait("sc", "V", sh_prev_hm[0])
                    for mi, bki in enumerate(bks):
                        m = mh * 2 + mi
                        em("sc", lambda E, m=m, bki=bki: E.activation(sg_sh[:, m * 512:(m + 1) * 512], gb_banks[bki], AF.Silu))
                    s_si[0] = mark("sc", "S")
                    for bki in bks:
                        gb_busy[bki] = ("S", s_si[0])
                else:
                    wait("ve", "T", tg)
                    wait("ve", "S", s_si[0])
                    for mi, bki in enumerate(bks):
                        m = mh * 2 + mi
                        dst = hsh[:, m * T + n * 512: m * T + (n + 1) * 512]
                        em("ve", lambda E, dst=dst, m=m, bki=bki: E.tensor_mul(dst, sg_sh[:, m * 512:(m + 1) * 512], gb_banks[bki]))
                    v_hm = mark("ve", "V")
                    sh_prev_hm[0] = v_hm
                    for bki in bks:
                        gb_busy[bki] = ("V", v_hm)
        return v_hm

    wait("te", "DW", dW_sh)
    v_hsh = None
    for n in range(NCH):
        v_hsh = sh_gu(n)
    st_rot = 0
    st_busy2 = [None, None]
    for tw in range(NQT):
        half = tw % 2
        if st_busy2[half] is not None:
            wait("ve", "DX", st_busy2[half])
        for cc in range(2):
            bank = d_banks[(st_rot := st_rot + 1) % 4]
            if d_busy[st_rot % 4] is not None:
                wait("te", "V", d_busy[st_rot % 4])
                d_busy[st_rot % 4] = None
            if tw == 0 and cc == 0:
                wait("te", "V", v_hsh)
            for kt in range(4):
                l = hsh[:, kt * T + tw * 128: kt * T + (tw + 1) * 128]
                r = ws_d[:, kt * C + cc * 512:(kt * C + cc * 512) + 512]
                em("te", lambda E, bank=bank, l=l, r=r, kt=kt: E.matmul(bank[:, :], l, r, start=(kt == 0), stop=(kt == 3)))
            td = mark("te", "T")
            wait("ve", "T", td)
            dst = ysh_st[:, half * C + cc * 512: half * C + (cc + 1) * 512]
            em("ve", lambda E, dst=dst, bank=bank: E.tensor_copy(dst, bank[:, :]))
            d_busy[st_rot % 4] = mark("ve", "V")
        v_ysh = cnt["V"]
        wait("sy", "V", v_ysh)
        dma("sy", bass.AP(ytok_dram.tensor, tw * 128 * C, [[C, 128], [1, C]]), ysh_st[:, half * C:(half + 1) * C], "DX")
        st_busy2[half] = cnt["DX"]
    dX_shared = cnt["DX"]
    run_phase()
    if tap == "ysh":
        return _finish_tap_dram(nc, sems, cnt, xo, ytok_dram, 16384), sem_guards
    free("ysh_st", "sg_sh", "hsh", "ws_d", "ws_u", "ws_g")

    exp_sb = [alloc("exp0", [128, EPB], BF16), alloc("exp1", [128, EPB], BF16)]
    gat = [alloc("gat0", [128, 8 * CAP], BF16), alloc("gat1", [128, 8 * CAP], BF16)]
    hm = [alloc("hm0", [128, 4 * CAP], BF16), alloc("hm1", [128, 4 * CAP], BF16)]
    sg = alloc("sg", [128, 4 * CAP], BF16)
    y_sb = alloc("y_sb", [128, 6 * C], BF16)
    phase()
    slot_done = [None] * NE
    gu_done = [None] * NE
    dW_exp = [None] * NE
    dW_gat = [None] * NE
    dW_scat = [None] * NE
    hm_guard = [None, None]
    CH = [(0, 512), (512, 256)]

    def emit_exp_dma(e):
        if e >= 2:
            wait("gp", "T", slot_done[e - 2])
        src = bass.AP(experts.tensor, e * 128 * EPB, [[EPB, 128], [1, EPB]])
        dma("gp", exp_sb[e % 2][:, :], src, "DW")
        dW_exp[e] = cnt["DW"]

    def emit_gather(e):
        if e >= 2:
            wait("gp", "T", gu_done[e - 2])
        em("gp", lambda E, e=e: E.dma_gather(
            gat[e % 2][:, :].rearrange("p (a b) -> p a b", b=CAP),
            h2tok_sb[:, :],
            idx16[:, e * 48:(e + 1) * 48], CAP, nidx_reg, C, transpose=True,
            sbuf_tokens_per_rank=128, sbuf_free_dim_per_rank=2 * C,
            sbuf_free_dim_pad_per_rank=0, sbuf_byte_offset=0))
        cnt["DW"] += 16
        sched["gp"][-1] = (lambda f: lambda E: f(E).then_inc(sems["DW"], 16))(sched["gp"][-1])
        dW_gat[e] = cnt["DW"]

    nidx_reg = nc.gpsimd.alloc_register("nidx_reg")
    em("gp", lambda E: E.reg_mov(nidx_reg, CAP))
    wait("gp", "V", v_h2tok)
    wait("gp", "DW", dW_disp)
    emit_exp_dma(0); emit_gather(0)
    emit_exp_dma(1); emit_gather(1)

    s_si2 = [None]
    def emit_gu_e(e):
        sl = exp_sb[e % 2]
        first = True
        v_hm = None
        for (c0, w) in CH:
            for mh in range(2):
                for woff, bks, tag in ((0, (0, 1), "g"), (KT * FF, (2, 3), "u")):
                    for mi, bki in enumerate(bks):
                        m = mh * 2 + mi
                        o = gb_banks[bki]
                        ow = psum_s[:, bki * 512: bki * 512 + w]
                        if gb_busy[bki] is not None:
                            eng, v = gb_busy[bki]
                            wait("te", eng, v)
                            gb_busy[bki] = None
                        if first:
                            wait("te", "DW", dW_exp[e])
                            wait("te", "DW", dW_gat[e])
                            first = False
                        for kt in range(KT):
                            l = sl[:, woff + kt * FF + m * 128: woff + kt * FF + (m + 1) * 128]
                            r = gat[e % 2][:, kt * CAP + c0: kt * CAP + c0 + w]
                            em("te", lambda E, ow=ow, l=l, r=r, kt=kt: E.matmul(ow, l, r, start=(kt == 0), stop=(kt == KT - 1)))
                    tg = mark("te", "T")
                    if tag == "g":
                        wait("sc", "T", tg)
                        for mi, bki in enumerate(bks):
                            m = mh * 2 + mi
                            em("sc", lambda E, m=m, bki=bki, w=w, c0=c0: E.activation(
                                sg[:, m * CAP + c0: m * CAP + c0 + w], psum_s[:, bki * 512: bki * 512 + w], AF.Silu))
                        s_si2[0] = mark("sc", "S")
                        for bki in bks:
                            gb_busy[bki] = ("S", s_si2[0])
                    else:
                        wait("ve", "T", tg)
                        wait("ve", "S", s_si2[0])
                        if mh == 0 and c0 == 0 and hm_guard[e % 2] is not None:
                            wait("ve", "T", hm_guard[e % 2])
                        for mi, bki in enumerate(bks):
                            m = mh * 2 + mi
                            dst = hm[e % 2][:, m * CAP + c0: m * CAP + c0 + w]
                            em("ve", lambda E, dst=dst, m=m, bki=bki, w=w, c0=c0: E.tensor_mul(
                                dst, sg[:, m * CAP + c0: m * CAP + c0 + w], psum_s[:, bki * 512: bki * 512 + w]))
                        v_hm = mark("ve", "V")
                        for bki in bks:
                            gb_busy[bki] = ("V", v_hm)
        gu_done[e] = cnt["T"]
        return v_hm

    dr = [0]
    v_cw_wait = [False]
    def emit_down_e(e, v_hm):
        wait("te", "V", v_hm)
        for tw in range(6):
            for cc in range(2):
                dr[0] += 1
                bank = d_banks[dr[0] % 4]
                if d_busy[dr[0] % 4] is not None:
                    wait("te", "V", d_busy[dr[0] % 4])
                    d_busy[dr[0] % 4] = None
                for kt in range(4):
                    l = hm[e % 2][:, kt * CAP + tw * 128: kt * CAP + (tw + 1) * 128]
                    r = exp_sb[e % 2][:, 2 * KT * FF + kt * C + cc * 512: 2 * KT * FF + kt * C + (cc + 1) * 512]
                    em("te", lambda E, bank=bank, l=l, r=r, kt=kt: E.matmul(bank[:, :], l, r, start=(kt == 0), stop=(kt == 3)))
                td = mark("te", "T")
                if not v_cw_wait[0]:
                    wait("ve", "DW", dW_disp)
                    wait("ve", "DX", dX_shared)
                    v_cw_wait[0] = True
                if e >= 1 and tw == 0 and cc == 0 and dW_scat[e - 1] is not None:
                    wait("ve", "DW", dW_scat[e - 1])
                wait("ve", "T", td)
                dst = y_sb[:, tw * C + cc * 512: tw * C + (cc + 1) * 512]
                em("ve", lambda E, dst=dst, bank=bank, e=e, tw=tw: E.tensor_single_scalar(
                    dst, bank[:, :], cw_sb[:, e * 6 + tw: e * 6 + tw + 1], ALU.mult))
                d_busy[dr[0] % 4] = mark("ve", "V")
        hm_guard[e % 2] = cnt["T"]
        slot_done[e] = cnt["T"]
        return cnt["V"]

    for e in range(NE):
        v_hm = emit_gu_e(e)
        v_y = emit_down_e(e, v_hm)
        if e + 2 < NE:
            emit_exp_dma(e + 2)
            emit_gather(e + 2)
        if e == 0:
            wait("gp", "DX", dX_shared)
        if e >= 2:
            wait("gp", "DW", dW_scat[e - 2])
        wait("gp", "V", v_y)
        ytgt = ytok_dram if e % 2 == 0 else ytok2_dram
        em("gp", lambda E, e=e, ytgt=ytgt: E.dma_scatter_add(
            ytgt, y_sb[:, :].rearrange("p (a b) -> p a b", b=C),
            idx16[:, e * 48:(e + 1) * 48], CAP, nidx_reg, C))
        cnt["DW"] += 16
        sched["gp"][-1] = (lambda f: lambda E: f(E).then_inc(sems["DW"], 16))(sched["gp"][-1])
        dW_scat[e] = cnt["DW"]
    run_phase()
    if tap == "ytok":
        return _finish_tap_dram(nc, sems, cnt, xo, ytok_dram, 16384), sem_guards
    free("ps_d", "ps_c")
    free("y_sb", "sg", "hm1", "hm0", "gat1", "gat0", "exp1", "exp0")

    phase()
    xacc = alloc("xacc2", [128, KT * T], F32)
    yst = [alloc("yst0", [128, 2 * C], BF16), alloc("yst1", [128, 2 * C], BF16)]
    ystf = alloc("ystf", [128, C], BF16)
    ps_tf0 = palloc("ps_tf0", [128, 1024], BF16)
    ps_tf1 = palloc("ps_tf1", [128, 1024], BF16)
    dma("sy", xacc[:, :], xp_dram, "DX")
    dX_xr = cnt["DX"]
    wait("sy", "DW", max(dW_scat[NE - 1], dW_scat[NE - 2]))
    dma("sy", yst[0][:, 0:C], bass.AP(ytok_dram.tensor, 0, [[C, 128], [1, C]]), "DX")
    dma("sy", yst[0][:, C:2 * C], bass.AP(ytok2_dram.tensor, 0, [[C, 128], [1, C]]), "DX")
    st_busy3 = [cnt["DX"], None]
    wait("ve", "DX", dX_xr)
    tp2 = [ps_a, ps_b]
    tb2 = [None, None]
    tile_done = [None] * NQT
    ii = 0
    for tt in range(NQT):
        if tt + 1 < NQT:
            if tt >= 1:
                wait("sy", "T", tile_done[tt - 1])
            dma("sy", yst[(tt + 1) % 2][:, 0:C], bass.AP(ytok_dram.tensor, (tt + 1) * 128 * C, [[C, 128], [1, C]]), "DX")
            dma("sy", yst[(tt + 1) % 2][:, C:2 * C], bass.AP(ytok2_dram.tensor, (tt + 1) * 128 * C, [[C, 128], [1, C]]), "DX")
            st_busy3[(tt + 1) % 2] = cnt["DX"]
        wait("ve", "DX", st_busy3[tt % 2])
        if tt >= 1:
            wait("ve", "T", tile_done[tt - 1])
        em("ve", lambda E, tt=tt: E.tensor_add(ystf[:, :], yst[tt % 2][:, 0:C], yst[tt % 2][:, C:2 * C]))
        v_ya = mark("ve", "V")
        wait("te", "V", v_ya)
        for kt in range(KT):
            pb = [ps_tf0, ps_tf1][ii % 2]
            if tb2[ii % 2] is not None:
                wait("te", "V", tb2[ii % 2])
            tv = inc("te", "T", lambda E, pb=pb, kt=kt: E.transpose(
                pb[:, 0:128], ystf[:, kt * 128:(kt + 1) * 128], idn_b[:, :]))
            wait("ve", "T", tv)
            dst = xacc[:, kt * T + tt * 128: kt * T + (tt + 1) * 128]
            tb2[ii % 2] = inc("ve", "V", lambda E, dst=dst, pb=pb: E.tensor_add(dst, dst, pb[:, 0:128]))
            ii += 1
        tile_done[tt] = cnt["T"]
    for mt in range(KT):
        em("ve", lambda E, mt=mt: E.tensor_single_scalar(xacc[:, mt * T:(mt + 1) * T], xacc[:, mt * T:(mt + 1) * T],
                                                         der[:, 40 + mt:41 + mt], ALU.mult))
    v_fin = mark("ve", "V")
    wait("sy", "V", v_fin)
    dma("sy", xo, xacc[:, :], "DO")
    wait("sy", "DO", cnt["DO"])
    run_phase()
    free("ps_tf1", "ps_tf0") if False else None
    free("ystf", "yst1", "yst0", "xacc2")
    free("cw_sb", "idx16")
    for g in reversed(sem_guards):
        g.__exit__(None, None, None)
    mybir.codegen_inst_isa_subclasses(nc)
    return nc


def _finish_tap_dram(nc, sems, cnt, xo, src_dram, nrows_elems):
    with nc.Block() as blk:
        @blk.gpsimd
        def _(E):
            E.dma_start(bass.AP(xo.tensor, 0, [[16384, 128], [1, 16384]]),
                        bass.AP(src_dram.tensor, 0, [[16384, 128], [1, 16384]])).then_inc(sems["DO"], 16)
            E.wait_ge(sems["DO"], 16)
    mybir.codegen_inst_isa_subclasses(nc)
    return nc


def _finish_tap(nc, sems, cnt, xo, taps):
    with nc.Block() as blk:
        @blk.gpsimd
        def _(E):
            n = 0
            for ap, off in taps:
                w = ap.shape[-1] if len(ap.shape) else 0
                dst = bass.AP(xo.tensor, off, [[xo.tensor.shape[1], ap.shape[0]], [1, int(np.prod(ap.shape[1:]))]])
                E.dma_start(dst, ap).then_inc(sems["DO"], 16)
                n += 16
            E.wait_ge(sems["DO"], n)
    mybir.codegen_inst_isa_subclasses(nc)
    return nc



def _swz(w):
    K, M = w.shape
    return np.ascontiguousarray(w.reshape(K // 128, 128, M).transpose(1, 0, 2).reshape(128, -1))


_CACHE = {}


def _staging(inp):
    f32 = np.float32
    perm = np.empty(1024, np.int64)
    for m in range(1024):
        blk = m // 512; h = (m % 512) // 32; j = m % 32
        perm[m] = 64 * h + 2 * j + blk
    st = {}
    st["wqs"] = _swz(np.ascontiguousarray(inp["L0_attn_Wq"].T[:, perm]))
    st["wks"] = _swz(np.ascontiguousarray(inp["L0_attn_Wk"].T[:, perm]))
    st["wvs"] = _swz(np.ascontiguousarray(inp["L0_attn_Wv"].T))
    st["wos"] = _swz(np.ascontiguousarray(inp["L0_attn_Wo"].T))
    st["bqs"] = np.ascontiguousarray(inp["L0_attn_bq"][perm].reshape(8, 128).T)
    st["bks"] = np.ascontiguousarray(inp["L0_attn_bk"][perm].reshape(8, 128).T)
    st["bvs"] = np.ascontiguousarray(inp["L0_attn_bv"].reshape(8, 128).T)
    st["bos"] = np.ascontiguousarray(inp["L0_attn_bo"].reshape(8, 128).T)
    st["wmods"] = _swz(inp["L0_wmod"])
    st["gws"] = _swz(np.ascontiguousarray(inp["L0_ffn_gate_w"].T))
    st["wsg"] = _swz(inp["L0_ffn_Ws_gate"])
    st["wsu"] = _swz(inp["L0_ffn_Ws_up"])
    st["wsd"] = _swz(inp["L0_ffn_Ws_down"])
    ex = np.empty((NE, 128, EPB), f32)
    for e in range(NE):
        ex[e, :, 0:4096] = _swz(inp["L0_ffn_We_gate"][e])
        ex[e, :, 4096:8192] = _swz(inp["L0_ffn_We_up"][e])
        ex[e, :, 8192:12288] = _swz(inp["L0_ffn_We_down"][e])
    st["experts"] = ex
    j = np.arange(32, dtype=np.float64)
    theta = 10000.0 ** (-2.0 * j / 64.0)
    t = np.arange(T, dtype=np.float64)
    fr = t[None, :] * theta[:, None]
    cos32 = np.cos(fr); sin32 = np.sin(fr)
    st["cosr"] = np.tile(cos32, (4, 1)).astype(ml_dtypes.bfloat16)
    st["sinr"] = np.tile(sin32, (4, 1)).astype(ml_dtypes.bfloat16)
    st["idnb"] = np.eye(128, dtype=ml_dtypes.bfloat16)
    st["idnf"] = np.eye(128, dtype=f32)
    q = np.arange(128)[:, None]; kk = np.arange(128)[None, :]
    mt = np.zeros((128, 1024), f32)
    mt[:, 0:512] = -1e30
    mt[:, 512:640] = np.where(q <= kk, 0.0, -1e30)
    st["masktd"] = mt
    st["brepd"] = np.broadcast_to(inp["L0_ffn_gate_bias"], (128, 64)).astype(f32).copy()
    st["tok16d"] = (np.arange(16)[None, :] * 128 + np.arange(128)[:, None]).astype(np.int16)
    st["ones128d"] = np.ones((128, 128), ml_dtypes.bfloat16)
    st["trisd"] = (q < kk).astype(ml_dtypes.bfloat16)
    e1 = np.arange(64)[:, None]; e2 = np.arange(64)[None, :]
    st["tri64d"] = (e1 < e2).astype(ml_dtypes.bfloat16)
    st["e48d"] = np.broadcast_to(np.arange(64, dtype=f32) * 48.0, (128, 64)).copy()
    st["e6d"] = np.broadcast_to(np.arange(64, dtype=f32) * 6.0, (128, 64)).copy()
    return st


def kernel(**inp):
    if "nc" not in _CACHE or _CACHE.get("tap") != TAP:
        _CACHE["nc"] = _build(TAP)
        if isinstance(_CACHE["nc"], tuple):
            _CACHE["nc"] = _CACHE["nc"][0]
        _CACHE["tap"] = TAP
    nc = _CACHE["nc"]
    st = _staging(inp)
    x = inp["L0_attn_x"]
    cva = inp["L0_c"]
    in_maps = []
    for b in range(8):
        m = dict(st)
        m["xt"] = _swz(np.ascontiguousarray(x[b].T))
        m["cv"] = np.ascontiguousarray(cva[b].reshape(8, 128).T)
        in_maps.append(m)
    res = run_bass_kernel_spmd(nc, in_maps, core_ids=list(range(8)))
    outs = []
    for b in range(8):
        o = res.results[b]["xo"]
        o = o.reshape(128, KT, T).transpose(1, 0, 2).reshape(C, T).T
        outs.append(o)
    return np.ascontiguousarray(np.stack(outs)).astype(np.float32)


# version 07 (final)
# speedup vs baseline: 1.4809x; 1.2021x over previous
import numpy as np
import ml_dtypes
import concourse.bass as bass
import concourse.mybir as mybir
from concourse.bass_utils import run_bass_kernel_spmd

F32 = mybir.dt.float32
BF16 = mybir.dt.bfloat16
AF = mybir.ActivationFunctionType
ALU = mybir.AluOpType
AX = mybir.AxisListType

T = 2048
C = 1024
KT = 8
NH = 16
HD = 64
NE = 64
FF = 512
NCH = 4
NQT = 16
EPB = 12288

TAP = None

ENGINES = ("sy", "gp", "te", "ve", "sc")


def _build(tap=None):
    nc = bass.Bass("TRN2", target_bir_lowering=False, detect_race_conditions=False)

    di = {}
    def dram_in(name, shape, dt=F32):
        di[name] = nc.dram_tensor(name, shape, dt, kind="ExternalInput").ap()
        return di[name]

    xt = dram_in("xt", [128, KT * T])
    wqs = dram_in("wqs", [128, KT * C]); wks = dram_in("wks", [128, KT * C])
    wvs = dram_in("wvs", [128, KT * C]); wos = dram_in("wos", [128, KT * C])
    bqs = dram_in("bqs", [128, 8]); bks = dram_in("bks", [128, 8])
    bvs = dram_in("bvs", [128, 8]); bos = dram_in("bos", [128, 8])
    wmods = dram_in("wmods", [128, KT * 6144])
    cv = dram_in("cv", [128, 8])
    gws = dram_in("gws", [128, KT * 64])
    wsg = dram_in("wsg", [128, KT * FF]); wsu = dram_in("wsu", [128, KT * FF])
    wsd = dram_in("wsd", [128, 4 * C])
    experts = dram_in("experts", [NE, 128, EPB])
    cosr = dram_in("cosr", [128, T], BF16); sinr = dram_in("sinr", [128, T], BF16)
    idnb_d = dram_in("idnb", [128, 128], BF16)
    idnf_d = dram_in("idnf", [128, 128], F32)
    maskd = dram_in("maskd", [128, 128], F32)
    masktd = dram_in("masktd", [128, 1024], F32)
    brepd = dram_in("brepd", [128, 64], F32)

    xo = nc.dram_tensor("xo", [128, KT * T], F32, kind="ExternalOutput").ap()
    xp_dram = nc.dram_tensor("xp_scratch", [128, KT * T], F32).ap()

    sem_guards = []
    sems = {}
    cnt = {}
    for k in ("DW", "DX", "DO", "T", "V", "S"):
        g = nc.semaphore("sem_" + k)
        sems[k] = g.__enter__(); sem_guards.append(g)
        cnt[k] = 0

    live = {}
    def alloc(name, shape, dt, side="right"):
        g = nc.sbuf_tensor(name, shape, dt, side=side)
        t = g.__enter__(); live[name] = g
        return t
    def palloc(name, shape, dt):
        g = nc.psum_tensor(name, shape, dt)
        t = g.__enter__(); live[name] = g
        return t
    def free(*names):
        for n in names:
            live.pop(n).__exit__(None, None, None)

    sched = None
    def em(e, fn):
        sched[e].append(fn)
    def dma(e, dst, src, key):
        cnt[key] += 16
        v = cnt[key]
        em(e, lambda E: E.dma_start(dst, src).then_inc(sems[key], 16))
        return v
    def wait(e, key, v):
        em(e, lambda E: E.wait_ge(sems[key], v))
    def inc(e, key, fn):
        cnt[key] += 1
        em(e, lambda E: fn(E).then_inc(sems[key], 1))
        return cnt[key]
    def vsep(e, key):
        v = mark(e, key)
        wait(e, key, v)
    def mark(e, key):
        cnt[key] += 1
        f = sched[e].pop()
        em(e, lambda E, f=f: f(E).then_inc(sems[key], 1))
        return cnt[key]
    def run_phase():
        nonlocal sched
        s = sched
        sched = None
        with nc.Block() as blk:
            @blk.sync
            def _(E):
                for f in s["sy"]: f(E)
            @blk.gpsimd
            def _(E):
                for f in s["gp"]: f(E)
            @blk.tensor
            def _(E):
                for f in s["te"]: f(E)
            @blk.vector
            def _(E):
                for f in s["ve"]: f(E)
            @blk.scalar
            def _(E):
                for f in s["sc"]: f(E)
    def phase():
        nonlocal sched
        sched = {e: [] for e in ENGINES}

    mod_sb = alloc("mod_sb", [128, 48], F32, side="left")
    der = alloc("der", [128, 64], F32, side="left")
    rscr = alloc("rscr", [128, 16], F32, side="left")
    bq_sb = alloc("bq_sb", [128, 8], F32, side="left"); bk_sb = alloc("bk_sb", [128, 8], F32, side="left")
    bv_sb = alloc("bv_sb", [128, 8], F32, side="left"); bo_sb = alloc("bo_sb", [128, 8], F32, side="left")
    bqs_s = alloc("bqs_s", [128, 8], F32, side="left"); bks_s = alloc("bks_s", [128, 8], F32, side="left")
    cv_sb = alloc("cv_sb", [128, 8], F32, side="left"); cvs = alloc("cvs", [128, 8], BF16, side="left")
    idn_b = alloc("idn_b", [128, 128], BF16, side="left")
    idn_f = alloc("idn_f", [128, 128], F32, side="left")
    mask_sb = alloc("mask_sb", [128, 128], F32, side="left")
    maskT_sb = alloc("maskT_sb", [128, 1024], F32, side="left")
    brep = alloc("brep", [128, 64], F32, side="left")
    gw_f = alloc("gw_f", [128, KT * 64], F32, side="left")
    ones_c = alloc("ones_c", [128, 1], BF16, side="left")
    ones_r = alloc("ones_r", [1, 128], F32, side="left")

    h_bf = alloc("h_bf", [128, KT * T], BF16, side="left")

    psum_s = palloc("psum_s", [128, 2048], F32)
    ps_a = palloc("ps_a", [128, 512], F32)
    ps_b = palloc("ps_b", [128, 512], F32)

    phase()
    wmod_buf = alloc("wmod_buf", [128, KT * 6144], BF16)

    dma("sy", cv_sb[:, :], cv, "DX")
    dma("sy", idn_b[:, :], idnb_d, "DX")
    dma("sy", idn_f[:, :], idnf_d, "DX")
    dma("sy", mask_sb[:, :], maskd, "DX")
    dma("sy", maskT_sb[:, :], masktd, "DX")
    dma("sy", brep[:, :], brepd, "DX")
    dma("sy", gw_f[:, :], gws, "DX")
    dma("sy", bq_sb[:, :], bqs, "DX")
    dma("sy", bk_sb[:, :], bks, "DX")
    dma("sy", bv_sb[:, :], bvs, "DX")
    dma("sy", bo_sb[:, :], bos, "DX")
    dX_consts = cnt["DX"]
    dma("gp", wmod_buf[:, :], wmods, "DW")
    dW_wmod = cnt["DW"]

    em("ve", lambda E: E.memset(der[:, :], 0.0))
    em("ve", lambda E: E.memset(ones_c[:, :], 1.0))
    em("ve", lambda E: E.memset(ones_r[:, :], 1.0))
    vs_ones = mark("ve", "V")

    wait("sc", "DX", dX_consts)
    ss_cvs = inc("sc", "S", lambda E: E.activation(cvs[:, :], cv_sb[:, :], AF.Silu))

    wait("te", "DW", dW_wmod)
    wait("te", "S", ss_cvs)
    for mt in range(48):
        for kt in range(KT):
            o, l, r = psum_s[:, mt:mt + 1], wmod_buf[:, kt * 6144 + mt * 128: kt * 6144 + mt * 128 + 128], cvs[:, kt:kt + 1]
            if kt < KT - 1:
                em("te", lambda E, o=o, l=l, r=r, kt=kt: E.matmul(o, l, r, start=(kt == 0), stop=False))
            else:
                em("te", lambda E, o=o, l=l, r=r: E.matmul(o, l, r, start=False, stop=True))
    ts_mod = mark("te", "T")

    wait("ve", "T", ts_mod)
    em("ve", lambda E: E.tensor_copy(mod_sb[:, :], psum_s[:, 0:48]))
    vsep("ve", "V")
    em("ve", lambda E: E.tensor_single_scalar(der[:, 8:16], mod_sb[:, 8:16], 1.0, ALU.add))
    em("ve", lambda E: E.tensor_single_scalar(der[:, 32:40], mod_sb[:, 32:40], 1.0, ALU.add))
    vs_mod1 = mark("ve", "V")
    wait("sc", "V", vs_mod1)
    em("sc", lambda E: E.sign(rscr[:, 0:8], mod_sb[:, 16:24]))
    vsep("sc", "S")
    em("sc", lambda E: E.activation(rscr[:, 8:16], mod_sb[:, 16:24], AF.Abs))
    ss_ga = mark("sc", "S")
    wait("ve", "S", ss_ga)
    em("ve", lambda E: E.tensor_scalar_max(rscr[:, 8:16], rscr[:, 8:16], 0.25))
    vsep("ve", "V")
    em("ve", lambda E: E.tensor_mul(der[:, 16:24], rscr[:, 0:8], rscr[:, 8:16]))
    vs_ga = mark("ve", "V")
    wait("ve", "V", vs_ga)
    wait("sc", "V", vs_ga)
    em("sc", lambda E: E.sign(rscr[:, 0:8], mod_sb[:, 40:48]))
    vsep("sc", "S")
    em("sc", lambda E: E.activation(rscr[:, 8:16], mod_sb[:, 40:48], AF.Abs))
    ss_gb = mark("sc", "S")
    wait("ve", "S", ss_gb)
    em("ve", lambda E: E.tensor_scalar_max(rscr[:, 8:16], rscr[:, 8:16], 0.25))
    vsep("ve", "V")
    em("ve", lambda E: E.tensor_mul(der[:, 40:48], rscr[:, 0:8], rscr[:, 8:16]))
    vsep("ve", "V")
    em("ve", lambda E: E.tensor_mul(der[:, 48:56], der[:, 16:24], bo_sb[:, :]))
    vsep("ve", "V")
    em("ve", lambda E: E.reciprocal(der[:, 56:64], der[:, 40:48]))
    em("ve", lambda E: E.tensor_single_scalar(bqs_s[:, :], bq_sb[:, :], 0.125, ALU.mult))
    em("ve", lambda E: E.tensor_single_scalar(bks_s[:, :], bk_sb[:, :], 0.125, ALU.mult))
    vs_modp = mark("ve", "V")
    wait("ve", "V", vs_modp)
    run_phase()
    free("wmod_buf")

    if tap == "mod":
        return _finish_tap(nc, sems, cnt, xo, [(mod_sb[:, 0:48], 0), (der[:, 8:64], 64)]), sem_guards

    x_sb = alloc("x_sb", [128, KT * T], F32)

    phase()
    dma("sy", x_sb[:, :], xt, "DX")
    dX_x = cnt["DX"]

    def emit_ln(src, one_gain_col, shift_col):
        rowbuf = live_get("rowbuf")
        a_rep = live_get("a_rep"); b_rep = live_get("b_rep")
        ra = rowbuf[0:1, 0:T]; rb = rowbuf[0:1, T:2 * T]; rc = rowbuf[0:1, 2 * T:3 * T]
        for mt in range(KT):
            em("sc", lambda E, mt=mt: E.activation(h_bf[:, mt * T:(mt + 1) * T], src[:, mt * T:(mt + 1) * T], AF.Square))
        s_sq = mark("sc", "S")
        wait("te", "S", s_sq)
        for c in range(NCH):
            for kt in range(KT):
                o = psum_s[0:1, c * 512:(c + 1) * 512]
                r = h_bf[:, kt * T + c * 512: kt * T + (c + 1) * 512]
                em("te", lambda E, o=o, r=r, kt=kt: E.matmul(o, ones_c[:, :], r, start=(kt == 0), stop=(kt == KT - 1)))
        t_sq = mark("te", "T")
        wait("ve", "T", t_sq)
        em("ve", lambda E: E.tensor_single_scalar(rb, psum_s[0:1, 0:T], 1.0 / C, ALU.mult))
        v_sq = mark("ve", "V")
        wait("sc", "T", t_sq)
        for mt in range(KT):
            em("sc", lambda E, mt=mt: E.copy(h_bf[:, mt * T:(mt + 1) * T], src[:, mt * T:(mt + 1) * T]))
        s_xbf = mark("sc", "S")
        wait("te", "S", s_xbf)
        wait("te", "V", v_sq)
        for c in range(NCH):
            for kt in range(KT):
                o = psum_s[0:1, c * 512:(c + 1) * 512]
                r = h_bf[:, kt * T + c * 512: kt * T + (c + 1) * 512]
                em("te", lambda E, o=o, r=r, kt=kt: E.matmul(o, ones_c[:, :], r, start=(kt == 0), stop=(kt == KT - 1)))
        t_sm = mark("te", "T")
        wait("ve", "T", t_sm)
        em("ve", lambda E: E.tensor_single_scalar(ra, psum_s[0:1, 0:T], 1.0 / C, ALU.mult))
        v_sm = mark("ve", "V")
        wait("sc", "V", v_sm)
        s_sqr = inc("sc", "S", lambda E: E.activation(psum_s[0:1, 0:T], ra, AF.Square))
        wait("ve", "S", s_sqr)
        em("ve", lambda E: E.tensor_sub(rb, rb, psum_s[0:1, 0:T]))
        em("ve", lambda E: E.tensor_single_scalar(rb, rb, 1e-5, ALU.add))
        em("ve", lambda E: E.reciprocal(rc, rb))
        v_var = mark("ve", "V")
        wait("sc", "V", v_var)
        s_rstd = inc("sc", "S", lambda E: E.activation(rb, rc, AF.Sqrt))
        wait("ve", "S", s_rstd)
        em("ve", lambda E: E.tensor_mul(ra, ra, rb))
        v_rows = mark("ve", "V")
        wait("te", "V", v_rows)
        for c in range(NCH):
            em("te", lambda E, c=c: E.matmul(psum_s[:, c * 512:(c + 1) * 512], ones_r[:, :], rb[:, c * 512:(c + 1) * 512], start=True, stop=True))
        t_ar = mark("te", "T")
        wait("ve", "T", t_ar)
        em("ve", lambda E: E.tensor_copy(a_rep[:, :], psum_s[:, 0:T]))
        v_ar = mark("ve", "V")
        wait("te", "V", v_ar)
        for c in range(NCH):
            em("te", lambda E, c=c: E.matmul(psum_s[:, c * 512:(c + 1) * 512], ones_r[:, :], ra[:, c * 512:(c + 1) * 512], start=True, stop=True))
        t_br = mark("te", "T")
        wait("ve", "T", t_br)
        em("ve", lambda E: E.tensor_copy(b_rep[:, :], psum_s[:, 0:T]))
        mark("ve", "V")
        return a_rep, b_rep

    def live_get(name):
        return live[name].tensor.ap() if hasattr(live[name], "tensor") else None

    rowbuf = alloc("rowbuf", [1, 3 * T], F32)
    a_rep = alloc("a_rep", [128, T], F32)
    b_rep = alloc("b_rep", [128, T], F32)
    tmpT = alloc("tmpT", [128, T], F32)

    def live_get(name, _m={"rowbuf": rowbuf, "a_rep": a_rep, "b_rep": b_rep}):
        return _m[name]

    wait("sc", "DX", dX_x)
    emit_ln(x_sb, None, None)
    for mt in range(KT):
        xs = x_sb[:, mt * T:(mt + 1) * T]
        em("ve", lambda E, xs=xs: E.tensor_mul(tmpT[:, :], xs, a_rep[:, :]))
        em("ve", lambda E: E.tensor_sub(tmpT[:, :], tmpT[:, :], b_rep[:, :]))
        em("ve", lambda E, mt=mt: E.tensor_scalar(h_bf[:, mt * T:(mt + 1) * T], tmpT[:, :],
                                                  der[:, 8 + mt:9 + mt], mod_sb[:, 0 + mt:1 + mt],
                                                  op0=ALU.mult, op1=ALU.add))
        em("ve", lambda E, xs=xs, mt=mt: E.tensor_single_scalar(xs, xs, der[:, 48 + mt:49 + mt], ALU.add))
    vs_ln1 = mark("ve", "V")
    wait("sy", "V", vs_ln1)
    dma("sy", xp_dram, x_sb[:, :], "DX")
    dX_spill = cnt["DX"]
    wait("sy", "DX", dX_spill)
    run_phase()
    free("tmpT", "b_rep", "a_rep", "rowbuf", "x_sb")

    if tap == "h1":
        return _finish_tap(nc, sems, cnt, xo, [(h_bf[:, :], 0)]), sem_guards

    q_rot = alloc("q_rot", [128, KT * T], BF16, side="left")
    k_rot = alloc("k_rot", [128, KT * T], BF16, side="left")

    phase()
    wq_sb = alloc("wq_sb", [128, KT * C], BF16)
    wk_sb = alloc("wk_sb", [128, KT * C], BF16)
    cos_sb = alloc("cos_sb", [128, T], BF16)
    sin_sb = alloc("sin_sb", [128, T], BF16)
    qk_raw = alloc("qk_raw", [128, 8 * 512], F32)
    tmpr = alloc("tmpr", [128, 2 * 512], F32)

    dma("sy", cos_sb[:, :], cosr, "DX")
    dma("sy", sin_sb[:, :], sinr, "DX")
    dX_rope = cnt["DX"]
    wait("gp", "T", ts_mod)
    dma("gp", wq_sb[:, :], wqs, "DW")
    dW_q = cnt["DW"]
    dma("gp", wk_sb[:, :], wks, "DW")
    dW_k = cnt["DW"]

    proj_bank_busy = {}
    def emit_proj(w_sb, rot_dst, bias_scaled, scale):
        bank_busy = proj_bank_busy
        for n in range(NCH):
            for m in range(KT):
                bk = m % 4
                o = psum_s[:, bk * 512:(bk + 1) * 512]
                if bank_busy.get(bk) is not None:
                    wait("te", "V", bank_busy[bk])
                for kt in range(KT):
                    l = w_sb[:, kt * C + m * 128: kt * C + (m + 1) * 128]
                    r = h_bf[:, kt * T + n * 512: kt * T + (n + 1) * 512]
                    em("te", lambda E, o=o, l=l, r=r, kt=kt: E.matmul(o, l, r, start=(kt == 0), stop=(kt == KT - 1)))
                tv = mark("te", "T")
                wait("ve", "T", tv)
                em("ve", lambda E, o=o, m=m: E.tensor_scalar(qk_raw[:, m * 512:(m + 1) * 512], o,
                                                             scale, bias_scaled[:, m:m + 1],
                                                             op0=ALU.mult, op1=ALU.add))
                bank_busy[bk] = mark("ve", "V")
            cc = cos_sb[:, n * 512:(n + 1) * 512]
            ss = sin_sb[:, n * 512:(n + 1) * 512]
            for at in range(4):
                A = qk_raw[:, at * 512:(at + 1) * 512]
                Bv = qk_raw[:, (4 + at) * 512:(5 + at) * 512]
                dA = rot_dst[:, at * T + n * 512: at * T + (n + 1) * 512]
                dB = rot_dst[:, (4 + at) * T + n * 512: (4 + at) * T + (n + 1) * 512]
                em("ve", lambda E, A=A, cc=cc: E.tensor_mul(tmpr[:, 0:512], A, cc))
                em("ve", lambda E, Bv=Bv, ss=ss: E.tensor_mul(tmpr[:, 512:1024], Bv, ss))
                em("ve", lambda E, dA=dA: E.tensor_sub(dA, tmpr[:, 0:512], tmpr[:, 512:1024]))
                em("ve", lambda E, A=A, ss=ss: E.tensor_mul(tmpr[:, 0:512], A, ss))
                em("ve", lambda E, Bv=Bv, cc=cc: E.tensor_mul(tmpr[:, 512:1024], Bv, cc))
                em("ve", lambda E, dB=dB: E.tensor_add(dB, tmpr[:, 0:512], tmpr[:, 512:1024]))
            mark("ve", "V")

    wait("te", "DW", dW_q)
    wait("ve", "DX", dX_rope)
    wait("ve", "V", vs_modp)
    emit_proj(wq_sb, q_rot, bqs_s, 0.125)
    wait("te", "DW", dW_k)
    emit_proj(wk_sb, k_rot, bk_sb, 1.0)
    run_phase()
    free("tmpr", "qk_raw", "sin_sb", "cos_sb", "wk_sb", "wq_sb")

    if tap == "qrot":
        return _finish_tap(nc, sems, cnt, xo, [(q_rot[:, :], 0)]), sem_guards
    if tap == "krot":
        return _finish_tap(nc, sems, cnt, xo, [(k_rot[:, :], 0)]), sem_guards

    phase()
    wv_sb = alloc("wv_sb", [128, KT * C], BF16)
    vt_sb = alloc("vt_sb", [128, KT * T], BF16)
    dma("gp", wv_sb[:, :], wvs, "DW")
    dW_v = cnt["DW"]
    wait("te", "DW", dW_v)
    bank_busy = proj_bank_busy
    for n in range(NCH):
        for m in range(KT):
            bk = m % 4
            o = psum_s[:, bk * 512:(bk + 1) * 512]
            if bank_busy.get(bk) is not None:
                wait("te", "V", bank_busy[bk])
            for kt in range(KT):
                l = wv_sb[:, kt * C + m * 128: kt * C + (m + 1) * 128]
                r = h_bf[:, kt * T + n * 512: kt * T + (n + 1) * 512]
                em("te", lambda E, o=o, l=l, r=r, kt=kt: E.matmul(o, l, r, start=(kt == 0), stop=(kt == KT - 1)))
            tv = mark("te", "T")
            wait("ve", "T", tv)
            em("ve", lambda E, o=o, m=m, n=n: E.tensor_single_scalar(vt_sb[:, m * T + n * 512: m * T + (n + 1) * 512],
                                                                     o, bv_sb[:, m:m + 1], ALU.add))
            bank_busy[bk] = mark("ve", "V")
    run_phase()

    phase()
    v_tok = alloc("v_tok", [128, NQT * NH * HD], BF16, side="left")
    ps_tr0 = palloc("ps_tr0", [128, 1024], BF16)
    ps_tr1 = palloc("ps_tr1", [128, 1024], BF16)
    trb = [ps_tr0, ps_tr1]
    tr_busy = [None, None]
    i = 0
    for h in range(NH):
        for lt in range(NQT):
            src = vt_sb[(h % 2) * 64:(h % 2) * 64 + 64, (h // 2) * T + lt * 128: (h // 2) * T + lt * 128 + 128]
            idn64 = idn_b[0:64, 0:64] if h % 2 == 0 else idn_b[64:128, 64:128]
            pb = trb[i % 2]
            if tr_busy[i % 2] is not None:
                wait("te", "V", tr_busy[i % 2])
            tv = inc("te", "T", lambda E, pb=pb, src=src, idn64=idn64: E.transpose(pb[:, 0:64], src, idn64))
            wait("ve", "T", tv)
            dst = v_tok[:, (lt * NH + h) * HD:(lt * NH + h) * HD + HD]
            tr_busy[i % 2] = inc("ve", "V", lambda E, dst=dst, pb=pb: E.tensor_copy(dst, pb[:, 0:64]))
            i += 1
    run_phase()
    free("vt_sb", "wv_sb")

    free("ps_tr1", "ps_tr0")
    phase()
    ps_dn = palloc("ps_dn", [128, 512], F32)
    attn_o = alloc("attn_o", [128, KT * T], BF16)
    pT2 = [alloc("pT0", [128, NQT * 512], BF16), alloc("pT1", [128, NQT * 512], BF16)]
    drow = alloc("drow", [1, 512], F32)
    rep_sb = alloc("rep_sb", [128, 512], F32)
    em("ve", lambda E: E.memset(rscr[:, 6:7], -30.0))
    v_c30 = mark("ve", "V")
    wait("sc", "V", v_c30)

    pv_ps = [ps_a, ps_b]
    pv_busy = [None, None]
    half_free = [None, None]
    pT_free = [None, None]
    dn_free = None
    drow_free = None

    qc_list = [0] if tap == "att0" else range(NCH)
    h_list = [0] if tap == "att0" else range(NH)
    hidx = 0
    for qc in qc_list:
        for h in h_list:
            r0 = 32 * (h % 4)
            a_ti, b_ti = h // 4, 4 + h // 4
            par = hidx % 2
            pT = pT2[par]
            nlt = 4 * qc + 4
            first_exp = None
            for p in range(nlt // 2):
                hf = p % 2
                for li in range(2):
                    lt = 2 * p + li
                    o = psum_s[:, hf * 1024 + li * 512: hf * 1024 + (li + 1) * 512]
                    if li == 0 and half_free[hf] is not None:
                        wait("te", "S", half_free[hf])
                    lA = k_rot[r0:r0 + 32, a_ti * T + lt * 128: a_ti * T + (lt + 1) * 128]
                    rA = q_rot[r0:r0 + 32, a_ti * T + qc * 512: a_ti * T + (qc + 1) * 512]
                    lB = k_rot[r0:r0 + 32, b_ti * T + lt * 128: b_ti * T + (lt + 1) * 128]
                    rB = q_rot[r0:r0 + 32, b_ti * T + qc * 512: b_ti * T + (qc + 1) * 512]
                    em("te", lambda E, o=o, lA=lA, rA=rA, r0=r0: E.matmul(o, lA, rA, start=True, stop=False, tile_position=(r0, 0)))
                    em("te", lambda E, o=o, lB=lB, rB=rB, r0=r0: E.matmul(o, lB, rB, start=False, stop=True, tile_position=(r0, 0)))
                t_s = mark("te", "T")
                need_mask = (2 * p + 1) >= 4 * qc
                v_mask = None
                if need_mask:
                    wait("ve", "T", t_s)
                    for li in range(2):
                        lt = 2 * p + li
                        if lt >= 4 * qc:
                            j = lt - 4 * qc
                            mv = maskT_sb[:, 512 - 128 * j: 1024 - 128 * j]
                            o = psum_s[:, hf * 1024 + li * 512: hf * 1024 + (li + 1) * 512]
                            em("ve", lambda E, o=o, mv=mv: E.tensor_add(o, o, mv))
                    v_mask = mark("ve", "V")
                if v_mask is not None:
                    wait("sc", "V", v_mask)
                else:
                    wait("sc", "T", t_s)
                if p == 0 and pT_free[par] is not None:
                    wait("sc", "T", pT_free[par])
                dst = pT[:, 2 * p * 512: (2 * p + 2) * 512]
                src = psum_s[:, hf * 1024: hf * 1024 + 1024]
                half_free[hf] = inc("sc", "S", lambda E, dst=dst, src=src:
                                    E.activation(dst, src, AF.Exp, bias=rscr[:, 6:7]))
                if first_exp is None:
                    first_exp = half_free[hf]
            s_done = half_free[(nlt // 2 - 1) % 2]
            if dn_free is not None:
                wait("te", "V", dn_free)
            wait("te", "S", s_done)
            for lt in range(nlt):
                o = ps_dn[0:1, 0:512]
                r = pT[:, lt * 512:(lt + 1) * 512]
                em("te", lambda E, o=o, r=r, lt=lt, nlt=nlt: E.matmul(o, ones_c[:, :], r, start=(lt == 0), stop=(lt == nlt - 1)))
            t_dn = mark("te", "T")
            pp = pv_ps[h % 2]
            row0 = 64 * (h % 2)
            if pv_busy[h % 2] is not None:
                wait("te", "V", pv_busy[h % 2])
            for lt in range(nlt):
                l = v_tok[:, (lt * NH + h) * HD:(lt * NH + h) * HD + HD]
                r = pT[:, lt * 512:(lt + 1) * 512]
                o = pp[row0:row0 + 64, 0:512]
                em("te", lambda E, o=o, l=l, r=r, lt=lt, row0=row0, nlt=nlt:
                   E.matmul(o, l, r, start=(lt == 0), stop=(lt == nlt - 1), tile_position=(0, row0)))
            t_pv = mark("te", "T")
            pT_free[par] = t_pv
            wait("ve", "T", t_dn)
            if drow_free is not None:
                wait("ve", "T", drow_free)
            v_rc = inc("ve", "V", lambda E: E.reciprocal(drow[0:1, :], ps_dn[0:1, 0:512]))
            wait("te", "V", v_rc)
            t_rep = inc("te", "T", lambda E: E.matmul(ps_dn[:, 0:512], ones_r[:, :], drow[0:1, :], start=True, stop=True))
            drow_free = t_rep
            wait("ve", "T", t_rep)
            v_rep = inc("ve", "V", lambda E: E.tensor_copy(rep_sb[:, :], ps_dn[:, 0:512]))
            dn_free = v_rep
            wait("ve", "T", t_pv)
            dst = attn_o[row0:row0 + 64, (h // 2) * T + qc * 512: (h // 2) * T + (qc + 1) * 512]
            pv_busy[h % 2] = inc("ve", "V", lambda E, dst=dst, pp=pp, row0=row0:
                                 E.tensor_mul(dst, pp[row0:row0 + 64, 0:512], rep_sb[row0:row0 + 64, :]))
            hidx += 1
    run_phase()
    if tap == "att0":
        return _finish_tap(nc, sems, cnt, xo, [(pT2[0][:, :], 0), (attn_o[:, 0:512], 8192),
                                               (rep_sb[:, :], 8192 + 512)]), sem_guards
    free("rep_sb", "drow", "pT1", "pT0", "v_tok", "k_rot", "q_rot", "ps_dn")

    if tap == "attn":
        return _finish_tap(nc, sems, cnt, xo, [(attn_o[:, :], 0)]), sem_guards

    xacc = alloc("xacc", [128, KT * T], F32, side="left")
    phase()
    wo_sb = alloc("wo_sb", [128, KT * C], BF16)
    dma("gp", wo_sb[:, :], wos, "DW")
    dW_o = cnt["DW"]
    dma("sy", xacc[:, :], xp_dram, "DX")
    dX_xp = cnt["DX"]
    wait("sy", "DX", dX_xp)
    if tap == "xp":
        run_phase()
        return _finish_tap(nc, sems, cnt, xo, [(xacc[:, :], 0)]), sem_guards
    wait("te", "DW", dW_o)
    wait("ve", "DX", dX_xp)
    bank_busy = {}
    for n in range(NCH):
        for m in range(KT):
            bk = m % 4
            o = psum_s[:, bk * 512:(bk + 1) * 512]
            if bank_busy.get(bk) is not None:
                wait("te", "V", bank_busy[bk])
            for kt in range(KT):
                l = wo_sb[:, kt * C + m * 128: kt * C + (m + 1) * 128]
                r = attn_o[:, kt * T + n * 512: kt * T + (n + 1) * 512]
                em("te", lambda E, o=o, l=l, r=r, kt=kt: E.matmul(o, l, r, start=(kt == 0), stop=(kt == KT - 1)))
            tv = mark("te", "T")
            wait("ve", "T", tv)
            dst = xacc[:, m * T + n * 512: m * T + (n + 1) * 512]
            em("ve", lambda E, dst=dst, o=o, m=m: E.scalar_tensor_tensor(dst, o, der[:, 16 + m:17 + m], dst,
                                                                         op0=ALU.mult, op1=ALU.add))
            bank_busy[bk] = mark("ve", "V")
    run_phase()
    free("wo_sb", "attn_o")

    if tap == "x2":
        return _finish_tap(nc, sems, cnt, xo, [(xacc[:, :], 0)]), sem_guards

    h_lo = alloc("h_lo", [128, KT * T], BF16)
    rowbuf = alloc("rowbuf2", [1, 3 * T], F32)
    a_rep = alloc("a_rep2", [128, T], F32)
    b_rep = alloc("b_rep2", [128, T], F32)
    tmpT = alloc("tmpT2", [128, T], F32)

    def live_get(name, _m=None):
        return {"rowbuf": rowbuf, "a_rep": a_rep, "b_rep": b_rep}[name]

    phase()
    emit_ln(xacc, None, None)
    for mt in range(KT):
        xs = xacc[:, mt * T:(mt + 1) * T]
        em("ve", lambda E, xs=xs: E.tensor_mul(tmpT[:, :], xs, a_rep[:, :]))
        em("ve", lambda E: E.tensor_sub(tmpT[:, :], tmpT[:, :], b_rep[:, :]))
        em("ve", lambda E, mt=mt: E.tensor_scalar(tmpT[:, :], tmpT[:, :],
                                                  der[:, 32 + mt:33 + mt], mod_sb[:, 24 + mt:25 + mt],
                                                  op0=ALU.mult, op1=ALU.add))
        v_h2 = mark("ve", "V")
        wait("sc", "V", v_h2)
        s_hbf = inc("sc", "S", lambda E, mt=mt: E.copy(h_bf[:, mt * T:(mt + 1) * T], tmpT[:, :]))
        wait("ve", "S", s_hbf)
        em("ve", lambda E, mt=mt: E.tensor_sub(h_lo[:, mt * T:(mt + 1) * T], tmpT[:, :], h_bf[:, mt * T:(mt + 1) * T]))
        em("ve", lambda E, xs=xs, mt=mt: E.tensor_single_scalar(xs, xs, der[:, 56 + mt:57 + mt], ALU.mult))
    vs_ln2 = mark("ve", "V")
    run_phase()
    free("tmpT2", "b_rep2", "a_rep2", "rowbuf2")

    if tap == "h2":
        return _finish_tap(nc, sems, cnt, xo, [(h_bf[:, :], 0)]), sem_guards

    phase()
    scoresT = alloc("scoresT", [64, T], F32)
    gw_hi = alloc("gw_hi", [128, KT * 64], BF16)
    gw_lo = alloc("gw_lo", [128, KT * 64], BF16)
    s_tok = alloc("s_tok", [128, NQT * 64], F32)
    sc_tok = alloc("sc_tok", [128, NQT * 64], F32)
    w_tok = alloc("w_tok", [128, NQT * 64], F32)
    r16 = alloc("r16", [128, NQT * 8], F32)
    r16b = alloc("r16b", [128, NQT * 8], F32)
    rr = alloc("rr", [128, NQT], F32)
    rr2 = alloc("rr2", [128, NQT], F32)
    w_dT = alloc("w_dT", [64, T], BF16, side="left")

    s_hi = inc("sc", "S", lambda E: E.copy(gw_hi[:, :], gw_f[:, :]))
    wait("ve", "S", s_hi)
    v_lo = inc("ve", "V", lambda E: E.tensor_sub(gw_lo[:, :], gw_f[:, :], gw_hi[:, :]))
    wait("te", "V", v_lo)
    for c in range(NCH):
        o = psum_s[0:64, c * 512:(c + 1) * 512]
        first = True
        for kt in range(KT):
            for (wl, rh) in ((gw_hi, h_bf), (gw_hi, h_lo), (gw_lo, h_bf)):
                l = wl[:, kt * 64:(kt + 1) * 64]
                r = rh[:, kt * T + c * 512: kt * T + (c + 1) * 512]
                last = (kt == KT - 1) and (wl is gw_lo)
                em("te", lambda E, o=o, l=l, r=r, first=first, last=last: E.matmul(o, l, r, start=first, stop=last))
                first = False
    t_gs = mark("te", "T")
    wait("sc", "T", t_gs)
    s_sig = inc("sc", "S", lambda E: E.activation(scoresT[:, :], psum_s[0:64, 0:T], AF.Sigmoid))
    wait("te", "S", s_sig)
    tp = [ps_a, ps_b]; tb = [None, None]
    for tt in range(NQT):
        pb = tp[tt % 2]
        if tb[tt % 2] is not None:
            wait("te", "V", tb[tt % 2])
        tv = inc("te", "T", lambda E, pb=pb, tt=tt: E.transpose(pb[:, 0:64], scoresT[:, tt * 128:(tt + 1) * 128], idn_f[0:64, 0:64]))
        wait("ve", "T", tv)
        tb[tt % 2] = inc("ve", "V", lambda E, pb=pb, tt=tt: E.tensor_copy(s_tok[:, tt * 64:(tt + 1) * 64], pb[:, 0:64]))
    E3 = lambda t: t.rearrange("p (a e) -> p a e", e=64)
    G3 = lambda t: t.rearrange("p (a g e) -> p (a g) e", g=8, e=8)
    GS = lambda t: t.rearrange("p (a g) -> p a g", g=8)
    bview = brep[:, :].rearrange("p (o e) -> p o e", o=1).broadcast_to([128, NQT, 64])
    em("ve", lambda E: E.tensor_tensor(E3(sc_tok[:, :]), E3(s_tok[:, :]), bview, op=ALU.add))
    em("ve", lambda E: E.tensor_reduce(r16[:, :].rearrange("p (a g o) -> p (a g) o", g=8, o=1),
                                       G3(sc_tok[:, :]), axis=AX.X, op=ALU.max))
    gmb = r16[:, :].rearrange("p (q o) -> p q o", o=1).broadcast_to([128, 128, 8])
    em("ve", lambda E: E.tensor_tensor(G3(w_tok[:, :]),
                                       G3(sc_tok[:, :]), gmb, op=ALU.is_equal))
    em("ve", lambda E: E.scalar_tensor_tensor(w_tok[:, :], w_tok[:, :], -1e30, sc_tok[:, :], op0=ALU.mult, op1=ALU.add))
    em("ve", lambda E: E.tensor_reduce(r16b[:, :].rearrange("p (q o) -> p q o", o=1),
                                       G3(w_tok[:, :]), axis=AX.X, op=ALU.max))
    em("ve", lambda E: E.tensor_add(r16[:, :], r16[:, :], r16b[:, :]))
    for it in range(3):
        em("ve", lambda E: E.tensor_reduce(rr[:, :].rearrange("p (a o) -> p a o", o=1),
                                           GS(r16[:, :]), axis=AX.X, op=ALU.max))
        vsep("ve", "V")
        rrb = rr[:, :].rearrange("p (a o) -> p a o", o=1).broadcast_to([128, NQT, 8])
        em("ve", lambda E: E.tensor_tensor(GS(r16b[:, :]), GS(r16[:, :]), rrb, op=ALU.is_equal))
        em("ve", lambda E: E.scalar_tensor_tensor(r16[:, :], r16b[:, :], -1e30, r16[:, :], op0=ALU.mult, op1=ALU.add))
    em("ve", lambda E: E.tensor_reduce(rr[:, :].rearrange("p (a o) -> p a o", o=1),
                                       GS(r16[:, :]), axis=AX.X, op=ALU.max))
    vsep("ve", "V")
    em("ve", lambda E: E.tensor_reduce(r16[:, :].rearrange("p (q o) -> p q o", o=1),
                                       G3(sc_tok[:, :]), axis=AX.X, op=ALU.max))
    em("ve", lambda E: E.tensor_tensor(G3(w_tok[:, :]),
                                       G3(sc_tok[:, :]),
                                       r16[:, :].rearrange("p (q o) -> p q o", o=1).broadcast_to([128, 128, 8]), op=ALU.is_equal))
    em("ve", lambda E: E.scalar_tensor_tensor(w_tok[:, :], w_tok[:, :], -1e30, sc_tok[:, :], op0=ALU.mult, op1=ALU.add))
    em("ve", lambda E: E.tensor_reduce(r16b[:, :].rearrange("p (q o) -> p q o", o=1),
                                       G3(w_tok[:, :]), axis=AX.X, op=ALU.max))
    em("ve", lambda E: E.tensor_add(r16[:, :], r16[:, :], r16b[:, :]))
    rrb = rr[:, :].rearrange("p (a o) -> p a o", o=1).broadcast_to([128, NQT, 8])
    em("ve", lambda E: E.tensor_tensor(GS(r16b[:, :]), GS(r16[:, :]), rrb, op=ALU.is_lt))
    em("ve", lambda E: E.tensor_single_scalar(r16b[:, :], r16b[:, :], -1e30, ALU.mult))
    penb = r16b[:, :].rearrange("p (q o) -> p q o", o=1).broadcast_to([128, 128, 8])
    em("ve", lambda E: E.tensor_tensor(G3(w_tok[:, :]), G3(sc_tok[:, :]), penb, op=ALU.add))
    for it in range(7):
        em("ve", lambda E: E.tensor_reduce(rr[:, :].rearrange("p (a o) -> p a o", o=1),
                                           E3(w_tok[:, :]), axis=AX.X, op=ALU.max))
        vsep("ve", "V")
        rrb2 = rr[:, :].rearrange("p (a o) -> p a o", o=1).broadcast_to([128, NQT, 64])
        em("ve", lambda E: E.tensor_tensor(E3(sc_tok[:, :]), E3(w_tok[:, :]), rrb2, op=ALU.is_equal))
        em("ve", lambda E: E.scalar_tensor_tensor(w_tok[:, :], sc_tok[:, :], -1e30, w_tok[:, :], op0=ALU.mult, op1=ALU.add))
    em("ve", lambda E: E.tensor_reduce(rr[:, :].rearrange("p (a o) -> p a o", o=1),
                                       E3(w_tok[:, :]), axis=AX.X, op=ALU.max))
    vsep("ve", "V")
    em("ve", lambda E: E.tensor_tensor(E3(sc_tok[:, :]), E3(s_tok[:, :]), bview, op=ALU.add))
    em("ve", lambda E: E.tensor_tensor(G3(w_tok[:, :]), G3(sc_tok[:, :]), penb, op=ALU.add))
    rrb2 = rr[:, :].rearrange("p (a o) -> p a o", o=1).broadcast_to([128, NQT, 64])
    em("ve", lambda E: E.tensor_tensor(E3(sc_tok[:, :]), E3(w_tok[:, :]), rrb2, op=ALU.is_ge))
    em("ve", lambda E: E.tensor_mul(w_tok[:, :], sc_tok[:, :], s_tok[:, :]))
    em("ve", lambda E: E.tensor_reduce(rr[:, :].rearrange("p (a o) -> p a o", o=1),
                                       E3(w_tok[:, :]), axis=AX.X, op=ALU.add))
    vsep("ve", "V")
    em("ve", lambda E: E.tensor_single_scalar(rr[:, :], rr[:, :], 1e-20, ALU.add))
    vsep("ve", "V")
    em("ve", lambda E: E.reciprocal(rr2[:, :], rr[:, :]))
    vsep("ve", "V")
    em("ve", lambda E: E.tensor_single_scalar(rr[:, :], rr2[:, :], 2.5, ALU.mult))
    vsep("ve", "V")
    rrb2 = rr[:, :].rearrange("p (a o) -> p a o", o=1).broadcast_to([128, NQT, 64])
    em("ve", lambda E: E.tensor_tensor(E3(w_tok[:, :]), E3(w_tok[:, :]), rrb2, op=ALU.mult))
    v_rt = mark("ve", "V")
    wait("te", "V", v_rt)
    tb = [None, None]
    for tt in range(NQT):
        pb = tp[tt % 2]
        if tb[tt % 2] is not None:
            wait("te", "V", tb[tt % 2])
        tv = inc("te", "T", lambda E, pb=pb, tt=tt: E.transpose(pb[0:64, 0:128], w_tok[:, tt * 64:(tt + 1) * 64], idn_f[:, :]))
        wait("ve", "T", tv)
        tb[tt % 2] = inc("ve", "V", lambda E, pb=pb, tt=tt: E.tensor_copy(w_dT[:, tt * 128:(tt + 1) * 128], pb[0:64, 0:128]))
    vs_wdt = cnt["V"]
    run_phase()
    free("rr2", "rr", "r16b", "r16", "w_tok", "sc_tok", "s_tok", "gw_lo", "gw_hi", "scoresT", "h_lo")

    if tap == "wd":
        return _finish_tap(nc, sems, cnt, xo, [(w_dT[:, :], 0)]), sem_guards

    phase()
    exp_sb = [alloc("exp0", [128, EPB], BF16), alloc("exp1", [128, EPB], BF16)]
    cw = [alloc("cw0", [128, T], BF16), alloc("cw1", [128, T], BF16)]
    hm = [alloc("hm0", [128, 4 * 512], BF16), alloc("hm1", [128, 4 * 512], BF16)]
    sg = alloc("sg", [128, 4 * 512], BF16)
    ws_g = alloc("ws_g", [128, KT * FF], BF16)
    ws_u = alloc("ws_u", [128, KT * FF], BF16)
    ws_d = alloc("ws_d", [128, 4 * C], BF16)
    ps_c = palloc("ps_c", [128, 512], F32)
    ps_d = palloc("ps_d", [128, 512], F32)

    dma("gp", ws_g[:, :], wsg, "DW")
    dma("gp", ws_u[:, :], wsu, "DW")
    dma("gp", ws_d[:, :], wsd, "DW")
    dW_sh = cnt["DW"]

    slot_done = [None] * NE
    dW_exp = [None] * NE
    gb_banks = [psum_s[:, 0:512], psum_s[:, 512:1024], psum_s[:, 1024:1536], psum_s[:, 1536:2048]]
    d_banks = [ps_a, ps_b, ps_c, ps_d]
    d_busy = [None, None, None, None]
    gb_busy = [None, None, None, None]

    hm_guard = [None, None]

    def emit_gu(WG, WU, n, cw_ap, weight_wait):
        hmb = hm[n % 2]
        v_hm = None
        s_si = None
        for mh in range(2):
            for src, bks, tag in ((WG, (0, 1), "g"), (WU, (2, 3), "u")):
                for mi, bki in enumerate(bks):
                    m = mh * 2 + mi
                    o = gb_banks[bki]
                    if gb_busy[bki] is not None:
                        eng, v = gb_busy[bki]
                        wait("te", eng, v)
                        gb_busy[bki] = None
                    if weight_wait:
                        for wkey in weight_wait:
                            wait("te", wkey[0], wkey[1])
                        weight_wait = None
                    for kt in range(KT):
                        l = src[:, kt * FF + m * 128: kt * FF + (m + 1) * 128]
                        r = h_bf[:, kt * T + n * 512: kt * T + (n + 1) * 512]
                        em("te", lambda E, o=o, l=l, r=r, kt=kt: E.matmul(o, l, r, start=(kt == 0), stop=(kt == KT - 1)))
                tg = mark("te", "T")
                if tag == "g":
                    wait("sc", "T", tg)
                    for mi, bki in enumerate(bks):
                        m = mh * 2 + mi
                        em("sc", lambda E, m=m, bki=bki: E.activation(sg[:, m * 512:(m + 1) * 512], gb_banks[bki], AF.Silu))
                    s_si = mark("sc", "S")
                    for bki in bks:
                        gb_busy[bki] = ("S", s_si)
                else:
                    wait("ve", "T", tg)
                    wait("ve", "S", s_si)
                    if mh == 0 and hm_guard[n % 2] is not None:
                        wait("ve", "T", hm_guard[n % 2])
                    for mi, bki in enumerate(bks):
                        m = mh * 2 + mi
                        dst = hmb[:, m * 512:(m + 1) * 512]
                        em("ve", lambda E, dst=dst, m=m, bki=bki: E.tensor_mul(dst, sg[:, m * 512:(m + 1) * 512], gb_banks[bki]))
                        if cw_ap is not None:
                            em("ve", lambda E, dst=dst, n=n: E.tensor_mul(dst, dst, cw_ap[:, n * 512:(n + 1) * 512]))
                    v_hm = mark("ve", "V")
                    for bki in bks:
                        gb_busy[bki] = ("V", v_hm)
        return v_hm

    def emit_d(WD, KD, n, v_hm):
        hmb = hm[n % 2]
        wait("te", "V", v_hm)
        for mo in range(KT):
            bank = d_banks[mo % 4]
            if d_busy[mo % 4] is not None:
                wait("te", "V", d_busy[mo % 4])
                d_busy[mo % 4] = None
            for kt in range(KD):
                l = WD[:, kt * C + mo * 128: kt * C + (mo + 1) * 128]
                r = hmb[:, kt * 512:(kt + 1) * 512]
                em("te", lambda E, bank=bank, l=l, r=r, kt=kt, KD=KD: E.matmul(bank[:, :], l, r, start=(kt == 0), stop=(kt == KD - 1)))
            td = mark("te", "T")
            wait("ve", "T", td)
            dst = xacc[:, mo * T + n * 512: mo * T + (n + 1) * 512]
            em("ve", lambda E, dst=dst, bank=bank: E.tensor_add(dst, dst, bank[:, :]))
            d_busy[mo % 4] = mark("ve", "V")
        hm_guard[n % 2] = cnt["T"]

    def emit_ffn_expert(WG, WU, WD, KD, cw_ap, weight_wait):
        vh = {}
        vh[0] = emit_gu(WG, WU, 0, cw_ap, weight_wait)
        vh[1] = emit_gu(WG, WU, 1, cw_ap, None)
        emit_d(WD, KD, 0, vh[0])
        vh[2] = emit_gu(WG, WU, 2, cw_ap, None)
        emit_d(WD, KD, 1, vh[1])
        vh[3] = emit_gu(WG, WU, 3, cw_ap, None)
        emit_d(WD, KD, 2, vh[2])
        emit_d(WD, KD, 3, vh[3])

    wait("te", "V", vs_ln2)
    emit_ffn_expert(ws_g, ws_u, ws_d, 4, None, [("DW", dW_sh)])

    def emit_exp_dma(e):
        if e >= 2:
            wait("gp", "T", slot_done[e - 2])
        src = bass.AP(experts.tensor, e * 128 * EPB, [[EPB, 128], [1, EPB]])
        dma("gp", exp_sb[e % 2][:, :], src, "DW")
        dW_exp[e] = cnt["DW"]

    emit_exp_dma(0)
    emit_exp_dma(1)
    rep_rot = [0]

    def emit_repl(e):
        for c in range(NCH):
            bk = rep_rot[0] % 4
            bank = d_banks[bk]
            if d_busy[bk] is not None:
                wait("te", "V", d_busy[bk])
                d_busy[bk] = None
            lhs = idn_b[0:64, e:e + 1].broadcast_to([64, 128])
            r = w_dT[:, c * 512:(c + 1) * 512]
            tv = inc("te", "T", lambda E, bank=bank, lhs=lhs, r=r: E.matmul(bank[:, :], lhs, r, start=True, stop=True))
            wait("ve", "T", tv)
            dst = cw[e % 2][:, c * 512:(c + 1) * 512]
            d_busy[bk] = inc("ve", "V", lambda E, dst=dst, bank=bank: E.tensor_copy(dst, bank[:, :]))
            rep_rot[0] += 1

    wait("te", "V", vs_wdt)
    emit_repl(0)
    for e in range(NE):
        sl = exp_sb[e % 2]
        WG = sl[:, 0:KT * FF]
        WU = sl[:, KT * FF:2 * KT * FF]
        WD = sl[:, 2 * KT * FF:EPB]
        vh0 = emit_gu(WG, WU, 0, cw[e % 2], [("DW", dW_exp[e])])
        if e + 1 < NE:
            emit_repl(e + 1)
        vh1 = emit_gu(WG, WU, 1, cw[e % 2], None)
        emit_d(WD, 4, 0, vh0)
        vh2 = emit_gu(WG, WU, 2, cw[e % 2], None)
        emit_d(WD, 4, 1, vh1)
        vh3 = emit_gu(WG, WU, 3, cw[e % 2], None)
        emit_d(WD, 4, 2, vh2)
        emit_d(WD, 4, 3, vh3)
        slot_done[e] = cnt["T"]
        if e + 2 < NE:
            emit_exp_dma(e + 2)

    v_fin = None
    for mt in range(KT):
        em("ve", lambda E, mt=mt: E.tensor_single_scalar(xacc[:, mt * T:(mt + 1) * T], xacc[:, mt * T:(mt + 1) * T],
                                                         der[:, 40 + mt:41 + mt], ALU.mult))
    v_fin = mark("ve", "V")
    wait("sy", "V", v_fin)
    dma("sy", xo, xacc[:, :], "DO")
    wait("sy", "DO", cnt["DO"])
    run_phase()
    free("ps_d", "ps_c", "ws_d", "ws_u", "ws_g", "sg", "hm1", "hm0", "cw1", "cw0", "exp1", "exp0")
    free("w_dT", "xacc", "h_bf")
    for g in reversed(sem_guards):
        g.__exit__(None, None, None)
    return nc


def _finish_tap(nc, sems, cnt, xo, taps):
    with nc.Block() as blk:
        @blk.gpsimd
        def _(E):
            n = 0
            for ap, off in taps:
                w = ap.shape[-1] if len(ap.shape) else 0
                dst = bass.AP(xo.tensor, off, [[xo.tensor.shape[1], ap.shape[0]], [1, int(np.prod(ap.shape[1:]))]])
                E.dma_start(dst, ap).then_inc(sems["DO"], 16)
                n += 16
            E.wait_ge(sems["DO"], n)
    return nc



def _swz(w):
    K, M = w.shape
    return np.ascontiguousarray(w.reshape(K // 128, 128, M).transpose(1, 0, 2).reshape(128, -1))


_CACHE = {}


def _staging(inp):
    f32 = np.float32
    perm = np.empty(1024, np.int64)
    for m in range(1024):
        blk = m // 512; h = (m % 512) // 32; j = m % 32
        perm[m] = 64 * h + 2 * j + blk
    st = {}
    st["wqs"] = _swz(np.ascontiguousarray(inp["L0_attn_Wq"].T[:, perm]))
    st["wks"] = _swz(np.ascontiguousarray(inp["L0_attn_Wk"].T[:, perm]))
    st["wvs"] = _swz(np.ascontiguousarray(inp["L0_attn_Wv"].T))
    st["wos"] = _swz(np.ascontiguousarray(inp["L0_attn_Wo"].T))
    st["bqs"] = np.ascontiguousarray(inp["L0_attn_bq"][perm].reshape(8, 128).T)
    st["bks"] = np.ascontiguousarray(inp["L0_attn_bk"][perm].reshape(8, 128).T)
    st["bvs"] = np.ascontiguousarray(inp["L0_attn_bv"].reshape(8, 128).T)
    st["bos"] = np.ascontiguousarray(inp["L0_attn_bo"].reshape(8, 128).T)
    st["wmods"] = _swz(inp["L0_wmod"])
    st["gws"] = _swz(np.ascontiguousarray(inp["L0_ffn_gate_w"].T))
    st["wsg"] = _swz(inp["L0_ffn_Ws_gate"])
    st["wsu"] = _swz(inp["L0_ffn_Ws_up"])
    st["wsd"] = _swz(inp["L0_ffn_Ws_down"])
    ex = np.empty((NE, 128, EPB), f32)
    for e in range(NE):
        ex[e, :, 0:4096] = _swz(inp["L0_ffn_We_gate"][e])
        ex[e, :, 4096:8192] = _swz(inp["L0_ffn_We_up"][e])
        ex[e, :, 8192:12288] = _swz(inp["L0_ffn_We_down"][e])
    st["experts"] = ex
    j = np.arange(32, dtype=np.float64)
    theta = 10000.0 ** (-2.0 * j / 64.0)
    t = np.arange(T, dtype=np.float64)
    fr = t[None, :] * theta[:, None]
    cos32 = np.cos(fr); sin32 = np.sin(fr)
    st["cosr"] = np.tile(cos32, (4, 1)).astype(ml_dtypes.bfloat16)
    st["sinr"] = np.tile(sin32, (4, 1)).astype(ml_dtypes.bfloat16)
    st["idnb"] = np.eye(128, dtype=ml_dtypes.bfloat16)
    st["idnf"] = np.eye(128, dtype=f32)
    q = np.arange(128)[:, None]; kk = np.arange(128)[None, :]
    st["maskd"] = np.where(kk <= q, 0.0, -1e30).astype(f32)
    mt = np.zeros((128, 1024), f32)
    mt[:, 0:512] = -1e30
    mt[:, 512:640] = np.where(q <= kk, 0.0, -1e30)
    st["masktd"] = mt
    st["brepd"] = np.broadcast_to(inp["L0_ffn_gate_bias"], (128, 64)).astype(f32).copy()
    return st


def kernel(**inp):
    if "nc" not in _CACHE or _CACHE.get("tap") != TAP:
        _CACHE["nc"] = _build(TAP)
        if isinstance(_CACHE["nc"], tuple):
            _CACHE["nc"] = _CACHE["nc"][0]
        _CACHE["tap"] = TAP
    nc = _CACHE["nc"]
    st = _staging(inp)
    x = inp["L0_attn_x"]
    cva = inp["L0_c"]
    in_maps = []
    for b in range(8):
        m = dict(st)
        m["xt"] = _swz(np.ascontiguousarray(x[b].T))
        m["cv"] = np.ascontiguousarray(cva[b].reshape(8, 128).T)
        in_maps.append(m)
    res = run_bass_kernel_spmd(nc, in_maps, core_ids=list(range(8)))
    outs = []
    for b in range(8):
        o = res.results[b]["xo"]
        o = o.reshape(128, KT, T).transpose(1, 0, 2).reshape(C, T).T
        outs.append(o)
    return np.ascontiguousarray(np.stack(outs)).astype(np.float32)
